# version 05
# baseline (speedup 1.0000x reference)
import math
from contextlib import ExitStack

import numpy as np
import ml_dtypes
import concourse.bass as bass
import concourse.mybir as mybir
from concourse.bass_utils import run_bass_kernel_spmd

F32 = mybir.dt.float32
BF16 = mybir.dt.bfloat16
AF = mybir.ActivationFunctionType
ALU = mybir.AluOpType
AX = mybir.AxisListType
BF = ml_dtypes.bfloat16

D = 2048
S = 2048
NK = D // 128
NEG = -1e30


def host_constants():
    c = {}
    i = np.arange(128)
    c["ident"] = np.eye(128, dtype=np.float32)
    c["ones_col"] = np.ones((128, 1), dtype=np.float32)
    c["ones_row_b"] = np.ones((1, 128), dtype=np.float32)
    c["mask_qdiag"] = np.where(i[:, None] >= i[None, :], 0.0, NEG).astype(np.float32)
    c["mask_qlow"] = np.where(i[None, :] >= i[:, None], 0.0, NEG).astype(np.float32)
    j = np.arange(128)[:, None]
    u = np.arange(-384, 512)[None, :]
    c["bin_kdiag_m"] = (u >= j).astype(np.float32)
    c["bin_klow_m"] = (u <= j).astype(np.float32)
    p64 = np.zeros((128, 128), dtype=np.float32)
    for r in range(128):
        b = (r // 64) * 64
        o = r - b
        if o < 32:
            p64[r, b + o + 32] = -1.0
        else:
            p64[r, b + o - 32] = 1.0
    c["rope_p64"] = np.ascontiguousarray(p64.T)
    psw = np.zeros((128, 128), dtype=np.float32)
    for r in range(128):
        psw[r, r ^ 64] = 1.0
    c["pswap"] = psw
    p128 = np.zeros((128, 128), dtype=np.float32)
    for r in range(128):
        if r < 64:
            p128[r, r + 64] = -1.0
        else:
            p128[r, r - 64] = 1.0
    c["rope_p128"] = np.ascontiguousarray(p128.T)
    ep = np.zeros((8, 512), dtype=np.float32)
    for e in range(8):
        ep[e, e * 64:(e + 1) * 64] = 1.0
    c["epat"] = ep
    c["eps5"] = np.full((128, 1), 1e-5, dtype=np.float32)
    c["eps6"] = np.full((128, 1), 1e-6, dtype=np.float32)
    return c


CONST_DT = {"ident": F32, "ones_col": BF16, "ones_row_b": BF16,
            "mask_qdiag": BF16, "mask_qlow": BF16,
            "eps5": F32, "eps6": F32,
            "bin_kdiag_m": BF16, "bin_klow_m": BF16, "epat": BF16,
            "rope_p64": BF16, "rope_p128": BF16, "pswap": BF16}

BNC_MU, BNC_R, BNC_RLN, BNC_AUX, BNC_C0, BNC_C1 = 0, 1, 2, 3, 4, 5
BNC_N = 8


class Sem:
    def __init__(self, h):
        self.h = h
        self.n = 0

    def inc(self, k=1):
        self.n += k
        return self.n


class MK:
    def __init__(self):
        self.nc = nc = bass.Bass("TRN2", target_bir_lowering=False)
        self.es = ExitStack()
        self.inputs = {}
        self.consts = host_constants()

        self.d_xin = nc.dram_tensor("x_in", [D, S], F32, kind="ExternalInput")
        self.d_xout = nc.dram_tensor("x_out", [D, S], F32, kind="ExternalOutput")
        self.d_xres = nc.dram_tensor("x_res", [D, S], F32)
        self.d_bnc = nc.dram_tensor("bnc", [BNC_N, 2048], F32)
        self.d_spill = nc.dram_tensor("spill", [2 * 2048 * 2048], BF16)
        self.d_spill2 = nc.dram_tensor("spill2", [8192 * 2048], BF16)
        self.d_lgm = nc.dram_tensor("lgm", [32 * 2048], F32)
        self.weight_tensors = {}

        self.c_handles = {}
        for k, v in self.consts.items():
            self.c_handles[k] = nc.dram_tensor("c_" + k, list(v.shape), F32,
                                               kind="ExternalInput")
            self.inputs["c_" + k] = np.ascontiguousarray(v)

        sb = lambda n, sh, dt: self.es.enter_context(nc.sbuf_tensor(n, sh, dt))
        ps = lambda n, sh: self.es.enter_context(nc.psum_tensor(n, sh, F32))
        self.XBF = sb("XBF", [128, NK * S], BF16)
        self.SA = sb("SA", [128, 32768], BF16)
        self.SB = sb("SBX", [128, 16384], BF16)
        self.SC = sb("SC", [128, 3072], F32)
        self.WBUF = sb("WBUF", [128, 4096], BF16)
        self.BCR = sb("BCR", [128, 2048], BF16)
        self.BCF = sb("BCF", [128, 4096], F32)
        self.ROWS = sb("ROWS", [128, 256], F32)
        self.LNC = sb("LNC", [128, 32], F32)
        self.CB = {}
        for k, v in self.consts.items():
            self.CB[k] = sb("cb_" + k, list(v.shape), CONST_DT[k])

        self.PB0 = ps("PB0", [128, 2048])
        self.PB1 = ps("PB1", [128, 2048])

        mksem = lambda n: Sem(self.es.enter_context(nc.semaphore(n)))
        self.sW = mksem("sW")
        self.sX = mksem("sX")
        self.sT = mksem("sT")
        self.sV = mksem("sV")
        self.sS = mksem("sS")
        self.sG = mksem("sG")
        self.sO = mksem("sO")

        self.block = self.es.enter_context(nc.Block())
        self.wbuf_tail = (self.sT, 0)
        self._prologue()

    def T(self, fn):
        self.block.tensor(fn)

    def V(self, fn):
        self.block.vector(fn)

    def Sc(self, fn):
        self.block.scalar(fn)

    def G(self, fn):
        self.block.gpsimd(fn)

    def Y(self, fn):
        self.block.sync(fn)

    def xbf(self, k, c0=0, w=S):
        return self.XBF[:, k * S + c0:k * S + c0 + w]

    def add_weight(self, name, arr):
        if name not in self.weight_tensors:
            h = self.nc.dram_tensor(name, list(arr.shape), F32,
                                    kind="ExternalInput")
            self.weight_tensors[name] = h
            self.inputs[name] = np.ascontiguousarray(arr, dtype=np.float32)
        return self.weight_tensors[name]

    def _prologue(self):
        mk, sW, sX = self, self.sW, self.sX

        def g(g):
            for k, h in mk.c_handles.items():
                g.dma_start(out=mk.CB[k][:], in_=h.ap()).then_inc(sW.h, 16)
            for i in range(4):
                g.dma_start(
                    out=bass.AP(mk.XBF, i * 4 * S, [[NK * S, 128], [S, 4], [1, S]]),
                    in_=bass.AP(mk.d_xin, i * 4 * 128 * S,
                                [[S, 128], [128 * S, 4], [1, S]]),
                ).then_inc(sW.h, 16)
        self.G(g)
        sW.inc(16 * len(mk.c_handles) + 64)

        def y(y):
            for i in range(4):
                y.dma_start(
                    out=bass.AP(mk.d_xres, i * 4 * 128 * S,
                                [[S, 128], [128 * S, 4], [1, S]]),
                    in_=bass.AP(mk.d_xin, i * 4 * 128 * S,
                                [[S, 128], [128 * S, 4], [1, S]]),
                ).then_inc(sX.h, 16)
        self.Y(y)
        sX.inc(64)
        self.xbf_ready = [(sW, sW.n), (sX, sX.n)]
        self.consts_ready = sW.n

    def row_to_dram(self, src_t, src_off, slot, wait, width=2048, part16=False):
        mk, sX = self, self.sX

        def y(y):
            y.wait_ge(wait[0].h, wait[1])
            if part16:
                y.dma_start(out=bass.AP(mk.d_bnc, slot * 2048, [[128, 16], [1, 128]]),
                            in_=src_t[0:16, src_off:src_off + 128]).then_inc(sX.h, 16)
            else:
                y.dma_start(out=bass.AP(mk.d_bnc, slot * 2048, [[1, 1], [1, width]]),
                            in_=src_t[0:1, src_off:src_off + width]).then_inc(sX.h, 16)
        self.Y(y)
        return (sX, sX.inc(16))

    def bcast_from_dram(self, slot, dst_t, dst_off, width, wait, use_g=False,
                        extra_wait=None):
        mk = self
        sem = self.sW if use_g else self.sX

        def f(e):
            e.wait_ge(wait[0].h, wait[1])
            if extra_wait is not None:
                e.wait_ge(extra_wait[0].h, extra_wait[1])
            e.dma_start(out=dst_t[:, dst_off:dst_off + width],
                        in_=bass.AP(mk.d_bnc, slot * 2048, [[0, 128], [1, width]])
                        ).then_inc(sem.h, 16)
        (self.G if use_g else self.Y)(f)
        return (sem, sem.inc(16))

    def emit_stats_center(self, rms2=False, r32=False):
        mk = self
        sT, sV, sS, sG, sX = self.sT, self.sV, self.sS, self.sG, self.sX
        BCF, SC = self.BCF, self.SC

        def t0(t):
            for sm, val in self.xbf_ready:
                t.wait_ge(sm.h, val)
            t.wait_ge(mk.sW.h, self.consts_ready)
            ins = None
            for k in range(NK):
                for b in range(4):
                    ins = t.matmul(mk.PB0[0:1, b * 512:(b + 1) * 512],
                                   mk.CB["ones_col"][:], mk.xbf(k, b * 512, 512),
                                   start=(k == 0), stop=(k == NK - 1))
            ins.then_inc(sT.h, 1)
        self.T(t0)
        sum_done = sT.inc()

        sq_s = {}
        sq_t = {}
        for k in range(NK):
            buf = (k % 2) * 2048

            def s(s, k=k, buf=buf):
                if k < 2:
                    for sm, val in self.xbf_ready:
                        s.wait_ge(sm.h, val)
                else:
                    s.wait_ge(sT.h, sq_t[k - 2])
                s.activation(mk.SA[:, buf:buf + S], mk.xbf(k),
                             AF.Square).then_inc(sS.h, 1)
            self.Sc(s)
            sq_s[k] = sS.inc()

            def t(t, k=k, buf=buf):
                t.wait_ge(sS.h, sq_s[k])
                ins = None
                for b in range(4):
                    ins = t.matmul(mk.PB1[0:1, b * 512:(b + 1) * 512],
                                   mk.CB["ones_col"][:],
                                   mk.SA[:, buf + b * 512:buf + (b + 1) * 512],
                                   start=(k == 0), stop=(k == NK - 1))
                ins.then_inc(sT.h, 1)
            self.T(t)
            sq_t[k] = sT.inc()
        mm_done = sq_t[NK - 1]

        def v(v):
            v.wait_ge(sT.h, mm_done)
            ins = v.tensor_scalar(BCF[0:1, 0:2048], mk.PB0[0:1, 0:2048],
                                  1.0 / D, None, ALU.mult)
            ins.then_inc(sV.h, 1)
        self.V(v)
        mu_done = sV.inc()
        w_mu = self.row_to_dram(BCF, 0, BNC_MU, (sV, mu_done))

        def v2(v):
            v.wait_ge(sX.h, w_mu[1])
            v.tensor_tensor(SC[0:1, 0:2048], BCF[0:1, 0:2048],
                            BCF[0:1, 0:2048], ALU.mult)
            v.memset(mk.ROWS[:, 12:13], 0.0)
            ins = v.scalar_tensor_tensor(BCF[0:1, 0:2048], mk.PB1[0:1, 0:2048],
                                         1.0 / D, SC[0:1, 0:2048],
                                         ALU.mult, ALU.subtract)
            ins.then_inc(sV.h, 1)
        self.V(v2)
        var_done = sV.inc()

        def s2(s):
            s.wait_ge(sV.h, var_done)
            s.activation(SC[0:1, 0:2048], BCF[0:1, 0:2048], AF.Ln, bias=mk.CB["eps5"][0:1, 0:1])
            ins = s.activation(BCF[0:1, 2048:4096], SC[0:1, 0:2048],
                               AF.Exp, scale=-0.5)
            ins.then_inc(sS.h, 1)
        self.Sc(s2)
        rln_done = sS.inc()

        if not rms2:
            r_final = (sS, rln_done)
        else:
            w_rln = self.row_to_dram(BCF, 2048, BNC_RLN, (sS, rln_done))

            def s3(s):
                s.wait_ge(sS.h, rln_done)
                ins = s.activation(BCF[0:1, 0:2048], SC[0:1, 0:2048],
                                   AF.Exp, scale=-1.0)
                ins.then_inc(sS.h, 1)
            self.Sc(s3)
            e2_done = sS.inc()

            def v3(v):
                v.wait_ge(sS.h, e2_done)
                ins = v.tensor_scalar(BCF[0:1, 0:2048], BCF[0:1, 0:2048],
                                      -1e-5, 1.0, ALU.mult, ALU.add)
                ins.then_inc(sV.h, 1)
            self.V(v3)
            u_done = sV.inc()

            def s4(s):
                s.wait_ge(sV.h, u_done)
                s.activation(SC[0:1, 0:2048], BCF[0:1, 0:2048], AF.Ln, bias=mk.CB["eps6"][0:1, 0:1])
                ins = s.activation(BCF[0:1, 0:2048], SC[0:1, 0:2048],
                                   AF.Exp, scale=-0.5)
                ins.then_inc(sS.h, 1)
            self.Sc(s4)
            rms_done = sS.inc()

            def v4(v):
                v.wait_ge(sS.h, rms_done)
                v.wait_ge(sX.h, w_rln[1])
                ins = v.tensor_tensor(BCF[0:1, 2048:4096], BCF[0:1, 2048:4096],
                                      BCF[0:1, 0:2048], ALU.mult)
                ins.then_inc(sV.h, 1)
            self.V(v4)
            r_final = (sV, sV.inc())

        w_r = self.row_to_dram(BCF, 2048, BNC_R, r_final)
        r_tile = self.bcast_from_dram(BNC_R, self.BCR, 0, 2048, w_r, use_g=True)
        mu_tile = self.bcast_from_dram(BNC_MU, self.BCF, 0, 2048, w_r)
        r32_tile = None
        if r32:
            r32_tile = self.bcast_from_dram(BNC_R, self.SC, 0, 2048, w_r)

        rc_last = None
        for jj in range(16):
            def t2(t, jj=jj, rc_last=rc_last):
                if jj == 0:
                    t.wait_ge(r_final[0].h, r_final[1])
                else:
                    t.wait_ge(sV.h, rc_last)
                t.transpose(mk.PB1[:, 1536:1537],
                            BCF[0:1, 2048 + jj * 128:2048 + (jj + 1) * 128],
                            mk.CB["ident"][0:1, 0:1]).then_inc(sT.h, 1)
            self.T(t2)
            td = sT.inc()

            def v5(v, jj=jj, td=td):
                v.wait_ge(sT.h, td)
                v.tensor_copy(mk.ROWS[:, 32 + jj:33 + jj],
                              mk.PB1[:, 1536:1537]).then_inc(sV.h, 1)
            self.V(v5)
            rc_last = sV.inc()
        rcol = (sV, rc_last)

        def gch(g):
            g.wait_ge(sX.h, mu_tile[1])
            ins = None
            for k in range(NK):
                ins = g.tensor_tensor(mk.xbf(k), mk.xbf(k),
                                      BCF[:, 0:2048], ALU.subtract)
            ins.then_inc(self.sG.h, 1)
        self.G(gch)
        centered = (self.sG, self.sG.inc())

        self.ln = dict(r_tile=r_tile, rcol=rcol, centered=centered,
                       stats_T=(sT, mm_done), r32_tile=r32_tile)
        return self.ln

    def emit_proj(self, wh, cols, dest_fn=None, scale=1.0, use_rtile=True,
                  extra_waits=(), n_in_chunks=NK, src_fn=None,
                  spill=None, stg_off=4096, r32=False, wb_w=False):
        mk = self
        sW, sT, sV, sO = self.sW, self.sT, self.sV, self.sO
        sp_d = {}
        if spill is not None:
            dest_fn = lambda ci, tb: mk.SB[:, stg_off + (ci % 2) * 2048 + tb * 512:
                                           stg_off + (ci % 2) * 2048 + (tb + 1) * 512]
        if src_fn is None:
            src_fn = lambda k, tb: mk.xbf(k, tb * 512, 512)
        C = wh.shape[1]
        mm_d, ev_d, wl_d = {}, {}, {}
        wbuf_w = self.wbuf_tail
        n = len(cols)
        for ci in range(n):
            col = cols[ci]
            par = ci % 2

            def g(g, ci=ci, col=col, par=par):
                if ci < 2:
                    g.wait_ge(wbuf_w[0].h, wbuf_w[1])
                else:
                    g.wait_ge(sT.h, mm_d[ci - 2])
                if wb_w:
                    g.dma_start(
                        out=bass.AP(mk.WBUF, par * 2048,
                                    [[4096, 128], [1, n_in_chunks * 128]]),
                        in_=bass.AP(wh, col * n_in_chunks * 128,
                                    [[C, 128], [1, n_in_chunks * 128]]),
                    ).then_inc(sW.h, 16)
                else:
                    g.dma_start(
                        out=bass.AP(mk.WBUF, par * 2048, [[4096, 128], [128, n_in_chunks], [1, 128]]),
                        in_=bass.AP(wh, col * 128, [[C, 128], [128 * C, n_in_chunks], [1, 128]]),
                    ).then_inc(sW.h, 16)
            self.G(g)
            wl_d[ci] = sW.inc(16)

            def t(t, ci=ci, par=par):
                t.wait_ge(sW.h, wl_d[ci])
                if ci == 0:
                    t.wait_ge(self.sG.h, self.ln["centered"][1])
                    for sm, val in extra_waits:
                        t.wait_ge(sm.h, val)
                if ci >= 2:
                    t.wait_ge(sV.h, ev_d[ci - 2])
                PB = mk.PB0 if par == 0 else mk.PB1
                ins = None
                for k in range(n_in_chunks):
                    for tb in range(4):
                        ins = t.matmul(PB[:, tb * 512:(tb + 1) * 512],
                                       mk.WBUF[:, par * 2048 + k * 128:par * 2048 + (k + 1) * 128],
                                       src_fn(k, tb),
                                       start=(k == 0), stop=(k == n_in_chunks - 1))
                ins.then_inc(sT.h, 1)
            self.T(t)
            mm_d[ci] = sT.inc()

            def v(v, ci=ci, par=par):
                v.wait_ge(sT.h, mm_d[ci])
                if spill is not None and ci >= 2:
                    v.wait_ge(sO.h, sp_d[ci - 2])
                if ci == 0 and use_rtile:
                    if r32:
                        v.wait_ge(self.ln["r32_tile"][0].h,
                                  self.ln["r32_tile"][1])
                    else:
                        v.wait_ge(self.ln["r_tile"][0].h,
                                  self.ln["r_tile"][1])
                PB = mk.PB0 if par == 0 else mk.PB1
                ins = None
                for tb in range(4):
                    pt = PB[:, tb * 512:(tb + 1) * 512]
                    dst = dest_fn(ci, tb)
                    if use_rtile:
                        rt = (mk.SC[:, tb * 512:(tb + 1) * 512] if r32
                              else mk.BCR[:, tb * 512:(tb + 1) * 512])
                        ins = v.scalar_tensor_tensor(
                            dst, pt, scale, rt, ALU.mult, ALU.mult)
                    elif scale != 1.0:
                        ins = v.tensor_scalar(dst, pt, scale, None, ALU.mult)
                    else:
                        ins = v.tensor_copy(dst, pt)
                ins.then_inc(sV.h, 1)
            self.V(v)
            ev_d[ci] = sV.inc()

            if spill is not None:
                def ysp(y, ci=ci):
                    y.wait_ge(sV.h, ev_d[ci])
                    y.dma_start(
                        out=bass.AP(spill[0], spill[1] + ci * 128 * 2048,
                                    [[2048, 128], [1, 2048]]),
                        in_=mk.SB[:, stg_off + (ci % 2) * 2048:
                                  stg_off + (ci % 2) * 2048 + 2048],
                    ).then_inc(sO.h, 16)
                self.Y(ysp)
                sp_d[ci] = sO.inc(16)
        self.wbuf_tail = (sT, mm_d[n - 1])
        return dict(ev=(sV, ev_d[n - 1]), mm=(sT, mm_d[n - 1]),
                    sp=(sO, sp_d[n - 1]) if spill is not None else None)

    def emit_vprod(self, w_sb_off, W, vdst_off, extra_waits=(), spill=None,
                   w_tensor=None, stg_off=0):
        mk = self
        sT, sV, sO = self.sT, self.sV, self.sO
        if w_tensor is None:
            w_tensor = mk.SA
        mm_d, ev_d, sp_d = {}, {}, {}
        for sc in range(16):
            par = sc % 2

            def t(t, sc=sc, par=par):
                if sc == 0:
                    t.wait_ge(self.sG.h, self.ln["centered"][1])
                    for sm, val in extra_waits:
                        t.wait_ge(sm.h, val)
                if sc >= 2:
                    t.wait_ge(sV.h, ev_d[sc - 2])
                PB = mk.PB0 if par == 0 else mk.PB1
                ins = None
                for k in range(NK):
                    for w0 in range(0, W, 512):
                        ww = min(512, W - w0)
                        ins = t.matmul(PB[:, w0:w0 + ww],
                                       mk.xbf(k, sc * 128, 128),
                                       w_tensor[:, w_sb_off + k * W + w0:
                                                w_sb_off + k * W + w0 + ww],
                                       start=(k == 0), stop=(k == NK - 1))
                ins.then_inc(sT.h, 1)
            self.T(t)
            mm_d[sc] = sT.inc()

            def v(v, sc=sc, par=par):
                v.wait_ge(sT.h, mm_d[sc])
                if sc == 0:
                    v.wait_ge(sV.h, self.ln["rcol"][1])
                if spill is not None and sc >= 2:
                    v.wait_ge(sO.h, sp_d[sc - 2])
                PB = mk.PB0 if par == 0 else mk.PB1
                off = (stg_off + (sc % 2) * W) if spill is not None                     else (vdst_off + sc * W)
                ins = v.tensor_scalar(mk.SB[:, off:off + W],
                                      PB[:, 0:W], mk.ROWS[:, 32 + sc:33 + sc],
                                      None, ALU.mult)
                ins.then_inc(sV.h, 1)
            self.V(v)
            ev_d[sc] = sV.inc()

            if spill is not None:
                def ysp(y, sc=sc):
                    y.wait_ge(sV.h, ev_d[sc])
                    y.dma_start(
                        out=bass.AP(spill[0], spill[1] + sc * 128 * W,
                                    [[W, 128], [1, W]]),
                        in_=mk.SB[:, stg_off + (sc % 2) * W:
                                  stg_off + (sc % 2) * W + W],
                    ).then_inc(sO.h, 16)
                self.Y(ysp)
                sp_d[sc] = sO.inc(16)
        return (sV, ev_d[15]) if spill is None else (sO, sp_d[15])

    def emit_attn_core(self, nheads, kt_off_fn, v_lhsT_fn, qkv_wait,
                       kt_tensor=None, q_src_fn=None, ot_dst_fn=None,
                       stage_ot=False, window=None, head_pre=None,
                       sel_cfg=False):
        mk = self
        sT, sV, sS, sX = self.sT, self.sV, self.sS, self.sX
        if kt_tensor is None:
            kt_tensor = mk.SB
        if q_src_fn is None:
            q_src_fn = lambda h: (mk.SA, h * 2048)
        if ot_dst_fn is None:
            ot_dst_fn = lambda h: (mk.SA, h * 2048)
        OT_STAGE = 14336
        exp_done = {}
        prev_tail = None
        pass2_stt_tail = {}
        for h in range(nheads):
            pre_wait = head_pre(h, prev_tail) if head_pre is not None else None
            kt0 = kt_off_fn(h)
            qsrc_t, qoff = q_src_fn(h)
            mm1 = {}
            for qt in range(16):
                if window is None:
                    klo = 0
                else:
                    klo = max(0, qt * 128 - window)
                khi = (qt + 1) * 128
                ext = khi - klo
                PB = mk.PB0 if qt % 2 == 0 else mk.PB1

                def t(t, h=h, qt=qt, klo=klo, ext=ext, PB=PB, kt0=kt0,
                      qoff=qoff, qsrc_t=qsrc_t, prev_tail=prev_tail,
                      pre_wait=pre_wait):
                    if qt < 2:
                        if h == 0:
                            t.wait_ge(qkv_wait[0].h, qkv_wait[1])
                        else:
                            t.wait_ge(prev_tail[0].h, prev_tail[1])
                            t.wait_ge(sS.h, exp_done[(h - 1, 14 + qt)])
                        if pre_wait is not None:
                            t.wait_ge(pre_wait[0].h, pre_wait[1])
                    else:
                        t.wait_ge(sS.h, exp_done[(h, qt - 2)])
                    ins = None
                    for b in range((ext + 511) // 512):
                        w = min(512, ext - b * 512)
                        ins = t.matmul(PB[:, b * 512:b * 512 + w],
                                       qsrc_t[:, qoff + qt * 128:qoff + (qt + 1) * 128],
                                       kt_tensor[:, kt0 + klo + b * 512:kt0 + klo + b * 512 + w],
                                       start=True, stop=True)
                    ins.then_inc(sT.h, 1)
                self.T(t)
                mm1[qt] = sT.inc()

                def v(v, qt=qt, klo=klo, ext=ext, PB=PB, h=h):
                    v.wait_ge(sT.h, mm1[qt])
                    v.reduce_max(mk.ROWS[:, 16 + qt:17 + qt], PB[:, 0:ext],
                                 axis=AX.X, negate=True)
                    if sel_cfg:
                        nb = (qt + 1) // 2
                        if nb >= 1:
                            v.tensor_tensor(
                                PB[:, 0:nb * 256], PB[:, 0:nb * 256],
                                bass.AP(mk.SB, 10240 + qt * 128 + h * 8,
                                        [[16384, 128], [1, nb], [0, 256]]),
                                ALU.add)
                    doff = qt * 128 - klo
                    ins = v.tensor_tensor(PB[:, doff:ext],
                                          PB[:, doff:ext],
                                          mk.CB["mask_qdiag"][:], ALU.add)
                    if window is not None and qt * 128 - window >= 0:
                        ins = v.tensor_tensor(PB[:, 0:128], PB[:, 0:128],
                                              mk.CB["mask_qlow"][:], ALU.add)
                    ins.then_inc(sV.h, 1)
                self.V(v)
                max_d = sV.inc()

                def s(s, qt=qt, ext=ext, PB=PB, max_d=max_d):
                    s.wait_ge(sV.h, max_d)
                    ins = s.activation(mk.WBUF[:, 2048:2048 + ext], PB[:, 0:ext],
                                       AF.Exp, bias=mk.ROWS[:, 16 + qt:17 + qt],
                                       accum_out=mk.ROWS[:, 48 + qt:49 + qt])
                    ins.then_inc(sS.h, 1)
                self.Sc(s)
                exp_done[(h, qt)] = sS.inc()

            def sln(s, h=h):
                s.wait_ge(sS.h, exp_done[(h, 15)])
                ins = s.activation(mk.ROWS[:, 208:224], mk.ROWS[:, 48:64],
                                   AF.Ln)
                ins.then_inc(sS.h, 1)
            self.Sc(sln)
            ln_d = sS.inc()

            def vc2(v, ln_d=ln_d):
                v.wait_ge(sS.h, ln_d)
                ins = v.tensor_tensor(mk.ROWS[:, 16:32], mk.ROWS[:, 208:224],
                                      mk.ROWS[:, 16:32], ALU.subtract)
                ins.then_inc(sV.h, 1)
            self.V(vc2)
            sV.inc()
            cmat_done = (sV, sV.n)

            def t2(t, cm=cmat_done):
                t.wait_ge(sV.h, cm[1])
                t.transpose(mk.PB1[0:16, 1536:1664], mk.ROWS[:, 16:32],
                            mk.CB["ident"][:]).then_inc(sT.h, 1)
            self.T(t2)
            ctp = sT.inc()

            def v3(v, ctp=ctp):
                v.wait_ge(sT.h, ctp)
                v.tensor_copy(mk.ROWS[0:16, 64:192],
                              mk.PB1[0:16, 1536:1664]).then_inc(sV.h, 1)
            self.V(v3)
            cev = sV.inc()
            wrow = self.row_to_dram(mk.ROWS, 64, BNC_C0 + h % 2, (sV, cev),
                                    part16=True)
            ew = None
            if h >= 2:
                ew = (sV, pass2_stt_tail[h - 2])
            ctile = self.bcast_from_dram(BNC_C0 + h % 2, mk.BCF,
                                         (h % 2) * 2048, 2048, wrow,
                                         extra_wait=ew)

            coff = (h % 2) * 2048
            ktg = 0
            stt_done = {}
            pexp_done = {}
            selmul_done = {}
            pv_mm = {}
            otv = {}
            for qb in range(4):
                if window is None:
                    kts = list(range(qb * 4 + 4))
                else:
                    kts = list(range(max(0, qb * 4 - window // 128),
                                     qb * 4 + 4))
                nkt = len(kts)
                for ki, kt in enumerate(kts):
                    pslot = ktg % 4
                    sslot = ktg % 2
                    ptile = ktg % 4

                    def t3(t, qb=qb, kt=kt, ktg=ktg, pslot=pslot, kt0=kt0,
                           qoff=qoff, qsrc_t=qsrc_t, ct=ctile):
                        if ktg == 0:
                            t.wait_ge(sX.h, ct[1])
                        if ktg >= 4:
                            t.wait_ge(sV.h, stt_done[ktg - 4])
                        t.matmul(mk.PB1[:, pslot * 512:(pslot + 1) * 512],
                                 kt_tensor[:, kt0 + kt * 128:kt0 + (kt + 1) * 128],
                                 qsrc_t[:, qoff + qb * 512:qoff + (qb + 1) * 512],
                                 start=True, stop=True).then_inc(sT.h, 1)
                    self.T(t3)
                    st_mm = sT.inc()

                    def v4(v, qb=qb, kt=kt, ktg=ktg, pslot=pslot, sslot=sslot,
                           st_mm=st_mm, coff=coff):
                        v.wait_ge(sT.h, st_mm)
                        if ktg >= 2:
                            v.wait_ge(sS.h, pexp_done[ktg - 2])
                        ins = v.scalar_tensor_tensor(
                            mk.SC[:, 2048 + sslot * 512:2048 + (sslot + 1) * 512],
                            mk.PB1[:, pslot * 512:(pslot + 1) * 512], 1.0,
                            mk.BCF[:, coff + qb * 512:coff + (qb + 1) * 512],
                            ALU.mult, ALU.subtract)
                        ins.then_inc(sV.h, 1)
                    self.V(v4)
                    stt_done[ktg] = sV.inc()

                    def s2(s, ktg=ktg, sslot=sslot, ptile=ptile):
                        s.wait_ge(sV.h, stt_done[ktg])
                        if ktg >= 4:
                            s.wait_ge(sT.h, pv_mm[ktg - 4])
                        ins = s.activation(
                            mk.WBUF[:, ptile * 512:(ptile + 1) * 512],
                            mk.SC[:, 2048 + sslot * 512:2048 + (sslot + 1) * 512],
                            AF.Exp)
                        ins.then_inc(sS.h, 1)
                    self.Sc(s2)
                    pexp_done[ktg] = sS.inc()

                    if sel_cfg:
                        blk = kt // 2
                        mslot = ktg % 2

                        def tmsk(t, h=h, qb=qb, ktg=ktg, blk=blk, mslot=mslot):
                            if ktg < 2:
                                t.wait_ge(sS.h, exp_done[(h, 14)])
                            else:
                                t.wait_ge(sV.h, selmul_done[ktg - 2])
                            moff = 512 + mslot * 512
                            t.matmul(mk.PB0[0:64, moff:moff + 512],
                                     mk.CB["epat"][0:8, blk * 64:blk * 64 + 64],
                                     mk.SB[0:8,
                                           8192 + qb * 512:8192 + (qb + 1) * 512],
                                     start=True, stop=True)
                            t.matmul(mk.PB0[64:128, moff:moff + 512],
                                     mk.CB["epat"][0:8, blk * 64:blk * 64 + 64],
                                     mk.SB[0:8,
                                           8192 + qb * 512:8192 + (qb + 1) * 512],
                                     start=True, stop=True,
                                     tile_position=(0, 64)).then_inc(sT.h, 1)
                        self.T(tmsk)
                        tm = sT.inc()

                        def v4s(v, ktg=ktg, ptile=ptile, mslot=mslot, tm=tm):
                            v.wait_ge(sT.h, tm)
                            v.wait_ge(sS.h, pexp_done[ktg])
                            ins = v.tensor_tensor(
                                mk.WBUF[:, ptile * 512:(ptile + 1) * 512],
                                mk.WBUF[:, ptile * 512:(ptile + 1) * 512],
                                mk.PB0[:, 512 + mslot * 512:1024 + mslot * 512],
                                ALU.mult)
                            ins.then_inc(sV.h, 1)
                            v.memset(mk.ROWS[:, 12:13], 0.0)
                        self.V(v4s)
                        selmul_done[ktg] = sV.inc()

                    pmask_done = None
                    if kt >= qb * 4:
                        vv = kt - qb * 4

                        def v4b(v, ktg=ktg, ptile=ptile, vv=vv):
                            if sel_cfg:
                                v.wait_ge(sV.h, selmul_done[ktg])
                            else:
                                v.wait_ge(sS.h, pexp_done[ktg])
                            ins = v.tensor_tensor(
                                mk.WBUF[:, ptile * 512:(ptile + 1) * 512],
                                mk.WBUF[:, ptile * 512:(ptile + 1) * 512],
                                mk.CB["bin_kdiag_m"][:, 384 - vv * 128:896 - vv * 128],
                                ALU.mult)
                            ins.then_inc(sV.h, 1)
                        self.V(v4b)
                        pmask_done = sV.inc()
                    elif window is not None and kt * 128 < qb * 512 - window + 512:
                        vv = kt - (qb * 4 - window // 128)

                        def v4c(v, ktg=ktg, ptile=ptile, vv=vv):
                            if sel_cfg:
                                v.wait_ge(sV.h, selmul_done[ktg])
                            else:
                                v.wait_ge(sS.h, pexp_done[ktg])
                            ins = v.tensor_tensor(
                                mk.WBUF[:, ptile * 512:(ptile + 1) * 512],
                                mk.WBUF[:, ptile * 512:(ptile + 1) * 512],
                                mk.CB["bin_klow_m"][:, 384 - vv * 128:896 - vv * 128],
                                ALU.mult)
                            ins.then_inc(sV.h, 1)
                        self.V(v4c)
                        pmask_done = sV.inc()

                    def t4(t, h=h, qb=qb, ki=ki, ktg=ktg, kt=kt, ptile=ptile,
                           nkt=nkt, pmask_done=pmask_done):
                        if pmask_done is not None:
                            t.wait_ge(sV.h, pmask_done)
                        elif sel_cfg:
                            t.wait_ge(sV.h, selmul_done[ktg])
                        else:
                            t.wait_ge(sS.h, pexp_done[ktg])
                        if ki == 0 and qb >= 1:
                            t.wait_ge(sV.h, otv[qb - 1])
                        t.matmul(mk.PB0[:, 0:512], v_lhsT_fn(h, kt),
                                 mk.WBUF[:, ptile * 512:(ptile + 1) * 512],
                                 start=(ki == 0), stop=(ki == nkt - 1)
                                 ).then_inc(sT.h, 1)
                    self.T(t4)
                    pv_mm[ktg] = sT.inc()
                    ktg += 1
                pv_done = (sT, sT.n)

                ot_t, ot_off = (mk.SB, OT_STAGE) if stage_ot else ot_dst_fn(h)

                def v5(v, qb=qb, pv=pv_done, ot_t=ot_t, ot_off=ot_off):
                    v.wait_ge(pv[0].h, pv[1])
                    ins = v.tensor_copy(
                        ot_t[:, ot_off + qb * 512:ot_off + (qb + 1) * 512],
                        mk.PB0[:, 0:512])
                    ins.then_inc(sV.h, 1)
                self.V(v5)
                otv[qb] = sV.inc()
            if stage_ot:
                dt_t, dt_off = ot_dst_fn(h)

                def v6(v, dt_t=dt_t, dt_off=dt_off, w=otv[3]):
                    v.wait_ge(sV.h, w)
                    ins = v.tensor_copy(dt_t[:, dt_off:dt_off + 2048],
                                        mk.SB[:, OT_STAGE:OT_STAGE + 2048])
                    ins.then_inc(sV.h, 1)
                self.V(v6)
                sV.inc()
            pass2_stt_tail[h] = stt_done[ktg - 1]
            prev_tail = (sV, sV.n)
        return (sV, sV.n)

    def emit_out_proj_residual(self, wh, last, scale=1.0, src_chunk_fn=None,
                               n_in_chunks=NK, attn_wait=None, mult_tile=None,
                               wb_w=False):
        mk = self
        sW, sT, sV, sG, sX, sO = self.sW, self.sT, self.sV, self.sG, self.sX, self.sO
        if src_chunk_fn is None:
            src_chunk_fn = lambda k, tb: mk.SA[:, k * 2048 + tb * 512:k * 2048 + (tb + 1) * 512]
        C = wh.shape[1]
        mm_d, ev_d, wl_d, xl_d, st_d, cast_d = {}, {}, {}, {}, {}, {}
        dst_dram = mk.d_xout if last else mk.d_xres
        wbuf_w = self.wbuf_tail
        for m in range(16):
            par = m % 2

            def g(g, m=m, par=par):
                if m < 2:
                    g.wait_ge(wbuf_w[0].h, wbuf_w[1])
                    if attn_wait is not None:
                        g.wait_ge(attn_wait[0].h, attn_wait[1])
                else:
                    g.wait_ge(sT.h, mm_d[m - 2])
                if wb_w:
                    g.dma_start(
                        out=bass.AP(mk.WBUF, par * 2048,
                                    [[4096, 128], [1, n_in_chunks * 128]]),
                        in_=bass.AP(wh, m * n_in_chunks * 128,
                                    [[C, 128], [1, n_in_chunks * 128]]),
                    ).then_inc(sW.h, 16)
                else:
                    g.dma_start(
                        out=bass.AP(mk.WBUF, par * 2048, [[4096, 128], [128, n_in_chunks], [1, 128]]),
                        in_=bass.AP(wh, m * 128, [[C, 128], [128 * C, n_in_chunks], [1, 128]]),
                    ).then_inc(sW.h, 16)
            self.G(g)
            wl_d[m] = sW.inc(16)

            def y(y, m=m):
                if m == 0 and attn_wait is not None:
                    y.wait_ge(attn_wait[0].h, attn_wait[1])
                if m >= 1:
                    y.wait_ge(sO.h, st_d[m - 1])
                    y.wait_ge(sG.h, cast_d[m - 1])
                for hh in range(2):
                    y.dma_start(
                        out=mk.SC[:, hh * 1024:(hh + 1) * 1024],
                        in_=bass.AP(mk.d_xres, m * 128 * S + hh * 1024,
                                    [[S, 128], [1, 1024]]),
                    ).then_inc(sX.h, 16)
            self.Y(y)
            xl_d[m] = sX.inc(32)

            def t(t, m=m, par=par):
                t.wait_ge(sW.h, wl_d[m])
                if m == 0 and attn_wait is not None:
                    t.wait_ge(attn_wait[0].h, attn_wait[1])
                if m >= 2:
                    t.wait_ge(sV.h, ev_d[m - 2])
                PB = mk.PB0 if par == 0 else mk.PB1
                ins = None
                for k in range(n_in_chunks):
                    for tb in range(4):
                        ins = t.matmul(PB[:, tb * 512:(tb + 1) * 512],
                                       mk.WBUF[:, par * 2048 + k * 128:par * 2048 + (k + 1) * 128],
                                       src_chunk_fn(k, tb),
                                       start=(k == 0), stop=(k == n_in_chunks - 1))
                ins.then_inc(sT.h, 1)
            self.T(t)
            mm_d[m] = sT.inc()

            def v(v, m=m, par=par):
                v.wait_ge(sT.h, mm_d[m])
                v.wait_ge(sX.h, xl_d[m])
                PB = mk.PB0 if par == 0 else mk.PB1
                ins = None
                if mult_tile is None:
                    for tb in range(4):
                        ins = v.scalar_tensor_tensor(
                            mk.SC[:, tb * 512:(tb + 1) * 512],
                            PB[:, tb * 512:(tb + 1) * 512], scale,
                            mk.SC[:, tb * 512:(tb + 1) * 512],
                            ALU.mult, ALU.add)
                else:
                    mt_t, mt_off = mult_tile
                    for tb in range(4):
                        v.scalar_tensor_tensor(
                            mk.SC[:, 2048 + (tb % 2) * 512:2048 + (tb % 2 + 1) * 512],
                            PB[:, tb * 512:(tb + 1) * 512], scale,
                            mt_t[:, mt_off + tb * 512:mt_off + (tb + 1) * 512],
                            ALU.mult, ALU.mult)
                        if tb >= 1:
                            v.tensor_tensor(
                                mk.SC[:, (tb - 1) * 512:tb * 512],
                                mk.SC[:, (tb - 1) * 512:tb * 512],
                                mk.SC[:, 2048 + ((tb - 1) % 2) * 512:2048 + ((tb - 1) % 2 + 1) * 512],
                                ALU.add)
                    v.memset(mk.ROWS[:, 12:13], 0.0)
                    ins = v.tensor_tensor(
                        mk.SC[:, 3 * 512:4 * 512], mk.SC[:, 3 * 512:4 * 512],
                        mk.SC[:, 2048 + 512:2048 + 1024], ALU.add)
                ins.then_inc(sV.h, 1)
            self.V(v)
            ev_d[m] = sV.inc()

            def gc(g, m=m):
                g.wait_ge(sV.h, ev_d[m])
                ins = None
                for tb in range(4):
                    ins = g.tensor_copy(mk.xbf(m, tb * 512, 512),
                                        mk.SC[:, tb * 512:(tb + 1) * 512])
                ins.then_inc(sG.h, 1)
            self.G(gc)
            cast_d[m] = sG.inc()

            def y2(y, m=m):
                y.wait_ge(sV.h, ev_d[m])
                for hh in range(2):
                    y.dma_start(
                        out=bass.AP(dst_dram, m * 128 * S + hh * 1024,
                                    [[S, 128], [1, 1024]]),
                        in_=mk.SC[:, hh * 1024:(hh + 1) * 1024],
                    ).then_inc(sO.h, 16)
            self.Y(y2)
            st_d[m] = sO.inc(32)
        self.xbf_ready = [(sG, cast_d[15])]
        self.wbuf_tail = (sT, mm_d[15])
        return (sO, st_d[15])

    def finish(self):
        mk = self

        def y(y):
            y.wait_ge(mk.sO.h, mk.sO.n)
        self.Y(y)
        self.es.close()


def sub_attn_b(mk, W, last):
    wq = mk.add_weight("L0b_wq", W["w_q_wb"])
    wkv = mk.add_weight("L0b_wkv", W["w_kv"])
    wkvb = mk.add_weight("L0b_wkv_wb", W["w_kv_wb"])
    wd = mk.add_weight("L0b_wd", W["w_dense_wb"])

    mk.emit_stats_center(rms2=False)

    kp = mk.emit_proj(wkvb, [0, 1, 2, 3],
                      lambda ci, tb: mk.SB[:, ci * 2048 + tb * 512:ci * 2048 + (tb + 1) * 512],
                      wb_w=True)

    def gv(g):
        g.wait_ge(mk.sT.h, mk.ln["stats_T"][1])
        g.dma_start(
            out=bass.AP(mk.SA, 0, [[32768, 128], [512, NK], [1, 512]]),
            in_=bass.AP(wkv, 512, [[1024, 128], [128 * 1024, NK], [1, 512]]),
        ).then_inc(mk.sW.h, 16)
    mk.G(gv)
    vw_loaded = mk.sW.inc(16)

    vp = mk.emit_vprod(0, 512, 8192,
                       extra_waits=[(mk.sW, vw_loaded), kp["ev"]])

    qp = mk.emit_proj(wq, list(range(16)),
                      lambda ci, tb: mk.SA[:, ci * 2048 + tb * 512:ci * 2048 + (tb + 1) * 512],
                      scale=1.0 / math.sqrt(128.0),
                      extra_waits=[vp], wb_w=True)

    att = mk.emit_attn_core(
        nheads=16,
        kt_off_fn=lambda h: (h // 4) * 2048,
        v_lhsT_fn=lambda h, kt: mk.SB[:, 8192 + kt * 512 + (h // 4) * 128:
                                      8192 + kt * 512 + (h // 4) * 128 + 128],
        qkv_wait=qp["ev"])

    return mk.emit_out_proj_residual(wd, last, attn_wait=att, wb_w=True)



def emit_streamed_attention(mk, wv_src, kt_cols, q_cols, wh_qkv, qscale,
                            window=None, nheads=16, kt_w=None, q_w=None,
                            m_wait=None):
    sW, sX, sT, sV = mk.sW, mk.sX, mk.sT, mk.sV
    QSP = (mk.d_spill, 0)
    VSP = (mk.d_spill, 2048 * 2048)

    wvh, wvoff = wv_src
    Cw = wvh.shape[1]

    def gwv(g):
        g.wait_ge(sT.h, mk.ln["stats_T"][1])
        for i in range(4):
            g.dma_start(
                out=bass.AP(mk.SA, i * 4 * 2048, [[32768, 128], [2048, 4], [1, 2048]]),
                in_=bass.AP(wvh, wvoff + i * 4 * 128 * Cw,
                            [[Cw, 128], [128 * Cw, 4], [1, 2048]]),
            ).then_inc(sW.h, 16)
    mk.G(gwv)
    wv_ld = sW.inc(64)
    vw = [(sW, wv_ld)] + ([m_wait] if m_wait is not None else [])
    vp = mk.emit_vprod(0, 2048, 0, extra_waits=vw,
                       spill=VSP, w_tensor=mk.SA, stg_off=0)
    kp = mk.emit_proj(kt_w if kt_w is not None else wh_qkv, kt_cols,
                      lambda ci, tb: mk.SA[:, ci * 2048 + tb * 512:ci * 2048 + (tb + 1) * 512],
                      extra_waits=[vp], wb_w=True)
    qp = mk.emit_proj(q_w if q_w is not None else wh_qkv, q_cols, scale=qscale,
                      spill=QSP, stg_off=4096, extra_waits=[kp["ev"]], wb_w=True)

    qv_ld = {}

    def emit_qv_load(hh, waits):
        def y0(y, hh=hh, waits=waits):
            for sm, val in waits:
                y.wait_ge(sm.h, val)
            y.dma_start(
                out=mk.SB[:, (hh % 2) * 2048:(hh % 2) * 2048 + 2048],
                in_=bass.AP(mk.d_spill, QSP[1] + hh * 128 * 2048,
                            [[2048, 128], [1, 2048]])).then_inc(sX.h, 16)
            y.dma_start(
                out=mk.SB[:, 4096 + (hh % 2) * 2048:4096 + (hh % 2) * 2048 + 2048],
                in_=bass.AP(mk.d_spill, VSP[1] + hh * 128,
                            [[2048, 128], [128 * 2048, 16], [1, 128]])
            ).then_inc(sX.h, 16)
        mk.Y(y0)
        qv_ld[hh] = sX.inc(32)

    def head_pre(h, prev_tail):
        if h == 0:
            emit_qv_load(0, [qp["sp"], vp])
        if h + 1 < nheads:
            emit_qv_load(h + 1, [prev_tail] if prev_tail is not None
                         else [qp["sp"], vp])
        return (sX, qv_ld[h])

    att = mk.emit_attn_core(
        nheads=nheads, kt_tensor=mk.SA, kt_off_fn=lambda h: h * 2048,
        q_src_fn=lambda h: (mk.SB, (h % 2) * 2048),
        v_lhsT_fn=lambda h, kt: mk.SB[:, 4096 + (h % 2) * 2048 + kt * 128:
                                      4096 + (h % 2) * 2048 + (kt + 1) * 128],
        ot_dst_fn=lambda h: (mk.SA, h * 2048),
        stage_ot=True, head_pre=head_pre, qkv_wait=qp["ev"], window=window)
    return att, qp


def sub_attn_a(mk, W, last):
    wqkv = mk.add_weight("L0a_wqkv", W["wqkv_l"])
    wqkvb = mk.add_weight("L0a_wqkv_wb", W["wqkv_wb"])
    wout = mk.add_weight("L0a_wout", W["wout_wb"])
    mk.emit_stats_center(rms2=False)
    att, qp = emit_streamed_attention(
        mk, (wqkv, 4096), list(range(16, 32)), list(range(16)),
        wqkvb, 1.0 / math.sqrt(128.0), window=512)
    return mk.emit_out_proj_residual(wout, last, attn_wait=att, wb_w=True)


def sub_ffn(mk, W, last, stage=99):
    nm = "L0f_"
    wq = mk.add_weight(nm + "wq", W["wq_l"])
    wk = mk.add_weight(nm + "wk", W["wk_l"])
    wv = mk.add_weight(nm + "wv", W["wv_l"])
    wo = mk.add_weight(nm + "wo", W["wo_raw"])
    wa = mk.add_weight(nm + "A", W["A"])
    wb = mk.add_weight(nm + "B", W["B"])
    win = mk.add_weight(nm + "win", W["win_l"])
    wup = mk.add_weight(nm + "wup", W["wup_l"])
    wdn = mk.add_weight(nm + "wdn", W["wdn_l"])
    lc1 = mk.add_weight(nm + "ln1c", W["ln1c"])
    lc2 = mk.add_weight(nm + "ln2c", W["ln2c"])
    sW, sX, sT, sV, sS, sG, sO = (mk.sW, mk.sX, mk.sT, mk.sV, mk.sS, mk.sG,
                                  mk.sO)
    QSP = (mk.d_spill, 0)
    VSP = (mk.d_spill, 2048 * 2048)

    ln = mk.emit_stats_center(rms2=True)
    if stage <= 0:
        return (mk.sO, mk.sO.n)

    def gld(g):
        g.dma_start(out=mk.SB[:, 8192:8448],
                    in_=bass.AP(wa, 0, [[16, 128], [128 * 16, 16], [1, 16]]),
                    ).then_inc(sW.h, 16)
        g.dma_start(out=mk.LNC[:, 0:16],
                    in_=lc1.ap()).then_inc(sW.h, 16)
        g.dma_start(out=mk.LNC[:, 16:32],
                    in_=lc2.ap()).then_inc(sW.h, 16)
    mk.G(gld)
    small_ld = sW.inc(48)

    wbw = mk.wbuf_tail
    mb_mm, mb_ev, mb_wl = {}, {}, {}
    for ec in range(16):
        par = ec % 2

        def g(g, ec=ec, par=par):
            if ec < 2:
                g.wait_ge(wbw[0].h, wbw[1])
            else:
                g.wait_ge(sT.h, mb_mm[ec - 2])
            g.dma_start(
                out=bass.AP(mk.WBUF, par * 2048, [[4096, 128], [1, 2048]]),
                in_=bass.AP(wo, ec * 2048, [[32768, 128], [1, 2048]]),
            ).then_inc(sW.h, 16)
        mk.G(g)
        mb_wl[ec] = sW.inc(16)

        def t(t, ec=ec, par=par):
            t.wait_ge(sW.h, mb_wl[ec])
            if ec == 0:
                t.wait_ge(sW.h, small_ld)
            if ec >= 2:
                t.wait_ge(sV.h, mb_ev[ec - 2])
            PB = mk.PB0 if par == 0 else mk.PB1
            ins = None
            for fc in range(NK):
                ins = t.matmul(PB[:, 0:16],
                               mk.WBUF[:, par * 2048 + fc * 128:par * 2048 + (fc + 1) * 128],
                               mk.SB[:, 8192 + fc * 16:8192 + (fc + 1) * 16],
                               start=(fc == 0), stop=(fc == NK - 1))
            ins.then_inc(sT.h, 1)
        mk.T(t)
        mb_mm[ec] = sT.inc()

        def v(v, ec=ec, par=par):
            v.wait_ge(sT.h, mb_mm[ec])
            PB = mk.PB0 if par == 0 else mk.PB1
            ins = v.tensor_copy(mk.SB[:, 8448 + ec * 16:8448 + (ec + 1) * 16],
                                PB[:, 0:16])
            ins.then_inc(sV.h, 1)
        mk.V(v)
        mb_ev[ec] = sV.inc()
    mk.wbuf_tail = (sT, mb_mm[15])
    m_done = (sV, mb_ev[15])

    def gs1(g):
        g.wait_ge(sG.h, ln["centered"][1])
        g.wait_ge(sW.h, small_ld)
        ins = None
        for k in range(NK):
            ins = g.tensor_scalar(mk.xbf(k), mk.xbf(k), mk.LNC[:, k:k + 1],
                                  None, ALU.mult)
        ins.then_inc(sG.h, 1)
    mk.G(gs1)
    xscaled1 = (sG, sG.inc())
    mk.ln["centered"] = xscaled1

    if stage <= 3:
        return (mk.sO, mk.sO.n)
    att, qp = emit_streamed_attention(
        mk, (wv, 0), list(range(16)), list(range(16)), None,
        1.0 / math.sqrt(128.0), kt_w=wk, q_w=wq, m_wait=m_done)

    if stage <= 4:
        return (mk.sO, mk.sO.n)
    md_mm = {}
    for tb in range(4):
        def t(t, tb=tb):
            if tb == 0:
                t.wait_ge(att[0].h, att[1])
            ins = None
            for k in range(NK):
                ins = t.matmul(mk.PB0[0:16, tb * 512:(tb + 1) * 512],
                               mk.SB[:, 8448 + k * 16:8448 + (k + 1) * 16],
                               mk.SA[:, k * 2048 + tb * 512:k * 2048 + (tb + 1) * 512],
                               start=(k == 0), stop=(k == NK - 1))
            ins.then_inc(sT.h, 1)
        mk.T(t)
        md_mm[tb] = sT.inc()

    def smod(s):
        ins = None
        for tb in range(4):
            s.wait_ge(sT.h, md_mm[tb])
            ins = s.activation(mk.BCF[0:16, 2048 + tb * 512:2048 + (tb + 1) * 512],
                               mk.PB0[0:16, tb * 512:(tb + 1) * 512], AF.Tanh)
        ins.then_inc(sS.h, 1)
    mk.Sc(smod)
    mod_done = (sS, sS.inc())

    def gre(g):
        g.wait_ge(qp["mm"][0].h, qp["mm"][1])
        for i in range(4):
            g.dma_start(
                out=bass.AP(mk.XBF, i * 4 * S, [[NK * S, 128], [S, 4], [1, S]]),
                in_=bass.AP(mk.d_xres, i * 4 * 128 * S,
                            [[S, 128], [128 * S, 4], [1, S]]),
            ).then_inc(sX.h, 16)
    mk.G(gre)
    xre = sX.inc(64)
    mu_tile2 = mk.bcast_from_dram(BNC_MU, mk.BCF, 0, 2048, (sX, xre),
                                  extra_wait=att)

    def gc2(g):
        g.wait_ge(sX.h, mu_tile2[1])
        ins = None
        for k in range(NK):
            ins = g.tensor_tensor(mk.xbf(k), mk.xbf(k), mk.BCF[:, 0:2048],
                                  ALU.subtract)
        ins.then_inc(sG.h, 1)
        for k in range(NK):
            ins = g.tensor_scalar(mk.xbf(k), mk.xbf(k), mk.LNC[:, 16 + k:17 + k],
                                  None, ALU.mult)
        ins.then_inc(sG.h, 1)
    mk.G(gc2)
    sG.inc()
    x2ready = (sG, sG.inc())

    ha_mm = {}
    for tb in range(4):
        def t(t, tb=tb):
            if tb == 0:
                t.wait_ge(x2ready[0].h, x2ready[1])
            ins = None
            for k in range(NK):
                ins = t.matmul(mk.PB1[0:16, tb * 512:(tb + 1) * 512],
                               mk.SB[:, 8192 + k * 16:8192 + (k + 1) * 16],
                               mk.xbf(k, tb * 512, 512),
                               start=(k == 0), stop=(k == NK - 1))
            ins.then_inc(sT.h, 1)
        mk.T(t)
        ha_mm[tb] = sT.inc()

    def vha(v):
        ins = None
        for tb in range(4):
            v.wait_ge(sT.h, ha_mm[tb])
            ins = v.scalar_tensor_tensor(
                mk.SC[0:16, tb * 512:(tb + 1) * 512],
                mk.PB1[0:16, tb * 512:(tb + 1) * 512], 1.0,
                mk.BCR[0:16, tb * 512:(tb + 1) * 512], ALU.mult, ALU.mult)
        ins.then_inc(sV.h, 1)
    mk.V(vha)
    ha_done = (sV, sV.inc())

    rrow = mk.bcast_from_dram(BNC_R, mk.BCF, 0, 2048, x2ready)

    def vinv(v):
        v.wait_ge(sX.h, rrow[1])
        ins = v.reciprocal(mk.BCF[0:16, 0:2048], mk.BCF[0:16, 0:2048])
        ins.then_inc(sV.h, 1)
    mk.V(vinv)
    inv_done = (sV, sV.inc())

    def vgp(v):
        v.wait_ge(mod_done[0].h, mod_done[1])
        v.wait_ge(ha_done[0].h, ha_done[1])
        v.wait_ge(inv_done[0].h, inv_done[1])
        v.tensor_tensor(mk.SC[0:16, 0:2048], mk.SC[0:16, 0:2048],
                        mk.BCF[0:16, 2048:4096], ALU.mult)
        v.memset(mk.ROWS[:, 12:13], 0.0)
        ins = v.scalar_tensor_tensor(mk.SB[0:16, 12288:14336],
                                     mk.SC[0:16, 0:2048], 0.1,
                                     mk.BCF[0:16, 0:2048], ALU.mult, ALU.mult)
        ins.then_inc(sV.h, 1)
    mk.V(vgp)
    gp_done = (sV, sV.inc())

    def gB(g):
        g.wait_ge(att[0].h, att[1])
        g.dma_start(out=mk.SB[0:16, 0:8192], in_=wb.ap()).then_inc(sW.h, 16)
    mk.G(gB)
    b_ld = sW.inc(16)

    if stage <= 5:
        return (mk.sO, mk.sO.n)
    import os
    NMCH = int(os.environ.get("FFN_NM", "64"))
    gl_mm, up_mm, pr_ev, pr_sp, wl_w, wl_u, g_ev = {}, {}, {}, {}, {}, {}, {}
    silu_d = {}
    wbw2 = mk.wbuf_tail
    for m in range(NMCH):
        par = m % 2

        def g(g, m=m, par=par):
            if m < 2:
                g.wait_ge(wbw2[0].h, wbw2[1])
                g.wait_ge(att[0].h, att[1])
            else:
                g.wait_ge(sT.h, gl_mm[m - 2])
            g.dma_start(
                out=bass.AP(mk.WBUF, par * 2048, [[4096, 128], [1, 2048]]),
                in_=bass.AP(win, m * 2048, [[131072, 128], [1, 2048]]),
            ).then_inc(sW.h, 16)
        mk.G(g)
        wl_w[m] = sW.inc(16)

        def g2(g, m=m):
            if m == 0:
                g.wait_ge(att[0].h, att[1])
            else:
                g.wait_ge(sT.h, up_mm[m - 1])
            g.dma_start(
                out=bass.AP(mk.SB, 14336, [[16384, 128], [1, 2048]]),
                in_=bass.AP(wup, m * 2048, [[131072, 128], [1, 2048]]),
            ).then_inc(sW.h, 16)
        mk.G(g2)
        wl_u[m] = sW.inc(16)

        def t(t, m=m, par=par):
            t.wait_ge(sW.h, wl_w[m])
            if m == 0:
                t.wait_ge(gp_done[0].h, gp_done[1])
                t.wait_ge(sW.h, b_ld)
            if m >= 1:
                t.wait_ge(sV.h, pr_ev[m - 1])
            PB = mk.PB0 if par == 0 else mk.PB1
            ins = None
            for k in range(NK):
                for tb in range(4):
                    ins = t.matmul(PB[:, tb * 512:(tb + 1) * 512],
                                   mk.WBUF[:, par * 2048 + k * 128:par * 2048 + (k + 1) * 128],
                                   mk.xbf(k, tb * 512, 512),
                                   start=(k == 0), stop=False)
            for tb in range(4):
                ins = t.matmul(PB[:, tb * 512:(tb + 1) * 512],
                               mk.SB[0:16, m * 128:(m + 1) * 128],
                               mk.SB[0:16, 12288 + tb * 512:12288 + (tb + 1) * 512],
                               start=False, stop=True)
            ins.then_inc(sT.h, 1)
        mk.T(t)
        gl_mm[m] = sT.inc()

        def v(v, m=m, par=par):
            v.wait_ge(sT.h, gl_mm[m])
            if m >= 1:
                v.wait_ge(sS.h, silu_d[m - 1])
            PB = mk.PB0 if par == 0 else mk.PB1
            ins = None
            for tb in range(4):
                ins = v.scalar_tensor_tensor(
                    mk.SC[:, tb * 512:(tb + 1) * 512],
                    PB[:, tb * 512:(tb + 1) * 512], 1.0,
                    mk.BCR[:, tb * 512:(tb + 1) * 512], ALU.mult, ALU.mult)
            ins.then_inc(sV.h, 1)
        mk.V(v)
        g_ev[m] = sV.inc()

        def s(s, m=m):
            s.wait_ge(sV.h, g_ev[m])
            if m >= 1:
                s.wait_ge(sV.h, pr_ev[m - 1])
            ins = s.activation(mk.SB[:, 8192:10240], mk.SC[:, 0:2048], AF.Silu)
            ins.then_inc(sS.h, 1)
        mk.Sc(s)
        silu_d[m] = sS.inc()

        def t2(t, m=m, par=par):
            t.wait_ge(sW.h, wl_u[m])
            if m == 0:
                t.wait_ge(ha_done[0].h, ha_done[1])
            if m >= 1:
                t.wait_ge(sV.h, g_ev[m - 1])
            PB = mk.PB1 if par == 0 else mk.PB0
            ins = None
            for k in range(NK):
                for tb in range(4):
                    ins = t.matmul(PB[:, tb * 512:(tb + 1) * 512],
                                   mk.SB[:, 14336 + k * 128:14336 + (k + 1) * 128],
                                   mk.xbf(k, tb * 512, 512),
                                   start=(k == 0), stop=(k == NK - 1))
            ins.then_inc(sT.h, 1)
        mk.T(t2)
        up_mm[m] = sT.inc()

        def v2(v, m=m, par=par):
            v.wait_ge(sT.h, up_mm[m])
            v.wait_ge(sS.h, silu_d[m])
            if m >= 1:
                v.wait_ge(sO.h, pr_sp[m - 1])
            PB = mk.PB1 if par == 0 else mk.PB0
            ins = None
            for tb in range(4):
                ins = v.tensor_tensor(
                    mk.SB[:, 10240 + tb * 512:10240 + (tb + 1) * 512],
                    mk.SB[:, 8192 + tb * 512:8192 + (tb + 1) * 512],
                    PB[:, tb * 512:(tb + 1) * 512], ALU.mult)
            ins.then_inc(sV.h, 1)
        mk.V(v2)
        pr_ev[m] = sV.inc()

        def y(y, m=m):
            y.wait_ge(sV.h, pr_ev[m])
            y.dma_start(
                out=bass.AP(mk.d_spill2, m * 128 * 2048, [[2048, 128], [1, 2048]]),
                in_=mk.SB[:, 10240:12288]).then_inc(sO.h, 16)
        mk.Y(y)
        pr_sp[m] = sO.inc(16)
    mk.wbuf_tail = (sT, gl_mm[NMCH - 1])

    if stage <= 6:
        return (mk.sO, mk.sO.n)
    r_r2 = mk.bcast_from_dram(BNC_R, mk.BCF, 0, 2048,
                              (sV, pr_ev[NMCH - 1]),
                              extra_wait=(sS, silu_d[NMCH - 1]))
    def vr1(v):
        v.wait_ge(sX.h, r_r2[1])
        ins = v.reciprocal(mk.BCF[0:1, 0:2048], mk.BCF[0:1, 0:2048])
        ins.then_inc(sV.h, 1)
    mk.V(vr1)
    invrow = (sV, sV.inc())
    r_rln = mk.bcast_from_dram(BNC_RLN, mk.BCF, 2048, 2048, invrow)
    r_mu2 = mk.bcast_from_dram(BNC_MU, mk.SC, 0, 2048, r_rln)
    def vrow(v):
        v.wait_ge(sX.h, r_mu2[1])
        v.wait_ge(sT.h, up_mm[NMCH - 1])
        v.tensor_tensor(mk.SC[0:1, 2048:3072][0:1, 0:1024], mk.SC[0:1, 0:1024],
                        mk.BCF[0:1, 2048:3072], ALU.mult)
        v.tensor_tensor(mk.SC[0:1, 1024:2048], mk.SC[0:1, 1024:2048],
                        mk.BCF[0:1, 3072:4096], ALU.mult)
        v.scalar_tensor_tensor(mk.SB[0:1, 14336:15360], mk.SC[0:1, 2048:3072][0:1, 0:1024],
                               -1.0, mk.BCF[0:1, 0:1024], ALU.mult, ALU.mult)
        ins = v.scalar_tensor_tensor(mk.SB[0:1, 15360:16384], mk.SC[0:1, 1024:2048],
                                     -1.0, mk.BCF[0:1, 1024:2048], ALU.mult, ALU.mult)
        v.memset(mk.ROWS[:, 12:13], 0.0)
        ins = v.tensor_scalar(mk.BCF[:, 2048:4096], mk.BCF[:, 2048:4096],
                              1.0, None, ALU.add)
        ins.then_inc(sV.h, 1)
    mk.V(vrow)
    rows_done = (sV, sV.inc())

    dst_dram = mk.d_xout if last else mk.d_xres
    dn_wl, dn_mm, dn_mmg, dn_ev, dn_xl, dn_st, dn_cast, blk_ld = {}, {}, {}, {}, {}, {}, {}, {}
    for tb in range(4):
        def yb(y, tb=tb):
            if tb == 0:
                y.wait_ge(sO.h, pr_sp[NMCH - 1])
                y.wait_ge(att[0].h, att[1])
            else:
                y.wait_ge(sT.h, dn_mm[(tb - 1, 15)])
            for i in range(4):
                y.dma_start(
                    out=bass.AP(mk.SA, i * 16 * 512, [[32768, 128], [512, 16], [1, 512]]),
                    in_=bass.AP(mk.d_spill2, tb * 512 + i * 16 * 128 * 2048,
                                [[2048, 128], [128 * 2048, 16], [1, 512]]),
                ).then_inc(sX.h, 16)
        mk.Y(yb)
        blk_ld[tb] = sX.inc(64)

        for m in range(16):
            par = m % 2

            def yx(y, tb=tb, m=m, par=par):
                if tb == 0 and m < 2:
                    y.wait_ge(rows_done[0].h, rows_done[1])
                if (tb, m) not in [(0, 0), (0, 1)]:
                    pm = (tb, m - 2) if m >= 2 else (tb - 1, 14 + m)
                    y.wait_ge(sO.h, dn_st[pm])
                    y.wait_ge(sG.h, dn_cast[pm])
                y.dma_start(
                    out=mk.SC[:, 1024 + par * 512:1024 + (par + 1) * 512],
                    in_=bass.AP(mk.d_xres, m * 128 * S + tb * 512,
                                [[S, 128], [1, 512]]),
                ).then_inc(sX.h, 16)
            mk.Y(yx)
            dn_xl[(tb, m)] = sX.inc(16)

            for kg in range(4):
                u = tb * 64 + m * 4 + kg
                upar = u % 2

                def g(g, tb=tb, m=m, kg=kg, u=u, upar=upar):
                    if u < 2:
                        g.wait_ge(sT.h, up_mm[NMCH - 1])
                    else:
                        g.wait_ge(sT.h, dn_mmg[u - 2])
                    g.dma_start(
                        out=bass.AP(mk.WBUF, upar * 2048, [[4096, 128], [1, 2048]]),
                        in_=bass.AP(wdn, (m * 4 + kg) * 2048,
                                    [[131072, 128], [1, 2048]]),
                    ).then_inc(sW.h, 16)
                mk.G(g)
                dn_wl[u] = sW.inc(16)

                def t(t, tb=tb, m=m, kg=kg, u=u, upar=upar, par=par):
                    t.wait_ge(sW.h, dn_wl[u])
                    if kg == 0:
                        t.wait_ge(sX.h, blk_ld[tb])
                        if tb == 0 and m == 0:
                            t.wait_ge(rows_done[0].h, rows_done[1])
                        if (tb, m) not in [(0, 0), (0, 1)]:
                            pm = (tb, m - 2) if m >= 2 else (tb - 1, 14 + m)
                            t.wait_ge(sV.h, dn_ev[pm])
                    pt = (mk.PB0 if par == 0 else mk.PB1)[:, 0:512]
                    ins = None
                    for k in range(16):
                        ins = t.matmul(pt,
                                       mk.WBUF[:, upar * 2048 + k * 128:upar * 2048 + (k + 1) * 128],
                                       mk.SA[:, (kg * 16 + k) * 512:(kg * 16 + k + 1) * 512],
                                       start=(kg == 0 and k == 0), stop=False)
                    if kg == 3:
                        ins = t.matmul(pt, mk.CB["ones_row_b"][:],
                                       mk.SB[0:1, 14336 + tb * 512:14336 + (tb + 1) * 512],
                                       start=False, stop=True)
                    ins.then_inc(sT.h, 1)
                mk.T(t)
                dn_mmg[u] = sT.inc()
            dn_mm[(tb, m)] = dn_mmg[tb * 64 + m * 4 + 3]

            def v(v, tb=tb, m=m, par=par):
                v.wait_ge(sT.h, dn_mm[(tb, m)])
                v.wait_ge(sX.h, dn_xl[(tb, m)])
                pt = (mk.PB0 if par == 0 else mk.PB1)[:, 0:512]
                xs = mk.SC[:, 1024 + par * 512:1024 + (par + 1) * 512]
                v.scalar_tensor_tensor(mk.SC[:, 0:512], pt, 1.0,
                                       mk.BCR[:, tb * 512:(tb + 1) * 512],
                                       ALU.mult, ALU.mult)
                v.scalar_tensor_tensor(xs, mk.BCF[:, 2048 + tb * 512:2048 + (tb + 1) * 512],
                                       1.0, xs, ALU.mult, ALU.mult)
                v.memset(mk.ROWS[:, 12:13], 0.0)
                ins = v.tensor_tensor(xs, xs, mk.SC[:, 0:512], ALU.add)
                ins.then_inc(sV.h, 1)
            mk.V(v)
            dn_ev[(tb, m)] = sV.inc()

            def gc(g, tb=tb, m=m, par=par):
                g.wait_ge(sV.h, dn_ev[(tb, m)])
                ins = g.tensor_copy(mk.xbf(m, tb * 512, 512),
                                    mk.SC[:, 1024 + par * 512:1024 + (par + 1) * 512])
                ins.then_inc(sG.h, 1)
            mk.G(gc)
            dn_cast[(tb, m)] = sG.inc()

            def y2(y, tb=tb, m=m, par=par):
                y.wait_ge(sV.h, dn_ev[(tb, m)])
                y.dma_start(
                    out=bass.AP(dst_dram, m * 128 * S + tb * 512,
                                [[S, 128], [1, 512]]),
                    in_=mk.SC[:, 1024 + par * 512:1024 + (par + 1) * 512],
                ).then_inc(sO.h, 16)
            mk.Y(y2)
            dn_st[(tb, m)] = sO.inc(16)
    mk.xbf_ready = [(sG, dn_cast[(3, 15)])]
    mk.wbuf_tail = (sT, dn_mm[(3, 15)])
    return (sO, dn_st[(3, 15)])


def emit_rope_kappa(mk, tiles, cos_off, sin_off, pmat, kappa, waits):
    sT, sV, sS = mk.sT, mk.sV, mk.sS
    last_v = None
    for (tt, off) in tiles:
        def t(t, tt=tt, off=off, last_v=last_v):
            if last_v is None:
                for sm, val in waits:
                    t.wait_ge(sm.h, val)
            else:
                t.wait_ge(sV.h, last_v)
            ins = None
            for b in range(4):
                ins = t.matmul(mk.PB0[:, b * 512:(b + 1) * 512],
                               mk.CB[pmat][:],
                               tt[:, off + b * 512:off + (b + 1) * 512],
                               start=True, stop=True)
            ins.then_inc(sT.h, 1)
        mk.T(t)
        mm_d = mk.sT.inc()

        def v(v, tt=tt, off=off, mm_d=mm_d):
            v.wait_ge(sT.h, mm_d)
            v.scalar_tensor_tensor(mk.WBUF[:, 0:2048], mk.PB0[:, 0:2048],
                                   1.0, mk.BCF[:, sin_off:sin_off + 2048],
                                   ALU.mult, ALU.mult)
            v.scalar_tensor_tensor(mk.WBUF[:, 2048:4096],
                                   tt[:, off:off + 2048], 1.0,
                                   mk.BCF[:, cos_off:cos_off + 2048],
                                   ALU.mult, ALU.mult)
            v.memset(mk.ROWS[:, 12:13], 0.0)
            ins = v.tensor_tensor(mk.SC[:, 0:2048], mk.WBUF[:, 0:2048],
                                  mk.WBUF[:, 2048:4096], ALU.add)
            if not kappa:
                v.memset(mk.ROWS[:, 13:14], 0.0)
                ins = v.tensor_copy(tt[:, off:off + 2048], mk.SC[:, 0:2048])
            ins.then_inc(sV.h, 1)
        mk.V(v)
        rp_d = mk.sV.inc()

        if kappa:
            def v2(v, rp_d=rp_d):
                v.wait_ge(sV.h, rp_d)
                ins = v.tensor_scalar(mk.WBUF[:, 0:2048], mk.SC[:, 0:2048],
                                      0.0, None, ALU.min)
                ins.then_inc(sV.h, 1)
            mk.V(v2)
            m_d = mk.sV.inc()

            def s(s, m_d=m_d):
                s.wait_ge(sV.h, m_d)
                ins = s.activation(mk.WBUF[:, 2048:4096], mk.WBUF[:, 0:2048],
                                   AF.Exp)
                ins.then_inc(sS.h, 1)
            mk.Sc(s)
            e_d = mk.sS.inc()

            def v3(v, tt=tt, off=off, e_d=e_d):
                v.wait_ge(sS.h, e_d)
                v.tensor_scalar(mk.SC[:, 0:2048], mk.SC[:, 0:2048], 0.0,
                                None, ALU.max)
                v.memset(mk.ROWS[:, 12:13], 0.0)
                ins = v.tensor_tensor(tt[:, off:off + 2048], mk.SC[:, 0:2048],
                                      mk.WBUF[:, 2048:4096], ALU.add)
                ins.then_inc(sV.h, 1)
            mk.V(v3)
            last_v = mk.sV.inc()
        else:
            last_v = rp_d
    return (sV, last_v)


def sub_l1attna(mk, W, last):
    wq = mk.add_weight("L1a_wq", W["wq_l"])
    wk = mk.add_weight("L1a_wk", W["wk_l"])
    wv = mk.add_weight("L1a_wv", W["wv_l"])
    wph = mk.add_weight("L1a_wphi", W["wphi_l"])
    wo = mk.add_weight("L1a_wo", W["wo_l"])
    cos_h = mk.add_weight("L1a_cos", W["cos64"])
    sin_h = mk.add_weight("L1a_sin", W["sin64"])
    sW, sX, sT, sV, sS, sG, sO = (mk.sW, mk.sX, mk.sT, mk.sV, mk.sS, mk.sG,
                                  mk.sO)
    ln = mk.emit_stats_center(rms2=False, r32=True)

    def gcs(g):
        g.wait_ge(sG.h, ln["centered"][1])
        g.dma_start(out=mk.BCF[:, 0:2048], in_=cos_h.ap()).then_inc(sW.h, 16)
        g.dma_start(out=mk.BCF[:, 2048:4096], in_=sin_h.ap()).then_inc(sW.h, 16)
    mk.G(gcs)
    cs_ld = sW.inc(32)

    def gwv(g):
        g.wait_ge(sT.h, mk.ln["stats_T"][1])
        g.dma_start(
            out=bass.AP(mk.SA, 0, [[32768, 128], [512, NK], [1, 512]]),
            in_=bass.AP(wv, 0, [[512, 128], [128 * 512, NK], [1, 512]]),
        ).then_inc(sW.h, 16)
    mk.G(gwv)
    wv_ld = sW.inc(16)
    vp = mk.emit_vprod(0, 512, 8192, extra_waits=[(sW, wv_ld)])

    kp = mk.emit_proj(wk, [0, 1, 2, 3],
                      lambda ci, tb: mk.SB[:, ci * 2048 + tb * 512:ci * 2048 + (tb + 1) * 512],
                      extra_waits=[vp], r32=True, wb_w=True)
    qp = mk.emit_proj(wq, list(range(16)),
                      lambda ci, tb: mk.SA[:, ci * 2048 + tb * 512:ci * 2048 + (tb + 1) * 512],
                      extra_waits=[vp], wb_w=True)

    kr = emit_rope_kappa(mk, [(mk.SB, j * 2048) for j in range(4)],
                         0, 2048, "rope_p64", True,
                         waits=[kp["ev"], qp["ev"], (sW, cs_ld)])
    qr = emit_rope_kappa(mk, [(mk.SA, j * 2048) for j in range(16)],
                         0, 2048, "rope_p64", True, waits=[kr])

    def vqg(v):
        v.wait_ge(qr[0].h, qr[1])
        for j in range(16):
            v.reduce_sum(mk.ROWS[:, 224 + j:225 + j],
                         mk.SA[:, j * 2048:(j + 1) * 2048], axis=AX.X)
        v.memset(mk.ROWS[:, 12:13], 0.0)
        ins = v.tensor_copy(mk.WBUF[:, 0:16], mk.ROWS[:, 224:240])
        ins.then_inc(sV.h, 1)
    mk.V(vqg)
    qg_d = sV.inc()

    def tsw(t):
        t.wait_ge(sV.h, qg_d)
        t.matmul(mk.PB1[:, 1536:1552], mk.CB["pswap"][:], mk.WBUF[:, 0:16],
                 start=True, stop=True).then_inc(sT.h, 1)
    mk.T(tsw)
    sw_mm = sT.inc()

    def vsw(v):
        v.wait_ge(sT.h, sw_mm)
        ins = v.tensor_copy(mk.WBUF[:, 16:32], mk.PB1[:, 1536:1552])
        ins.then_inc(sV.h, 1)
    mk.V(vsw)
    qg_d = sV.inc()

    lg_mm, lg_ev, lg_st = {}, {}, {}
    for n in range(32):
        j = n // 2
        pq = (n % 2) * 64
        kvt = n // 8
        pk = ((n // 4) % 2) * 64

        def t(t, n=n, j=j, pq=pq, kvt=kvt, pk=pk):
            if n == 0:
                t.wait_ge(sV.h, qg_d)
            if n >= 1:
                t.wait_ge(sV.h, lg_ev[n - 1])
            ins = None
            for b in range(4):
                qcol = j if pq == pk else 16 + j
                ins = t.matmul(mk.PB1[0:1, b * 512:(b + 1) * 512],
                               mk.WBUF[pk:pk + 64, qcol:qcol + 1],
                               mk.SB[pk:pk + 64, kvt * 2048 + b * 512:kvt * 2048 + (b + 1) * 512],
                               start=True, stop=True)
            ins.then_inc(sT.h, 1)
        mk.T(t)
        lg_mm[n] = sT.inc()

        def v(v, n=n):
            v.wait_ge(sT.h, lg_mm[n])
            if n >= 1:
                v.wait_ge(sX.h, lg_st[n - 1])
            ins = v.tensor_scalar(mk.BCF[0:1, 0:2048], mk.PB1[0:1, 0:2048],
                                  1.0 / 2048.0, None, ALU.mult)
            ins.then_inc(sV.h, 1)
        mk.V(v)
        lg_ev[n] = sV.inc()

        def y(y, n=n):
            y.wait_ge(sV.h, lg_ev[n])
            y.dma_start(out=bass.AP(mk.d_lgm, n * 2048, [[1, 1], [1, 2048]]),
                        in_=mk.BCF[0:1, 0:2048]).then_inc(sX.h, 16)
        mk.Y(y)
        lg_st[n] = sX.inc(16)

    def ylg(y):
        y.wait_ge(sX.h, lg_st[31])
        y.dma_start(out=mk.SC[0:32, 0:2048],
                    in_=bass.AP(mk.d_lgm, 0, [[2048, 32], [1, 2048]])
                    ).then_inc(sX.h, 16)
    mk.Y(ylg)
    lgm_ld = sX.inc(16)

    def vsm(v):
        v.wait_ge(sX.h, lgm_ld)
        ins = v.reduce_max(mk.SC[0:32, 2048:2049], mk.SC[0:32, 0:2048],
                           axis=AX.X, negate=True)
        ins.then_inc(sV.h, 1)
    mk.V(vsm)
    sm1 = sV.inc()

    def ssm(s):
        s.wait_ge(sV.h, sm1)
        ins = s.activation(mk.BCF[0:32, 0:2048], mk.SC[0:32, 0:2048], AF.Exp,
                           bias=mk.SC[0:32, 2048:2049],
                           accum_out=mk.SC[0:32, 2052:2053])
        ins.then_inc(sS.h, 1)
    mk.Sc(ssm)
    sm2 = sS.inc()

    def vsm2(v):
        v.wait_ge(sS.h, sm2)
        v.reciprocal(mk.SC[0:32, 2056:2057], mk.SC[0:32, 2052:2053])
        v.memset(mk.ROWS[:, 12:13], 0.0)
        v.tensor_scalar(mk.SC[0:32, 2060:2061], mk.SC[0:32, 2056:2057],
                        2048.0, None, ALU.mult)
        v.memset(mk.ROWS[:, 13:14], 0.0)
        inv_b = bass.AP(mk.SC, 2060, [[3072, 32], [0, 2048]])
        ins = v.tensor_tensor(mk.SC[0:32, 0:2048], mk.BCF[0:32, 0:2048],
                              inv_b, ALU.mult)
        ins.then_inc(sV.h, 1)
    mk.V(vsm2)
    al_d = sV.inc()

    ac_d = {}
    for sc in range(16):
        def t(t, sc=sc):
            t.wait_ge(sV.h, al_d)
            if sc >= 2:
                t.wait_ge(sV.h, ac_d[sc - 2])
            t.transpose(mk.PB1[:, (sc % 2) * 512:(sc % 2) * 512 + 32],
                        mk.SC[0:32, sc * 128:(sc + 1) * 128],
                        mk.CB["ident"][0:32, 0:32]).then_inc(sT.h, 1)
        mk.T(t)
        tr = sT.inc()

        def v(v, sc=sc, tr=tr):
            v.wait_ge(sT.h, tr)
            ins = v.tensor_copy(mk.BCF[:, sc * 32:(sc + 1) * 32],
                                mk.PB1[:, (sc % 2) * 512:(sc % 2) * 512 + 32])
            ins.then_inc(sV.h, 1)
        mk.V(v)
        ac_d[sc] = sV.inc()
    acol_done = (sV, ac_d[15])

    kkt_d = {}
    kkt_tr = {}
    for j in range(4):
        for c in range(16):
            idx = j * 16 + c

            def vc(v, j=j, c=c, idx=idx):
                if idx == 0:
                    v.wait_ge(acol_done[0].h, acol_done[1])
                if idx >= 2:
                    v.wait_ge(sT.h, kkt_tr[idx - 2])
                ins = v.tensor_copy(
                    mk.SC[:, 2048 + (idx % 2) * 128:2176 + (idx % 2) * 128],
                    mk.SB[:, j * 2048 + c * 128:j * 2048 + (c + 1) * 128])
                ins.then_inc(sV.h, 1)
            mk.V(vc)
            cp = sV.inc()

            def t(t, j=j, c=c, idx=idx, cp=cp):
                t.wait_ge(sV.h, cp)
                if idx >= 2:
                    t.wait_ge(sV.h, kkt_d[idx - 2])
                t.transpose(mk.PB0[:, (idx % 2) * 512:(idx % 2) * 512 + 128],
                            mk.SC[:, 2048 + (idx % 2) * 128:2176 + (idx % 2) * 128],
                            mk.CB["ident"][:]).then_inc(sT.h, 1)
            mk.T(t)
            kkt_tr[idx] = sT.inc()

            def v(v, j=j, c=c, idx=idx):
                v.wait_ge(sT.h, kkt_tr[idx])
                ins = v.tensor_copy(
                    mk.SB[:, j * 2048 + c * 128:j * 2048 + (c + 1) * 128],
                    mk.PB0[:, (idx % 2) * 512:(idx % 2) * 512 + 128])
                ins.then_inc(sV.h, 1)
            mk.V(v)
            kkt_d[idx] = sV.inc()
    kkt_done = (sV, kkt_d[63])

    res_done = {}
    sc_cnt = {}
    for n in range(32):
        j = n // 2
        kvh = n // 4
        kvt = kvh // 2
        kc = (kvh % 2) * 64
        vc = kvh * 64
        nbase = (n % 2) * 64
        mm_c = {}
        for c in range(16):
            def vsc(v, n=n, c=c, kvt=kvt, kc=kc):
                if n == 0 and c == 0:
                    v.wait_ge(kkt_done[0].h, kkt_done[1])
                if (n, c) > (0, 3):
                    pc = (n, c - 4) if c >= 4 else (n - 1, 12 + c)
                    v.wait_ge(sT.h, sc_cnt[pc])
                ins = v.scalar_tensor_tensor(
                    mk.WBUF[:, 2048 + (c % 4) * 64:2048 + (c % 4) * 64 + 64],
                    mk.SB[:, kvt * 2048 + c * 128 + kc:kvt * 2048 + c * 128 + kc + 64],
                    1.0,
                    bass.AP(mk.BCF, c * 32 + n, [[4096, 128], [0, 64]]),
                    ALU.mult, ALU.mult)
                ins.then_inc(sV.h, 1)
            mk.V(vsc)
            scv = sV.inc()

            def t(t, n=n, c=c, vc=vc, scv=scv, nbase=nbase):
                t.wait_ge(sV.h, scv)
                if c == 0 and n >= 2:
                    t.wait_ge(sV.h, res_done[n - 2])
                t.matmul(mk.PB1[nbase:nbase + 64, 1536 + (n % 2) * 64:1600 + (n % 2) * 64],
                         mk.WBUF[:, 2048 + (c % 4) * 64:2048 + (c % 4) * 64 + 64],
                         mk.SB[:, 8192 + c * 512 + vc:8192 + c * 512 + vc + 64],
                         start=(c == 0), stop=(c == 15),
                         tile_position=(0, nbase)).then_inc(sT.h, 1)
            mk.T(t)
            sc_cnt[(n, c)] = sT.inc()
        ou_mm = sc_cnt[(n, 15)]

        def vpo(v, n=n, ou_mm=ou_mm, nbase=nbase):
            v.wait_ge(sT.h, ou_mm)
            v.memset(mk.WBUF[:, (n % 2) * 64:(n % 2) * 64 + 64], 0.0)
            v.memset(mk.ROWS[:, 12:13], 0.0)
            ins = v.tensor_copy(
                mk.WBUF[nbase:nbase + 64, (n % 2) * 64:(n % 2) * 64 + 64],
                mk.PB1[nbase:nbase + 64, 1536 + (n % 2) * 64:1600 + (n % 2) * 64])
            ins.then_inc(sV.h, 1)
        mk.V(vpo)
        po_d = sV.inc()

        def t2(t, n=n, j=j, po_d=po_d, nbase=nbase):
            t.wait_ge(sV.h, po_d)
            if n >= 2:
                t.wait_ge(sV.h, res_done[n - 2])
            for b in range(4):
                ins = t.matmul(mk.PB0[nbase:nbase + 64, b * 512:(b + 1) * 512],
                               mk.WBUF[:, (n % 2) * 64:(n % 2) * 64 + 64],
                               mk.SA[:, j * 2048 + b * 512:j * 2048 + (b + 1) * 512],
                               start=True, stop=True,
                               tile_position=(0, nbase))
            ins.then_inc(sT.h, 1)
        mk.T(t2)
        rs_mm = sT.inc()

        if n % 2 == 0:
            res_done[n] = None
            rs_even = rs_mm
        else:
            def vrs(v, n=n, j=j, rs_mm=rs_mm, rs_even=rs_even):
                v.wait_ge(sT.h, rs_mm)
                ins = v.tensor_copy(mk.SA[:, j * 2048:(j + 1) * 2048],
                                    mk.PB0[:, 0:2048])
                ins.then_inc(sV.h, 1)
            mk.V(vrs)
            res_done[n] = sV.inc()
            res_done[n - 1] = res_done[n]
    res_all = (sV, res_done[31])

    ph_mm, ph_ev, ph_wl = {}, {}, {}
    wbw = mk.wbuf_tail
    for ci in range(16):
        par = ci % 2

        def g(g, ci=ci, par=par):
            if ci < 2:
                g.wait_ge(res_all[0].h, res_all[1])
            else:
                g.wait_ge(sT.h, ph_mm[ci - 2])
            g.dma_start(
                out=bass.AP(mk.WBUF, par * 2048, [[4096, 128], [1, 2048]]),
                in_=bass.AP(wph, ci * 2048, [[32768, 128], [1, 2048]]),
            ).then_inc(sW.h, 16)
        mk.G(g)
        ph_wl[ci] = sW.inc(16)

        def t(t, ci=ci, par=par):
            t.wait_ge(sW.h, ph_wl[ci])
            if ci == 0:
                t.wait_ge(res_all[0].h, res_all[1])
            if ci >= 2:
                t.wait_ge(sV.h, ph_ev[ci - 2])
            PB = mk.PB0 if par == 0 else mk.PB1
            ins = None
            for k in range(NK):
                for tb in range(4):
                    ins = t.matmul(PB[:, tb * 512:(tb + 1) * 512],
                                   mk.WBUF[:, par * 2048 + k * 128:par * 2048 + (k + 1) * 128],
                                   mk.xbf(k, tb * 512, 512),
                                   start=(k == 0), stop=(k == NK - 1))
            ins.then_inc(sT.h, 1)
        mk.T(t)
        ph_mm[ci] = sT.inc()

        def v(v, ci=ci, par=par):
            v.wait_ge(sT.h, ph_mm[ci])
            PB = mk.PB0 if par == 0 else mk.PB1
            for tb in range(4):
                v.scalar_tensor_tensor(
                    mk.SC[:, 2048 + (tb % 2) * 512:2048 + (tb % 2 + 1) * 512],
                    PB[:, tb * 512:(tb + 1) * 512], 1.0,
                    mk.BCR[:, tb * 512:(tb + 1) * 512], ALU.mult, ALU.mult)
                if tb >= 1:
                    v.tensor_tensor(
                        mk.SA[:, ci * 2048 + (tb - 1) * 512:ci * 2048 + tb * 512],
                        mk.SA[:, ci * 2048 + (tb - 1) * 512:ci * 2048 + tb * 512],
                        mk.SC[:, 2048 + ((tb - 1) % 2) * 512:2048 + ((tb - 1) % 2 + 1) * 512],
                        ALU.mult)
            v.memset(mk.ROWS[:, 12:13], 0.0)
            ins = v.tensor_tensor(
                mk.SA[:, ci * 2048 + 3 * 512:ci * 2048 + 4 * 512],
                mk.SA[:, ci * 2048 + 3 * 512:ci * 2048 + 4 * 512],
                mk.SC[:, 2048 + 512:2048 + 1024], ALU.mult)
            ins.then_inc(sV.h, 1)
        mk.V(v)
        ph_ev[ci] = sV.inc()
    mk.wbuf_tail = (sT, ph_mm[15])

    return mk.emit_out_proj_residual(wo, last, attn_wait=(sV, ph_ev[15]),
                                     wb_w=True)


def sub_l1ffn(mk, W, last):
    wa = mk.add_weight("L1f_A", W["A_l"])
    wb = mk.add_weight("L1f_Bw", W["Bw_l"])
    wg = mk.add_weight("L1f_Wg", W["Wg_l"])
    sW, sX, sT, sV, sS, sG, sO = (mk.sW, mk.sX, mk.sT, mk.sV, mk.sS, mk.sG,
                                  mk.sO)
    ln = mk.emit_stats_center(rms2=False)

    hp = mk.emit_proj(wa, [0, 1, 2, 3],
                      lambda ci, tb: mk.SB[:, ci * 2048 + tb * 512:ci * 2048 + (tb + 1) * 512],
                      wb_w=True)

    def gwg(g):
        g.dma_start(out=bass.AP(mk.SB, 10240, [[16384, 128], [8, NK], [1, 8]]),
                    in_=bass.AP(wg, 0, [[8, 128], [128 * 8, NK], [1, 8]]),
                    ).then_inc(sW.h, 16)
    mk.G(gwg)
    wg_ld = sW.inc(16)

    gt_mm = {}
    for tt in range(16):
        def t(t, tt=tt):
            t.wait_ge(sW.h, wg_ld)
            if tt == 0:
                t.wait_ge(sG.h, ln["centered"][1])
                t.wait_ge(hp["ev"][0].h, hp["ev"][1])
            ins = None
            for k in range(NK):
                ins = t.matmul(mk.PB1[:, 1536 + tt * 8:1544 + tt * 8],
                               mk.xbf(k, tt * 128, 128),
                               mk.SB[:, 10240 + k * 8:10240 + (k + 1) * 8],
                               start=(k == 0), stop=(k == NK - 1))
            ins.then_inc(sT.h, 1)
        mk.T(t)
        gt_mm[tt] = sT.inc()

    def vg1(v):
        v.wait_ge(sT.h, gt_mm[15])
        v.wait_ge(sV.h, ln["rcol"][1])
        rcol_b = bass.AP(mk.ROWS, 32, [[256, 128], [1, 16], [0, 8]])
        v.tensor_tensor(bass.AP(mk.SC, 0, [[3072, 128], [8, 16], [1, 8]]),
                        bass.AP(mk.PB1, 1536, [[2048, 128], [8, 16], [1, 8]]),
                        rcol_b, ALU.mult)
        v.memset(mk.ROWS[:, 12:13], 0.0)
        v.reduce_max(mk.SC[:, 128:144],
                     bass.AP(mk.SC, 0, [[3072, 128], [8, 16], [1, 8]]),
                     axis=AX.X, negate=True)
        v.memset(mk.ROWS[:, 13:14], 0.0)
        nm_b = bass.AP(mk.SC, 128, [[3072, 128], [1, 16], [0, 8]])
        ins = v.tensor_tensor(bass.AP(mk.SC, 256, [[3072, 128], [8, 16], [1, 8]]),
                              bass.AP(mk.SC, 0, [[3072, 128], [8, 16], [1, 8]]),
                              nm_b, ALU.add)
        ins.then_inc(sV.h, 1)
    mk.V(vg1)
    g1_d = sV.inc()

    def sg1(s):
        s.wait_ge(sV.h, g1_d)
        ins = s.activation(mk.SC[:, 384:512], mk.SC[:, 256:384], AF.Exp)
        ins.then_inc(sS.h, 1)
    mk.Sc(sg1)
    ge_d = sS.inc()

    def vg2(v):
        v.wait_ge(sS.h, ge_d)
        v.reduce_sum(mk.SC[:, 144:160],
                     bass.AP(mk.SC, 384, [[3072, 128], [8, 16], [1, 8]]),
                     axis=AX.X)
        v.tensor_scalar(mk.SC[:, 512:640], mk.SC[:, 256:384], 0.0,
                        None, ALU.is_ge)
        ins = v.reciprocal(mk.SC[:, 160:176], mk.SC[:, 144:160])
        ins.then_inc(sV.h, 1)
    mk.V(vg2)
    g2_d = sV.inc()

    gt_sel = {}
    for tt in range(16):
        def t2(t, tt=tt):
            t.wait_ge(sV.h, g2_d)
            if tt >= 1:
                t.wait_ge(sV.h, gt_sel[tt - 1])
            t.transpose(mk.PB0[0:8, 1024:1152], mk.SC[:, 512 + tt * 8:520 + tt * 8],
                        mk.CB["ident"][:]).then_inc(sT.h, 1)
        mk.T(t2)
        tr_d = sT.inc()

        def v3(v, tt=tt, tr_d=tr_d):
            v.wait_ge(sT.h, tr_d)
            ins = v.tensor_copy(mk.SB[0:8, 10496 + tt * 128:10496 + (tt + 1) * 128],
                                mk.PB0[0:8, 1024:1152])
            ins.then_inc(sV.h, 1)
        mk.V(v3)
        gt_sel[tt] = sV.inc()

    def t3(t):
        t.wait_ge(sV.h, gt_sel[15])
        t.transpose(mk.PB0[0:16, 1024:1152], mk.SC[:, 160:176],
                    mk.CB["ident"][:]).then_inc(sT.h, 1)
    mk.T(t3)
    pst = sT.inc()

    def v4(v):
        v.wait_ge(sT.h, pst)
        v.tensor_copy(mk.ROWS[0:16, 64:192], mk.PB0[0:16, 1024:1152]
                      ).then_inc(sV.h, 1)
    mk.V(v4)
    psev = sV.inc()
    wps = mk.row_to_dram(mk.ROWS, 64, BNC_AUX, (sV, psev), part16=True)
    ps_tile = mk.bcast_from_dram(BNC_AUX, mk.BCF, 0, 2048, wps)

    msk_mm, msk_d = {}, {}
    for c in range(4):
        for tb in range(4):
            def t4(t, c=c, tb=tb):
                if c == 0 and tb == 0:
                    t.wait_ge(hp["ev"][0].h, hp["ev"][1])
                t.wait_ge(sV.h, gt_sel[min(15, tb * 4 + 3)])
                idx = c * 4 + tb
                if idx >= 2:
                    pc = ((idx - 2) // 4, (idx - 2) % 4)
                    t.wait_ge(sV.h, msk_d[pc])
                t.matmul(mk.PB0[:, (tb % 2) * 512:(tb % 2) * 512 + 512],
                         mk.CB["epat"][:, c * 128:(c + 1) * 128],
                         mk.SB[0:8, 10496 + tb * 512:10496 + (tb + 1) * 512],
                         start=True, stop=True).then_inc(sT.h, 1)
            mk.T(t4)
            msk_mm[(c, tb)] = sT.inc()

            def v5(v, c=c, tb=tb):
                v.wait_ge(sT.h, msk_mm[(c, tb)])
                ins = v.tensor_tensor(
                    mk.SB[:, c * 2048 + tb * 512:c * 2048 + (tb + 1) * 512],
                    mk.SB[:, c * 2048 + tb * 512:c * 2048 + (tb + 1) * 512],
                    mk.PB0[:, (tb % 2) * 512:(tb % 2) * 512 + 512], ALU.mult)
                ins.then_inc(sV.h, 1)
            mk.V(v5)
            msk_d[(c, tb)] = sV.inc()

    st = mk.emit_out_proj_residual(
        wb, last, scale=4.0,
        src_chunk_fn=lambda k, tb: mk.SB[:, k * 2048 + tb * 512:k * 2048 + (tb + 1) * 512],
        n_in_chunks=4, attn_wait=(sV, msk_d[(3, 3)]),
        mult_tile=(mk.BCF, 0), wb_w=True)
    return st


SELCFG = True
SELCFG2 = True
NH_DBG = 16


def sub_l1attnb(mk, W, last, stage=99):
    wq = mk.add_weight("L1b_wq", W["wq_l"])
    wk = mk.add_weight("L1b_wk", W["wk_l"])
    wv = mk.add_weight("L1b_wv", W["wv_l"])
    wo = mk.add_weight("L1b_wo", W["wo_l"])
    cos_h = mk.add_weight("L1b_cos", W["cos128"])
    sin_h = mk.add_weight("L1b_sin", W["sin128"])
    onw_h = mk.add_weight("L1b_onw", W["onw"])
    sW, sX, sT, sV, sS, sG, sO = (mk.sW, mk.sX, mk.sT, mk.sV, mk.sS, mk.sG,
                                  mk.sO)
    ln = mk.emit_stats_center(rms2=False)
    VSP_off = 2048 * 2048

    def gcs(g):
        g.wait_ge(sG.h, ln["centered"][1])
        g.dma_start(out=mk.BCF[:, 0:2048], in_=cos_h.ap()).then_inc(sW.h, 16)
        g.dma_start(out=mk.BCF[:, 2048:4096], in_=sin_h.ap()).then_inc(sW.h, 16)
    mk.G(gcs)
    cs_ld = sW.inc(32)

    def yonw(y):
        y.wait_ge(sG.h, ln["centered"][1])
        y.dma_start(out=mk.LNC[:, 0:1], in_=onw_h.ap()).then_inc(sX.h, 16)
    mk.Y(yonw)
    onw_ld = sX.inc(16)

    def gwv(g):
        g.wait_ge(sT.h, mk.ln["stats_T"][1])
        for i in range(4):
            g.dma_start(
                out=bass.AP(mk.SA, i * 4 * 2048, [[32768, 128], [2048, 4], [1, 2048]]),
                in_=bass.AP(wv, i * 4 * 128 * 2048,
                            [[2048, 128], [128 * 2048, 4], [1, 2048]]),
            ).then_inc(sW.h, 16)
    mk.G(gwv)
    wv_ld = sW.inc(64)
    vp = mk.emit_vprod(0, 2048, 0, extra_waits=[(sW, wv_ld)],
                       spill=(mk.d_spill, VSP_off), w_tensor=mk.SA, stg_off=0)

    kp = mk.emit_proj(wk, list(range(16)),
                      lambda ci, tb: mk.SA[:, ci * 2048 + tb * 512:ci * 2048 + (tb + 1) * 512],
                      extra_waits=[vp], wb_w=True)
    kr = emit_rope_kappa(mk, [(mk.SA, j * 2048) for j in range(16)],
                         0, 2048, "rope_p128", False,
                         waits=[kp["ev"], (sW, cs_ld)])

    def vkm(v):
        v.wait_ge(kr[0].h, kr[1])
        for h in range(16):
            v.reduce_sum(mk.SC[:, 2048 + h * 8:2048 + h * 8 + 8],
                         bass.AP(mk.SA, h * 2048, [[32768, 128], [256, 8], [1, 256]]),
                         axis=AX.X)
        v.memset(mk.ROWS[:, 12:13], 0.0)
        ins = v.tensor_copy(mk.SB[:, 14336:14464], mk.SC[:, 2048:2176])
        ins.then_inc(sV.h, 1)
    mk.V(vkm)
    km_d = sV.inc()

    ksp_d = {}
    for h in range(16):
        def yk(y, h=h):
            if h == 0:
                y.wait_ge(kr[0].h, kr[1])
            y.dma_start(out=bass.AP(mk.d_spill, h * 128 * 2048,
                                    [[2048, 128], [1, 2048]]),
                        in_=mk.SA[:, h * 2048:(h + 1) * 2048]).then_inc(sO.h, 16)
        mk.Y(yk)
        ksp_d[h] = sO.inc(16)
    ksp = (sO, ksp_d[15])

    mk.wbuf_tail = kr
    qp = mk.emit_proj(wq, list(range(16)),
                      lambda ci, tb: mk.SA[:, ci * 2048 + tb * 512:ci * 2048 + (tb + 1) * 512],
                      scale=1.0 / math.sqrt(128.0),
                      extra_waits=[ksp, (sV, km_d)], wb_w=True)
    mk.dbg_qp = qp["ev"]
    qr = emit_rope_kappa(mk, [(mk.SA, j * 2048) for j in range(16)],
                         0, 2048, "rope_p128", False, waits=[qp["ev"]])
    mk.dbg_qr = qr

    def vfsm(v):
        v.wait_ge(sG.h, ln["centered"][1])
        v.memset(mk.SB[:, 14464:14592], 0.0)
        v.memset(mk.ROWS[:, 12:13], 0.0)
        ins = None
        for tc in range(16):
            bq = tc // 2
            if bq + 1 < 8:
                ins = v.memset(mk.SB[:, 14464 + tc * 8 + bq + 1:
                                     14464 + (tc + 1) * 8], NEG)
            ins = v.memset(mk.SB[:, 14464 + tc * 8 + bq:
                                 14464 + tc * 8 + bq + 1], -NEG)
        ins.then_inc(sV.h, 1)
    mk.V(vfsm)
    fsm_d = sV.inc()

    g_mm, g_ev = {}, {}
    for h in range(16):
        def tg(t, h=h):
            if h == 0:
                t.wait_ge(qr[0].h, qr[1])
            if h >= 2:
                t.wait_ge(sV.h, g_ev[h - 2])
            ins = None
            for tc in range(16):
                ins = t.matmul(
                    mk.PB0[:, (h % 2) * 512 + tc * 8:(h % 2) * 512 + tc * 8 + 8],
                    mk.SA[:, h * 2048 + tc * 128:h * 2048 + (tc + 1) * 128],
                    mk.SB[:, 14336 + h * 8:14336 + h * 8 + 8],
                    start=True, stop=True)
            ins.then_inc(sT.h, 1)
        mk.T(tg)
        g_mm[h] = sT.inc()

        def vg(v, h=h):
            v.wait_ge(sT.h, g_mm[h])
            if h == 0:
                v.wait_ge(sV.h, fsm_d)
            ins = v.scalar_tensor_tensor(
                bass.AP(mk.SC, h * 8, [[3072, 128], [128, 16], [1, 8]]),
                mk.PB0[:, (h % 2) * 512:(h % 2) * 512 + 128], 1.0,
                mk.SB[:, 14464:14592], ALU.mult, ALU.add)
            ins.then_inc(sV.h, 1)
        mk.V(vg)
        g_ev[h] = sV.inc()

    def vsel(v):
        v.wait_ge(sV.h, g_ev[15])
        v.tensor_copy(mk.BCF[:, 0:2048], mk.SC[:, 0:2048])
        v.memset(mk.ROWS[:, 12:13], 0.0)
        for it in range(4):
            v.reduce_max(mk.BCF[:, 2048 + it * 256:2048 + (it + 1) * 256],
                         bass.AP(mk.BCF, 0, [[4096, 128], [8, 256], [1, 8]]),
                         axis=AX.X)
            v.memset(mk.ROWS[:, 12:13], 0.0)
            if it < 3:
                v.tensor_tensor(mk.WBUF[:, 0:2048], mk.BCF[:, 0:2048],
                                bass.AP(mk.BCF, 2048 + it * 256,
                                        [[4096, 128], [1, 256], [0, 8]]),
                                ALU.is_equal)
                v.memset(mk.ROWS[:, 13:14], 0.0)
                v.scalar_tensor_tensor(mk.BCF[:, 0:2048], mk.WBUF[:, 0:2048],
                                       -1e32, mk.BCF[:, 0:2048],
                                       ALU.mult, ALU.add)
                v.memset(mk.ROWS[:, 12:13], 0.0)
        v.tensor_tensor(mk.BCF[:, 0:2048], mk.SC[:, 0:2048],
                        bass.AP(mk.BCF, 2048 + 3 * 256,
                                [[4096, 128], [1, 256], [0, 8]]),
                        ALU.is_ge)
        v.memset(mk.ROWS[:, 13:14], 0.0)
        v.tensor_scalar(mk.WBUF[:, 0:2048], mk.SC[:, 0:2048], -5e29,
                        None, ALU.is_gt)
        v.memset(mk.ROWS[:, 12:13], 0.0)
        v.tensor_tensor(mk.SC[:, 0:2048], mk.BCF[:, 0:2048],
                        mk.WBUF[:, 0:2048], ALU.mult)
        v.memset(mk.ROWS[:, 13:14], 0.0)
        ins = v.tensor_scalar(mk.SB[:, 10240:12288], mk.SC[:, 0:2048],
                              1e15, -1e15, ALU.mult, ALU.add)
        ins.then_inc(sV.h, 1)
    mk.V(vsel)
    sel_d = sV.inc()
    mk.dbg_sel = sel_d
    if stage <= 1:
        return mk.emit_out_proj_residual(wo, last, attn_wait=(sV, sel_d))

    qv_ld = {}

    def emit_kv_load(hh, waits):
        def y0(y, hh=hh, waits=waits):
            for sm, val in waits:
                y.wait_ge(sm.h, val)
            y.dma_start(
                out=mk.SB[:, (hh % 2) * 2048:(hh % 2) * 2048 + 2048],
                in_=bass.AP(mk.d_spill, hh * 128 * 2048,
                            [[2048, 128], [1, 2048]])).then_inc(sX.h, 16)
            y.dma_start(
                out=mk.SB[:, 4096 + (hh % 2) * 2048:4096 + (hh % 2) * 2048 + 2048],
                in_=bass.AP(mk.d_spill, VSP_off + hh * 128,
                            [[2048, 128], [128 * 2048, 16], [1, 128]])
            ).then_inc(sX.h, 16)
        mk.Y(y0)
        qv_ld[hh] = sX.inc(32)

    def emit_selt8(h, prev_tail):
        tr_d, ev_d = {}, {}
        for tc in range(16):
            def t5(t, h=h, tc=tc, prev_tail=prev_tail):
                if tc == 0:
                    t.wait_ge(sV.h, sel_d)
                    if prev_tail is not None:
                        t.wait_ge(prev_tail[0].h, prev_tail[1])
                if tc >= 2:
                    t.wait_ge(sV.h, ev_d[tc - 2])
                t.matmul(mk.PB0[0:8, 1024 + (tc % 2) * 512:1024 + (tc % 2) * 512 + 128],
                         mk.SC[:, tc * 128 + h * 8:tc * 128 + h * 8 + 8],
                         mk.CB["ident"][:],
                         start=True, stop=True).then_inc(sT.h, 1)
            mk.T(t5)
            tr_d[tc] = sT.inc()

            def v5(v, h=h, tc=tc):
                v.wait_ge(sT.h, tr_d[tc])
                if tc == 15:
                    v.wait_ge(sX.h, qv_ld[h])
                ins = v.tensor_copy(
                    mk.SB[0:8, 8192 + tc * 128:8192 + (tc + 1) * 128],
                    mk.PB0[0:8, 1024 + (tc % 2) * 512:1024 + (tc % 2) * 512 + 128])
                ins.then_inc(sV.h, 1)
            mk.V(v5)
            ev_d[tc] = sV.inc()
        return (sV, ev_d[15])

    def head_pre(h, prev_tail):
        if h == 0:
            emit_kv_load(0, [ksp, vp, (sV, sel_d)])
        if h + 1 < 16:
            emit_kv_load(h + 1, [prev_tail] if prev_tail is not None
                         else [ksp, vp, (sV, sel_d)])
        if SELCFG:
            return emit_selt8(h, prev_tail)
        return (sX, qv_ld[h])

    att = mk.emit_attn_core(
        nheads=NH_DBG,
        kt_tensor=mk.SB, kt_off_fn=lambda h: (h % 2) * 2048,
        v_lhsT_fn=lambda h, kt: mk.SB[:, 4096 + (h % 2) * 2048 + kt * 128:
                                      4096 + (h % 2) * 2048 + (kt + 1) * 128],
        qkv_wait=(sV, sel_d), head_pre=head_pre, sel_cfg=(SELCFG and SELCFG2))
    if stage <= 2:
        return mk.emit_out_proj_residual(wo, last, attn_wait=att)

    on_d = {}
    for h in range(16):
        def s5(s, h=h):
            if h == 0:
                s.wait_ge(att[0].h, att[1])
            else:
                s.wait_ge(sV.h, on_d[h - 1])
            ins = s.activation(mk.WBUF[:, (h % 2) * 2048:(h % 2) * 2048 + 2048],
                               mk.SA[:, h * 2048:(h + 1) * 2048], AF.Square)
            ins.then_inc(sS.h, 1)
        mk.Sc(s5)
        sq_d = sS.inc()

        def t6(t, h=h, sq_d=sq_d):
            t.wait_ge(sS.h, sq_d)
            ins = None
            for b in range(4):
                ins = t.matmul(mk.PB1[0:1, b * 512:(b + 1) * 512],
                               mk.CB["ones_col"][:],
                               mk.WBUF[:, (h % 2) * 2048 + b * 512:
                                       (h % 2) * 2048 + (b + 1) * 512],
                               start=True, stop=True)
            ins.then_inc(sT.h, 1)
        mk.T(t6)
        mmo = sT.inc()

        def v7(v, h=h, mmo=mmo):
            v.wait_ge(sT.h, mmo)
            ins = v.tensor_scalar(mk.SC[0:1, 0:2048], mk.PB1[0:1, 0:2048],
                                  1.0 / 128.0, None, ALU.mult)
            ins.then_inc(sV.h, 1)
        mk.V(v7)
        mean_d = sV.inc()

        def s6(s, h=h, mean_d=mean_d):
            s.wait_ge(sV.h, mean_d)
            s.activation(mk.BCF[0:1, (h % 2) * 2048:(h % 2) * 2048 + 2048],
                         mk.SC[0:1, 0:2048], AF.Ln,
                         bias=mk.CB["eps6"][0:1, 0:1])
            ins = s.activation(mk.SC[0:1, 0:2048],
                               mk.BCF[0:1, (h % 2) * 2048:(h % 2) * 2048 + 2048],
                               AF.Exp, scale=-0.5)
            ins.then_inc(sS.h, 1)
        mk.Sc(s6)
        rso_d = sS.inc()

        w_on = mk.row_to_dram(mk.SC, 0, 6 + h % 2, (sS, rso_d))
        on_tile = mk.bcast_from_dram(6 + h % 2, mk.BCF, (h % 2) * 2048, 2048,
                                     w_on)

        def v8(v, h=h, on_tile=on_tile):
            v.wait_ge(sX.h, on_tile[1])
            v.wait_ge(sX.h, onw_ld)
            v.tensor_scalar(mk.BCF[:, (h % 2) * 2048:(h % 2) * 2048 + 2048],
                            mk.BCF[:, (h % 2) * 2048:(h % 2) * 2048 + 2048],
                            mk.LNC[:, 0:1], None, ALU.mult)
            v.memset(mk.ROWS[:, 12:13], 0.0)
            ins = v.scalar_tensor_tensor(
                mk.SA[:, h * 2048:(h + 1) * 2048],
                mk.SA[:, h * 2048:(h + 1) * 2048], 1.0,
                mk.BCF[:, (h % 2) * 2048:(h % 2) * 2048 + 2048],
                ALU.mult, ALU.mult)
            ins.then_inc(sV.h, 1)
        mk.V(v8)
        on_d[h] = sV.inc()

    return mk.emit_out_proj_residual(wo, last, attn_wait=(sV, on_d[15]),
                                     wb_w=True)


def build_ffn_only(weights_np):
    mk = MK()
    sub_ffn(mk, weights_np, last=True)
    mk.finish()
    return mk


def build_attn_b_only(weights_np):
    mk = MK()
    sub_attn_b(mk, weights_np, last=True)
    mk.finish()
    return mk


def _wb(w, n_in=16):
    w = np.asarray(w, dtype=np.float32)
    r, c = w.shape
    assert r == n_in * 128
    return np.ascontiguousarray(
        w.reshape(n_in, 128, c // 128, 128).transpose(1, 2, 0, 3)
         .reshape(128, (c // 128) * n_in * 128))


def kernel(**inputs):
    T = lambda a: np.ascontiguousarray(np.asarray(a, dtype=np.float32).T)
    A_ = lambda a: np.ascontiguousarray(np.asarray(a, dtype=np.float32))
    x = np.asarray(inputs["L0_attn_a_x"], dtype=np.float32)

    Wa = dict(wqkv_l=T(inputs["L0_attn_a_w_qkv"]),
              wqkv_wb=_wb(T(inputs["L0_attn_a_w_qkv"])),
              wout_wb=_wb(T(inputs["L0_attn_a_w_out"])))
    Wb = dict(w_q_wb=_wb(A_(inputs["L0_attn_b_w_q"])),
              w_kv=A_(inputs["L0_attn_b_w_kv"]),
              w_kv_wb=_wb(A_(inputs["L0_attn_b_w_kv"])),
              w_dense_wb=_wb(A_(inputs["L0_attn_b_w_dense"])))
    ln1 = np.asarray(inputs["L0_ffn_ln1_w"], dtype=np.float32)
    ln2 = np.asarray(inputs["L0_ffn_ln2_w"], dtype=np.float32)
    wdn_t = T(inputs["L0_ffn_W_down"])
    wdn3 = np.ascontiguousarray(
        wdn_t.reshape(4, 16, 128, 16, 128).transpose(2, 3, 0, 1, 4)
             .reshape(128, 131072))
    Wf = dict(wq_l=_wb(T(inputs["L0_ffn_Wq"])), wk_l=_wb(T(inputs["L0_ffn_Wk"])),
              wv_l=T(inputs["L0_ffn_Wv"]), wo_raw=_wb(A_(inputs["L0_ffn_Wo"])),
              A=A_(inputs["L0_ffn_A"]), B=A_(inputs["L0_ffn_B"]),
              win_l=_wb(T(inputs["L0_ffn_W_in_base"])),
              wup_l=_wb(T(inputs["L0_ffn_W_up_base"])),
              wdn_l=wdn3,
              ln1c=np.ascontiguousarray(ln1.reshape(16, 128).T),
              ln2c=np.ascontiguousarray(ln2.reshape(16, 128).T))
    s = np.arange(2048)
    inv64 = 1.0 / 10000.0 ** (np.arange(0, 64, 2) / 64.0)
    fr64 = s[None, :] * inv64[(np.arange(128) % 32)][:, None]
    W1a = dict(wq_l=_wb(A_(inputs["L1_attn_a_Wq"])),
               wk_l=_wb(A_(inputs["L1_attn_a_Wk"])),
               wv_l=A_(inputs["L1_attn_a_Wv"]),
               wphi_l=_wb(A_(inputs["L1_attn_a_Wphi"])),
               wo_l=_wb(A_(inputs["L1_attn_a_Wo"])),
               cos64=np.cos(fr64).astype(np.float32),
               sin64=np.sin(fr64).astype(np.float32))
    W1f = dict(
        A_l=_wb(np.asarray(inputs["L1_ffn_A"], dtype=np.float32
                           ).transpose(2, 0, 1).reshape(2048, 512)),
        Bw_l=_wb(np.asarray(inputs["L1_ffn_Bw"], dtype=np.float32
                            ).transpose(0, 2, 1).reshape(512, 2048), n_in=4),
        Wg_l=T(inputs["L1_ffn_Wg"]))
    inv128 = 1.0 / 10000.0 ** (np.arange(0, 128, 2) / 128.0)
    fr128 = s[None, :] * inv128[(np.arange(128) % 64)][:, None]
    onw = np.asarray(inputs["L1_attn_b_o_norm_w"], dtype=np.float32).reshape(128, 1)
    W1b = dict(wq_l=_wb(T(inputs["L1_attn_b_Wq"])),
               wk_l=_wb(T(inputs["L1_attn_b_Wk"])),
               wv_l=T(inputs["L1_attn_b_Wv"]),
               wo_l=_wb(T(inputs["L1_attn_b_Wo"])),
               cos128=np.cos(fr128).astype(np.float32),
               sin128=np.sin(fr128).astype(np.float32),
               onw=np.ascontiguousarray(onw))

    mk = MK()
    sub_attn_a(mk, Wa, last=False)
    sub_ffn(mk, Wf, last=False)
    sub_attn_b(mk, Wb, last=False)
    sub_ffn(mk, Wf, last=False)
    sub_l1attna(mk, W1a, last=False)
    sub_l1ffn(mk, W1f, last=False)
    sub_l1attnb(mk, W1b, last=False)
    sub_l1ffn(mk, W1f, last=True)
    mk.finish()

    in_maps = []
    for b in range(8):
        ins = dict(mk.inputs)
        ins["x_in"] = np.ascontiguousarray(x[b].T)
        in_maps.append(ins)
    res = run_bass_kernel_spmd(mk.nc, in_maps, core_ids=list(range(8)))
    out = np.stack([np.ascontiguousarray(res.results[b]["x_out"].T)
                    for b in range(8)], axis=0)
    return out.astype(np.float32)


# version 06 (final)
# speedup vs baseline: 1.0554x; 1.0554x over previous
import math
from contextlib import ExitStack

import numpy as np
import ml_dtypes
import concourse.bass as bass
import concourse.mybir as mybir
from concourse.bass_utils import run_bass_kernel_spmd

F32 = mybir.dt.float32
BF16 = mybir.dt.bfloat16
AF = mybir.ActivationFunctionType
ALU = mybir.AluOpType
AX = mybir.AxisListType
BF = ml_dtypes.bfloat16

D = 2048
S = 2048
NK = D // 128
NEG = -1e30


def host_constants():
    c = {}
    i = np.arange(128)
    c["ident"] = np.eye(128, dtype=np.float32)
    c["ones_col"] = np.ones((128, 1), dtype=np.float32)
    c["ones_row_b"] = np.ones((1, 128), dtype=np.float32)
    c["mask_qdiag"] = np.where(i[:, None] >= i[None, :], 0.0, NEG).astype(np.float32)
    c["mask_qlow"] = np.where(i[None, :] >= i[:, None], 0.0, NEG).astype(np.float32)
    j = np.arange(128)[:, None]
    u = np.arange(-384, 512)[None, :]
    c["bin_kdiag_m"] = (u >= j).astype(np.float32)
    c["bin_klow_m"] = (u <= j).astype(np.float32)
    p64 = np.zeros((128, 128), dtype=np.float32)
    for r in range(128):
        b = (r // 64) * 64
        o = r - b
        if o < 32:
            p64[r, b + o + 32] = -1.0
        else:
            p64[r, b + o - 32] = 1.0
    c["rope_p64"] = np.ascontiguousarray(p64.T)
    psw = np.zeros((128, 128), dtype=np.float32)
    for r in range(128):
        psw[r, r ^ 64] = 1.0
    c["pswap"] = psw
    p128 = np.zeros((128, 128), dtype=np.float32)
    for r in range(128):
        if r < 64:
            p128[r, r + 64] = -1.0
        else:
            p128[r, r - 64] = 1.0
    c["rope_p128"] = np.ascontiguousarray(p128.T)
    ep = np.zeros((8, 512), dtype=np.float32)
    for e in range(8):
        ep[e, e * 64:(e + 1) * 64] = 1.0
    c["epat"] = ep
    c["eps5"] = np.full((128, 1), 1e-5, dtype=np.float32)
    c["eps6"] = np.full((128, 1), 1e-6, dtype=np.float32)
    return c


CONST_DT = {"ident": F32, "ones_col": BF16, "ones_row_b": BF16,
            "mask_qdiag": BF16, "mask_qlow": BF16,
            "eps5": F32, "eps6": F32,
            "bin_kdiag_m": BF16, "bin_klow_m": BF16, "epat": BF16,
            "rope_p64": BF16, "rope_p128": BF16, "pswap": BF16}

BNC_MU, BNC_R, BNC_RLN, BNC_AUX, BNC_C0, BNC_C1 = 0, 1, 2, 3, 4, 5
BNC_N = 8


class Sem:
    def __init__(self, h):
        self.h = h
        self.n = 0

    def inc(self, k=1):
        self.n += k
        return self.n


class MK:
    def __init__(self):
        self.nc = nc = bass.Bass("TRN2", target_bir_lowering=False)
        self.es = ExitStack()
        self.inputs = {}
        self.consts = host_constants()

        self.d_xin = nc.dram_tensor("x_in", [D, S], F32, kind="ExternalInput")
        self.d_xout = nc.dram_tensor("x_out", [D, S], F32, kind="ExternalOutput")
        self.d_xres = nc.dram_tensor("x_res", [D, S], F32)
        self.d_bnc = nc.dram_tensor("bnc", [BNC_N, 2048], F32)
        self.d_spill = nc.dram_tensor("spill", [2 * 2048 * 2048], BF16)
        self.d_spill2 = nc.dram_tensor("spill2", [8192 * 2048], BF16)
        self.d_lgm = nc.dram_tensor("lgm", [32 * 2048], F32)
        self.weight_tensors = {}

        self.c_handles = {}
        for k, v in self.consts.items():
            self.c_handles[k] = nc.dram_tensor("c_" + k, list(v.shape), F32,
                                               kind="ExternalInput")
            self.inputs["c_" + k] = np.ascontiguousarray(v)

        sb = lambda n, sh, dt: self.es.enter_context(nc.sbuf_tensor(n, sh, dt))
        ps = lambda n, sh: self.es.enter_context(nc.psum_tensor(n, sh, F32))
        self.XBF = sb("XBF", [128, NK * S], BF16)
        self.SA = sb("SA", [128, 32768], BF16)
        self.SB = sb("SBX", [128, 16384], BF16)
        self.SC = sb("SC", [128, 3072], F32)
        self.WBUF = sb("WBUF", [128, 4096], BF16)
        self.BCR = sb("BCR", [128, 2048], BF16)
        self.BCF = sb("BCF", [128, 4096], F32)
        self.ROWS = sb("ROWS", [128, 256], F32)
        self.LNC = sb("LNC", [128, 32], F32)
        self.CB = {}
        for k, v in self.consts.items():
            self.CB[k] = sb("cb_" + k, list(v.shape), CONST_DT[k])

        self.PB0 = ps("PB0", [128, 2048])
        self.PB1 = ps("PB1", [128, 2048])

        mksem = lambda n: Sem(self.es.enter_context(nc.semaphore(n)))
        self.sW = mksem("sW")
        self.sX = mksem("sX")
        self.sT = mksem("sT")
        self.sV = mksem("sV")
        self.sS = mksem("sS")
        self.sG = mksem("sG")
        self.sO = mksem("sO")

        self.block = self.es.enter_context(nc.Block())
        self.wbuf_tail = (self.sT, 0)
        self._prologue()

    def T(self, fn):
        self.block.tensor(fn)

    def V(self, fn):
        self.block.vector(fn)

    def Sc(self, fn):
        self.block.scalar(fn)

    def G(self, fn):
        self.block.gpsimd(fn)

    def Y(self, fn):
        self.block.sync(fn)

    def xbf(self, k, c0=0, w=S):
        return self.XBF[:, k * S + c0:k * S + c0 + w]

    def add_weight(self, name, arr):
        if name not in self.weight_tensors:
            h = self.nc.dram_tensor(name, list(arr.shape), F32,
                                    kind="ExternalInput")
            self.weight_tensors[name] = h
            self.inputs[name] = np.ascontiguousarray(arr, dtype=np.float32)
        return self.weight_tensors[name]

    def _prologue(self):
        mk, sW, sX = self, self.sW, self.sX

        def g(g):
            for k, h in mk.c_handles.items():
                g.dma_start(out=mk.CB[k][:], in_=h.ap()).then_inc(sW.h, 16)
            for i in range(4):
                g.dma_start(
                    out=bass.AP(mk.XBF, i * 4 * S, [[NK * S, 128], [S, 4], [1, S]]),
                    in_=bass.AP(mk.d_xin, i * 4 * 128 * S,
                                [[S, 128], [128 * S, 4], [1, S]]),
                ).then_inc(sW.h, 16)
        self.G(g)
        sW.inc(16 * len(mk.c_handles) + 64)

        def y(y):
            for i in range(4):
                y.dma_start(
                    out=bass.AP(mk.d_xres, i * 4 * 128 * S,
                                [[S, 128], [128 * S, 4], [1, S]]),
                    in_=bass.AP(mk.d_xin, i * 4 * 128 * S,
                                [[S, 128], [128 * S, 4], [1, S]]),
                ).then_inc(sX.h, 16)
        self.Y(y)
        sX.inc(64)
        self.xbf_ready = [(sW, sW.n), (sX, sX.n)]
        self.consts_ready = sW.n

    def row_to_dram(self, src_t, src_off, slot, wait, width=2048, part16=False):
        mk, sX = self, self.sX

        def y(y):
            y.wait_ge(wait[0].h, wait[1])
            if part16:
                y.dma_start(out=bass.AP(mk.d_bnc, slot * 2048, [[128, 16], [1, 128]]),
                            in_=src_t[0:16, src_off:src_off + 128]).then_inc(sX.h, 16)
            else:
                y.dma_start(out=bass.AP(mk.d_bnc, slot * 2048, [[1, 1], [1, width]]),
                            in_=src_t[0:1, src_off:src_off + width]).then_inc(sX.h, 16)
        self.Y(y)
        return (sX, sX.inc(16))

    def bcast_from_dram(self, slot, dst_t, dst_off, width, wait, use_g=False,
                        extra_wait=None):
        mk = self
        sem = self.sW if use_g else self.sX

        def f(e):
            e.wait_ge(wait[0].h, wait[1])
            if extra_wait is not None:
                e.wait_ge(extra_wait[0].h, extra_wait[1])
            e.dma_start(out=dst_t[:, dst_off:dst_off + width],
                        in_=bass.AP(mk.d_bnc, slot * 2048, [[0, 128], [1, width]])
                        ).then_inc(sem.h, 16)
        (self.G if use_g else self.Y)(f)
        return (sem, sem.inc(16))

    def emit_stats_center(self, rms2=False, r32=False):
        mk = self
        sT, sV, sS, sG, sX = self.sT, self.sV, self.sS, self.sG, self.sX
        BCF, SC = self.BCF, self.SC

        def t0(t):
            for sm, val in self.xbf_ready:
                t.wait_ge(sm.h, val)
            t.wait_ge(mk.sW.h, self.consts_ready)
            ins = None
            for k in range(NK):
                for b in range(4):
                    ins = t.matmul(mk.PB0[0:1, b * 512:(b + 1) * 512],
                                   mk.CB["ones_col"][:], mk.xbf(k, b * 512, 512),
                                   start=(k == 0), stop=(k == NK - 1))
            ins.then_inc(sT.h, 1)
        self.T(t0)
        sum_done = sT.inc()

        sq_s = {}
        sq_t = {}
        for kp in range(1, NK, 2):
            def s(s, kp=kp):
                if kp < 2:
                    for sm, val in self.xbf_ready:
                        s.wait_ge(sm.h, val)
                else:
                    s.wait_ge(sT.h, sq_t[kp - 2])
                s.activation(mk.SA[:, 0:S], mk.xbf(kp - 1), AF.Square)
                s.activation(mk.SA[:, 2048:2048 + S], mk.xbf(kp),
                             AF.Square).then_inc(sS.h, 1)
            self.Sc(s)
            sq_s[kp - 1] = sq_s[kp] = sS.inc()

            for k in (kp - 1, kp):
                buf = (k % 2) * 2048

                def t(t, k=k, buf=buf):
                    t.wait_ge(sS.h, sq_s[k])
                    ins = None
                    for b in range(4):
                        ins = t.matmul(mk.PB1[0:1, b * 512:(b + 1) * 512],
                                       mk.CB["ones_col"][:],
                                       mk.SA[:, buf + b * 512:buf + (b + 1) * 512],
                                       start=(k == 0), stop=(k == NK - 1))
                    ins.then_inc(sT.h, 1)
                self.T(t)
                sq_t[k] = sT.inc()
        mm_done = sq_t[NK - 1]

        def v(v):
            v.wait_ge(sT.h, mm_done)
            ins = v.tensor_scalar(BCF[0:1, 0:2048], mk.PB0[0:1, 0:2048],
                                  1.0 / D, None, ALU.mult)
            ins.then_inc(sV.h, 1)
        self.V(v)
        mu_done = sV.inc()
        w_mu = self.row_to_dram(BCF, 0, BNC_MU, (sV, mu_done))

        def v2(v):
            v.wait_ge(sX.h, w_mu[1])
            v.tensor_tensor(SC[0:1, 0:2048], BCF[0:1, 0:2048],
                            BCF[0:1, 0:2048], ALU.mult)
            v.memset(mk.ROWS[:, 12:13], 0.0)
            ins = v.scalar_tensor_tensor(BCF[0:1, 0:2048], mk.PB1[0:1, 0:2048],
                                         1.0 / D, SC[0:1, 0:2048],
                                         ALU.mult, ALU.subtract)
            ins.then_inc(sV.h, 1)
        self.V(v2)
        var_done = sV.inc()

        def s2(s, rms2=rms2):
            s.wait_ge(sV.h, var_done)
            s.activation(SC[0:1, 0:2048], BCF[0:1, 0:2048], AF.Ln, bias=mk.CB["eps5"][0:1, 0:1])
            ins = s.activation(BCF[0:1, 2048:4096], SC[0:1, 0:2048],
                               AF.Exp, scale=-0.5)
            if rms2:
                ins = s.activation(BCF[0:1, 0:2048], SC[0:1, 0:2048],
                                   AF.Exp, scale=-1.0)
            ins.then_inc(sS.h, 1)
        self.Sc(s2)
        rln_done = sS.inc()

        if not rms2:
            r_final = (sS, rln_done)
        else:
            w_rln = self.row_to_dram(BCF, 2048, BNC_RLN, (sS, rln_done))
            e2_done = rln_done

            def v3(v):
                v.wait_ge(sS.h, e2_done)
                ins = v.tensor_scalar(BCF[0:1, 0:2048], BCF[0:1, 0:2048],
                                      -1e-5, 1.0, ALU.mult, ALU.add)
                ins.then_inc(sV.h, 1)
            self.V(v3)
            u_done = sV.inc()

            def s4(s):
                s.wait_ge(sV.h, u_done)
                s.activation(SC[0:1, 0:2048], BCF[0:1, 0:2048], AF.Ln, bias=mk.CB["eps6"][0:1, 0:1])
                ins = s.activation(BCF[0:1, 0:2048], SC[0:1, 0:2048],
                                   AF.Exp, scale=-0.5)
                ins.then_inc(sS.h, 1)
            self.Sc(s4)
            rms_done = sS.inc()

            def v4(v):
                v.wait_ge(sS.h, rms_done)
                v.wait_ge(sX.h, w_rln[1])
                ins = v.tensor_tensor(BCF[0:1, 2048:4096], BCF[0:1, 2048:4096],
                                      BCF[0:1, 0:2048], ALU.mult)
                ins.then_inc(sV.h, 1)
            self.V(v4)
            r_final = (sV, sV.inc())

        w_r = self.row_to_dram(BCF, 2048, BNC_R, r_final)
        r_tile = self.bcast_from_dram(BNC_R, self.BCR, 0, 2048, w_r, use_g=True)
        mu_tile = self.bcast_from_dram(BNC_MU, self.BCF, 0, 2048, w_r)
        r32_tile = None
        if r32:
            r32_tile = self.bcast_from_dram(BNC_R, self.SC, 0, 2048, w_r)

        rc_last = None
        for jj in range(16):
            def t2(t, jj=jj, rc_last=rc_last):
                if jj == 0:
                    t.wait_ge(r_final[0].h, r_final[1])
                else:
                    t.wait_ge(sV.h, rc_last)
                t.transpose(mk.PB1[:, 1536:1537],
                            BCF[0:1, 2048 + jj * 128:2048 + (jj + 1) * 128],
                            mk.CB["ident"][0:1, 0:1]).then_inc(sT.h, 1)
            self.T(t2)
            td = sT.inc()

            def v5(v, jj=jj, td=td):
                v.wait_ge(sT.h, td)
                v.tensor_copy(mk.ROWS[:, 32 + jj:33 + jj],
                              mk.PB1[:, 1536:1537]).then_inc(sV.h, 1)
            self.V(v5)
            rc_last = sV.inc()
        rcol = (sV, rc_last)

        def gch(g):
            g.wait_ge(sX.h, mu_tile[1])
            ins = None
            for k in range(NK):
                ins = g.tensor_tensor(mk.xbf(k), mk.xbf(k),
                                      BCF[:, 0:2048], ALU.subtract)
            ins.then_inc(self.sG.h, 1)
        self.G(gch)
        centered = (self.sG, self.sG.inc())

        self.ln = dict(r_tile=r_tile, rcol=rcol, centered=centered,
                       stats_T=(sT, mm_done), r32_tile=r32_tile)
        return self.ln

    def emit_proj(self, wh, cols, dest_fn=None, scale=1.0, use_rtile=True,
                  extra_waits=(), n_in_chunks=NK, src_fn=None,
                  spill=None, stg_off=4096, r32=False, wb_w=False):
        mk = self
        sW, sT, sV, sO = self.sW, self.sT, self.sV, self.sO
        sp_d = {}
        if spill is not None:
            dest_fn = lambda ci, tb: mk.SB[:, stg_off + (ci % 2) * 2048 + tb * 512:
                                           stg_off + (ci % 2) * 2048 + (tb + 1) * 512]
        if src_fn is None:
            src_fn = lambda k, tb: mk.xbf(k, tb * 512, 512)
        C = wh.shape[1]
        mm_d, ev_d, wl_d = {}, {}, {}
        wbuf_w = self.wbuf_tail
        n = len(cols)
        for ci in range(n):
            col = cols[ci]
            par = ci % 2

            def g(g, ci=ci, col=col, par=par):
                if ci < 2:
                    g.wait_ge(wbuf_w[0].h, wbuf_w[1])
                else:
                    g.wait_ge(sT.h, mm_d[ci - 2])
                if wb_w:
                    g.dma_start(
                        out=bass.AP(mk.WBUF, par * 2048,
                                    [[4096, 128], [1, n_in_chunks * 128]]),
                        in_=bass.AP(wh, col * n_in_chunks * 128,
                                    [[C, 128], [1, n_in_chunks * 128]]),
                    ).then_inc(sW.h, 16)
                else:
                    g.dma_start(
                        out=bass.AP(mk.WBUF, par * 2048, [[4096, 128], [128, n_in_chunks], [1, 128]]),
                        in_=bass.AP(wh, col * 128, [[C, 128], [128 * C, n_in_chunks], [1, 128]]),
                    ).then_inc(sW.h, 16)
            self.G(g)
            wl_d[ci] = sW.inc(16)

            def t(t, ci=ci, par=par):
                t.wait_ge(sW.h, wl_d[ci])
                if ci == 0:
                    t.wait_ge(self.sG.h, self.ln["centered"][1])
                    for sm, val in extra_waits:
                        t.wait_ge(sm.h, val)
                if ci >= 2:
                    t.wait_ge(sV.h, ev_d[ci - 2])
                PB = mk.PB0 if par == 0 else mk.PB1
                ins = None
                for k in range(n_in_chunks):
                    for tb in range(4):
                        ins = t.matmul(PB[:, tb * 512:(tb + 1) * 512],
                                       mk.WBUF[:, par * 2048 + k * 128:par * 2048 + (k + 1) * 128],
                                       src_fn(k, tb),
                                       start=(k == 0), stop=(k == n_in_chunks - 1))
                ins.then_inc(sT.h, 1)
            self.T(t)
            mm_d[ci] = sT.inc()

            def v(v, ci=ci, par=par):
                v.wait_ge(sT.h, mm_d[ci])
                if spill is not None and ci >= 2:
                    v.wait_ge(sO.h, sp_d[ci - 2])
                if ci == 0 and use_rtile:
                    if r32:
                        v.wait_ge(self.ln["r32_tile"][0].h,
                                  self.ln["r32_tile"][1])
                    else:
                        v.wait_ge(self.ln["r_tile"][0].h,
                                  self.ln["r_tile"][1])
                PB = mk.PB0 if par == 0 else mk.PB1
                ins = None
                for tb in range(4):
                    pt = PB[:, tb * 512:(tb + 1) * 512]
                    dst = dest_fn(ci, tb)
                    if use_rtile:
                        rt = (mk.SC[:, tb * 512:(tb + 1) * 512] if r32
                              else mk.BCR[:, tb * 512:(tb + 1) * 512])
                        ins = v.scalar_tensor_tensor(
                            dst, pt, scale, rt, ALU.mult, ALU.mult)
                    elif scale != 1.0:
                        ins = v.tensor_scalar(dst, pt, scale, None, ALU.mult)
                    else:
                        ins = v.tensor_copy(dst, pt)
                ins.then_inc(sV.h, 1)
            self.V(v)
            ev_d[ci] = sV.inc()

            if spill is not None:
                def ysp(y, ci=ci):
                    y.wait_ge(sV.h, ev_d[ci])
                    y.dma_start(
                        out=bass.AP(spill[0], spill[1] + ci * 128 * 2048,
                                    [[2048, 128], [1, 2048]]),
                        in_=mk.SB[:, stg_off + (ci % 2) * 2048:
                                  stg_off + (ci % 2) * 2048 + 2048],
                    ).then_inc(sO.h, 16)
                self.Y(ysp)
                sp_d[ci] = sO.inc(16)
        self.wbuf_tail = (sT, mm_d[n - 1])
        return dict(ev=(sV, ev_d[n - 1]), mm=(sT, mm_d[n - 1]),
                    sp=(sO, sp_d[n - 1]) if spill is not None else None)

    def emit_vprod(self, w_sb_off, W, vdst_off, extra_waits=(), spill=None,
                   w_tensor=None, stg_off=0):
        mk = self
        sT, sV, sO = self.sT, self.sV, self.sO
        if w_tensor is None:
            w_tensor = mk.SA
        mm_d, ev_d, sp_d = {}, {}, {}
        for sc in range(16):
            par = sc % 2

            def t(t, sc=sc, par=par):
                if sc == 0:
                    t.wait_ge(self.sG.h, self.ln["centered"][1])
                    for sm, val in extra_waits:
                        t.wait_ge(sm.h, val)
                if sc >= 2:
                    t.wait_ge(sV.h, ev_d[sc - 2])
                PB = mk.PB0 if par == 0 else mk.PB1
                ins = None
                for k in range(NK):
                    for w0 in range(0, W, 512):
                        ww = min(512, W - w0)
                        ins = t.matmul(PB[:, w0:w0 + ww],
                                       mk.xbf(k, sc * 128, 128),
                                       w_tensor[:, w_sb_off + k * W + w0:
                                                w_sb_off + k * W + w0 + ww],
                                       start=(k == 0), stop=(k == NK - 1))
                ins.then_inc(sT.h, 1)
            self.T(t)
            mm_d[sc] = sT.inc()

            def v(v, sc=sc, par=par):
                v.wait_ge(sT.h, mm_d[sc])
                if sc == 0:
                    v.wait_ge(sV.h, self.ln["rcol"][1])
                if spill is not None and sc >= 2:
                    v.wait_ge(sO.h, sp_d[sc - 2])
                PB = mk.PB0 if par == 0 else mk.PB1
                off = (stg_off + (sc % 2) * W) if spill is not None                     else (vdst_off + sc * W)
                ins = v.tensor_scalar(mk.SB[:, off:off + W],
                                      PB[:, 0:W], mk.ROWS[:, 32 + sc:33 + sc],
                                      None, ALU.mult)
                ins.then_inc(sV.h, 1)
            self.V(v)
            ev_d[sc] = sV.inc()

            if spill is not None:
                def ysp(y, sc=sc):
                    y.wait_ge(sV.h, ev_d[sc])
                    y.dma_start(
                        out=bass.AP(spill[0], spill[1] + sc * 128 * W,
                                    [[W, 128], [1, W]]),
                        in_=mk.SB[:, stg_off + (sc % 2) * W:
                                  stg_off + (sc % 2) * W + W],
                    ).then_inc(sO.h, 16)
                self.Y(ysp)
                sp_d[sc] = sO.inc(16)
        return (sV, ev_d[15]) if spill is None else (sO, sp_d[15])

    def emit_attn_core(self, nheads, kt_off_fn, v_lhsT_fn, qkv_wait,
                       kt_tensor=None, q_src_fn=None, ot_dst_fn=None,
                       stage_ot=False, window=None, head_pre=None,
                       sel_cfg=False):
        mk = self
        sT, sV, sS, sX = self.sT, self.sV, self.sS, self.sX
        if kt_tensor is None:
            kt_tensor = mk.SB
        if q_src_fn is None:
            q_src_fn = lambda h: (mk.SA, h * 2048)
        if ot_dst_fn is None:
            ot_dst_fn = lambda h: (mk.SA, h * 2048)
        OT_STAGE = 14336
        exp_done = {}
        prev_tail = None
        pass2_stt_tail = {}
        for h in range(nheads):
            pre_wait = head_pre(h, prev_tail) if head_pre is not None else None
            kt0 = kt_off_fn(h)
            qsrc_t, qoff = q_src_fn(h)
            mm1 = {}
            for qt in range(16):
                if window is None:
                    klo = 0
                else:
                    klo = max(0, qt * 128 - window)
                khi = (qt + 1) * 128
                ext = khi - klo
                PB = mk.PB0 if qt % 2 == 0 else mk.PB1

                def t(t, h=h, qt=qt, klo=klo, ext=ext, PB=PB, kt0=kt0,
                      qoff=qoff, qsrc_t=qsrc_t, prev_tail=prev_tail,
                      pre_wait=pre_wait):
                    if qt < 2:
                        if h == 0:
                            t.wait_ge(qkv_wait[0].h, qkv_wait[1])
                        else:
                            t.wait_ge(prev_tail[0].h, prev_tail[1])
                            t.wait_ge(sS.h, exp_done[(h - 1, 14 + qt)])
                        if pre_wait is not None:
                            t.wait_ge(pre_wait[0].h, pre_wait[1])
                    else:
                        t.wait_ge(sS.h, exp_done[(h, qt - 2)])
                    ins = None
                    for b in range((ext + 511) // 512):
                        w = min(512, ext - b * 512)
                        ins = t.matmul(PB[:, b * 512:b * 512 + w],
                                       qsrc_t[:, qoff + qt * 128:qoff + (qt + 1) * 128],
                                       kt_tensor[:, kt0 + klo + b * 512:kt0 + klo + b * 512 + w],
                                       start=True, stop=True)
                    ins.then_inc(sT.h, 1)
                self.T(t)
                mm1[qt] = sT.inc()

                def v(v, qt=qt, klo=klo, ext=ext, PB=PB, h=h):
                    v.wait_ge(sT.h, mm1[qt])
                    v.reduce_max(mk.ROWS[:, 16 + qt:17 + qt], PB[:, 0:ext],
                                 axis=AX.X, negate=True)
                    if sel_cfg:
                        nb = (qt + 1) // 2
                        if nb >= 1:
                            v.tensor_tensor(
                                PB[:, 0:nb * 256], PB[:, 0:nb * 256],
                                bass.AP(mk.SB, 10240 + qt * 128 + h * 8,
                                        [[16384, 128], [1, nb], [0, 256]]),
                                ALU.add)
                    doff = qt * 128 - klo
                    ins = v.tensor_tensor(PB[:, doff:ext],
                                          PB[:, doff:ext],
                                          mk.CB["mask_qdiag"][:], ALU.add)
                    if window is not None and qt * 128 - window >= 0:
                        ins = v.tensor_tensor(PB[:, 0:128], PB[:, 0:128],
                                              mk.CB["mask_qlow"][:], ALU.add)
                    ins.then_inc(sV.h, 1)
                self.V(v)
                max_d = sV.inc()

                if qt % 2 == 1:
                    klo_p = 0 if window is None else max(0, (qt - 1) * 128 - window)
                    ext_p = qt * 128 - klo_p
                    PB_p = mk.PB0

                    def s(s, qt=qt, ext=ext, PB=PB, max_d=max_d,
                          ext_p=ext_p, PB_p=PB_p):
                        s.wait_ge(sV.h, max_d)
                        s.activation(mk.WBUF[:, 2048:2048 + ext_p],
                                     PB_p[:, 0:ext_p], AF.Exp,
                                     bias=mk.ROWS[:, 15 + qt:16 + qt],
                                     accum_out=mk.ROWS[:, 47 + qt:48 + qt])
                        ins = s.activation(mk.WBUF[:, 2048:2048 + ext],
                                           PB[:, 0:ext], AF.Exp,
                                           bias=mk.ROWS[:, 16 + qt:17 + qt],
                                           accum_out=mk.ROWS[:, 48 + qt:49 + qt])
                        ins.then_inc(sS.h, 1)
                    self.Sc(s)
                    exp_done[(h, qt - 1)] = exp_done[(h, qt)] = sS.inc()

            def sln(s, h=h):
                s.wait_ge(sS.h, exp_done[(h, 15)])
                ins = s.activation(mk.ROWS[:, 208:224], mk.ROWS[:, 48:64],
                                   AF.Ln)
                ins.then_inc(sS.h, 1)
            self.Sc(sln)
            ln_d = sS.inc()

            def vc2(v, ln_d=ln_d):
                v.wait_ge(sS.h, ln_d)
                ins = v.tensor_tensor(mk.ROWS[:, 16:32], mk.ROWS[:, 208:224],
                                      mk.ROWS[:, 16:32], ALU.subtract)
                ins.then_inc(sV.h, 1)
            self.V(vc2)
            sV.inc()
            cmat_done = (sV, sV.n)

            def t2(t, cm=cmat_done):
                t.wait_ge(sV.h, cm[1])
                t.transpose(mk.PB1[0:16, 1536:1664], mk.ROWS[:, 16:32],
                            mk.CB["ident"][:]).then_inc(sT.h, 1)
            self.T(t2)
            ctp = sT.inc()

            def v3(v, ctp=ctp):
                v.wait_ge(sT.h, ctp)
                v.tensor_copy(mk.ROWS[0:16, 64:192],
                              mk.PB1[0:16, 1536:1664]).then_inc(sV.h, 1)
            self.V(v3)
            cev = sV.inc()
            wrow = self.row_to_dram(mk.ROWS, 64, BNC_C0 + h % 2, (sV, cev),
                                    part16=True)
            ew = None
            if h >= 2:
                ew = (sV, pass2_stt_tail[h - 2])
            ctile = self.bcast_from_dram(BNC_C0 + h % 2, mk.BCF,
                                         (h % 2) * 2048, 2048, wrow,
                                         extra_wait=ew)

            coff = (h % 2) * 2048
            ktg = 0
            stt_done = {}
            pexp_done = {}
            selmul_done = {}
            pv_mm = {}
            otv = {}
            for qb in range(4):
                if window is None:
                    kts = list(range(qb * 4 + 4))
                else:
                    kts = list(range(max(0, qb * 4 - window // 128),
                                     qb * 4 + 4))
                nkt = len(kts)
                gi0 = 0
                while gi0 < nkt:
                    gn = 2 if gi0 + 1 < nkt else 1
                    grp = [(gi0 + j, kts[gi0 + j], ktg + j) for j in range(gn)]
                    for ki, kt, kg in grp:
                        pslot = kg % 4
                        sslot = kg % 2

                        def t3(t, qb=qb, kt=kt, ktg=kg, pslot=pslot, kt0=kt0,
                               qoff=qoff, qsrc_t=qsrc_t, ct=ctile):
                            if ktg == 0:
                                t.wait_ge(sX.h, ct[1])
                            if ktg >= 4:
                                t.wait_ge(sV.h, stt_done[ktg - 4])
                            t.matmul(mk.PB1[:, pslot * 512:(pslot + 1) * 512],
                                     kt_tensor[:, kt0 + kt * 128:kt0 + (kt + 1) * 128],
                                     qsrc_t[:, qoff + qb * 512:qoff + (qb + 1) * 512],
                                     start=True, stop=True).then_inc(sT.h, 1)
                        self.T(t3)
                        st_mm = sT.inc()

                        def v4(v, qb=qb, kt=kt, ktg=kg, pslot=pslot, sslot=sslot,
                               st_mm=st_mm, coff=coff):
                            v.wait_ge(sT.h, st_mm)
                            if ktg >= 2:
                                v.wait_ge(sS.h, pexp_done[ktg - 2])
                            ins = v.scalar_tensor_tensor(
                                mk.SC[:, 2048 + sslot * 512:2048 + (sslot + 1) * 512],
                                mk.PB1[:, pslot * 512:(pslot + 1) * 512], 1.0,
                                mk.BCF[:, coff + qb * 512:coff + (qb + 1) * 512],
                                ALU.mult, ALU.subtract)
                            ins.then_inc(sV.h, 1)
                        self.V(v4)
                        stt_done[kg] = sV.inc()

                    def s2(s, grp=grp):
                        kg_last = grp[-1][2]
                        s.wait_ge(sV.h, stt_done[kg_last])
                        if kg_last >= 4:
                            s.wait_ge(sT.h, pv_mm[kg_last - 4])
                        ins = None
                        for _, _, kg in grp:
                            ins = s.activation(
                                mk.WBUF[:, (kg % 4) * 512:(kg % 4 + 1) * 512],
                                mk.SC[:, 2048 + (kg % 2) * 512:
                                      2048 + (kg % 2 + 1) * 512],
                                AF.Exp)
                        ins.then_inc(sS.h, 1)
                    self.Sc(s2)
                    pe_val = sS.inc()
                    for _, _, kg in grp:
                        pexp_done[kg] = pe_val

                    for ki, kt, kg in grp:
                        ptile = kg % 4
                        self._attn_tile_tail(h, qb, ki, kt, kg, nkt, ptile,
                                             sel_cfg, window, exp_done,
                                             selmul_done, pexp_done, pv_mm,
                                             otv, v_lhsT_fn)
                    ktg += gn
                    gi0 += gn
                pv_done = (sT, sT.n)

                ot_t, ot_off = (mk.SB, OT_STAGE) if stage_ot else ot_dst_fn(h)

                def v5(v, qb=qb, pv=pv_done, ot_t=ot_t, ot_off=ot_off):
                    v.wait_ge(pv[0].h, pv[1])
                    ins = v.tensor_copy(
                        ot_t[:, ot_off + qb * 512:ot_off + (qb + 1) * 512],
                        mk.PB0[:, 0:512])
                    ins.then_inc(sV.h, 1)
                self.V(v5)
                otv[qb] = sV.inc()
            if stage_ot:
                dt_t, dt_off = ot_dst_fn(h)

                def v6(v, dt_t=dt_t, dt_off=dt_off, w=otv[3]):
                    v.wait_ge(sV.h, w)
                    ins = v.tensor_copy(dt_t[:, dt_off:dt_off + 2048],
                                        mk.SB[:, OT_STAGE:OT_STAGE + 2048])
                    ins.then_inc(sV.h, 1)
                self.V(v6)
                sV.inc()
            pass2_stt_tail[h] = stt_done[ktg - 1]
            prev_tail = (sV, sV.n)
        return (sV, sV.n)

    def _attn_tile_tail(self, h, qb, ki, kt, ktg, nkt, ptile, sel_cfg, window,
                        exp_done, selmul_done, pexp_done, pv_mm, otv,
                        v_lhsT_fn):
        mk = self
        sT, sV, sS = self.sT, self.sV, self.sS
        if True:
            if True:
                if True:
                    if sel_cfg:
                        blk = kt // 2
                        mslot = ktg % 2

                        def tmsk(t, h=h, qb=qb, ktg=ktg, blk=blk, mslot=mslot):
                            if ktg < 2:
                                t.wait_ge(sS.h, exp_done[(h, 14)])
                            else:
                                t.wait_ge(sV.h, selmul_done[ktg - 2])
                            moff = 512 + mslot * 512
                            t.matmul(mk.PB0[0:64, moff:moff + 512],
                                     mk.CB["epat"][0:8, blk * 64:blk * 64 + 64],
                                     mk.SB[0:8,
                                           8192 + qb * 512:8192 + (qb + 1) * 512],
                                     start=True, stop=True)
                            t.matmul(mk.PB0[64:128, moff:moff + 512],
                                     mk.CB["epat"][0:8, blk * 64:blk * 64 + 64],
                                     mk.SB[0:8,
                                           8192 + qb * 512:8192 + (qb + 1) * 512],
                                     start=True, stop=True,
                                     tile_position=(0, 64)).then_inc(sT.h, 1)
                        self.T(tmsk)
                        tm = sT.inc()

                        def v4s(v, ktg=ktg, ptile=ptile, mslot=mslot, tm=tm):
                            v.wait_ge(sT.h, tm)
                            v.wait_ge(sS.h, pexp_done[ktg])
                            ins = v.tensor_tensor(
                                mk.WBUF[:, ptile * 512:(ptile + 1) * 512],
                                mk.WBUF[:, ptile * 512:(ptile + 1) * 512],
                                mk.PB0[:, 512 + mslot * 512:1024 + mslot * 512],
                                ALU.mult)
                            ins.then_inc(sV.h, 1)
                            v.memset(mk.ROWS[:, 12:13], 0.0)
                        self.V(v4s)
                        selmul_done[ktg] = sV.inc()

                    pmask_done = None
                    if kt >= qb * 4:
                        vv = kt - qb * 4

                        def v4b(v, ktg=ktg, ptile=ptile, vv=vv):
                            if sel_cfg:
                                v.wait_ge(sV.h, selmul_done[ktg])
                            else:
                                v.wait_ge(sS.h, pexp_done[ktg])
                            ins = v.tensor_tensor(
                                mk.WBUF[:, ptile * 512:(ptile + 1) * 512],
                                mk.WBUF[:, ptile * 512:(ptile + 1) * 512],
                                mk.CB["bin_kdiag_m"][:, 384 - vv * 128:896 - vv * 128],
                                ALU.mult)
                            ins.then_inc(sV.h, 1)
                        self.V(v4b)
                        pmask_done = sV.inc()
                    elif window is not None and kt * 128 < qb * 512 - window + 512:
                        vv = kt - (qb * 4 - window // 128)

                        def v4c(v, ktg=ktg, ptile=ptile, vv=vv):
                            if sel_cfg:
                                v.wait_ge(sV.h, selmul_done[ktg])
                            else:
                                v.wait_ge(sS.h, pexp_done[ktg])
                            ins = v.tensor_tensor(
                                mk.WBUF[:, ptile * 512:(ptile + 1) * 512],
                                mk.WBUF[:, ptile * 512:(ptile + 1) * 512],
                                mk.CB["bin_klow_m"][:, 384 - vv * 128:896 - vv * 128],
                                ALU.mult)
                            ins.then_inc(sV.h, 1)
                        self.V(v4c)
                        pmask_done = sV.inc()

                    def t4(t, h=h, qb=qb, ki=ki, ktg=ktg, kt=kt, ptile=ptile,
                           nkt=nkt, pmask_done=pmask_done):
                        if pmask_done is not None:
                            t.wait_ge(sV.h, pmask_done)
                        elif sel_cfg:
                            t.wait_ge(sV.h, selmul_done[ktg])
                        else:
                            t.wait_ge(sS.h, pexp_done[ktg])
                        if ki == 0 and qb >= 1:
                            t.wait_ge(sV.h, otv[qb - 1])
                        t.matmul(mk.PB0[:, 0:512], v_lhsT_fn(h, kt),
                                 mk.WBUF[:, ptile * 512:(ptile + 1) * 512],
                                 start=(ki == 0), stop=(ki == nkt - 1)
                                 ).then_inc(sT.h, 1)
                    self.T(t4)
                    pv_mm[ktg] = sT.inc()

    def emit_out_proj_residual(self, wh, last, scale=1.0, src_chunk_fn=None,
                               n_in_chunks=NK, attn_wait=None, mult_tile=None,
                               wb_w=False):
        mk = self
        sW, sT, sV, sG, sX, sO = self.sW, self.sT, self.sV, self.sG, self.sX, self.sO
        if src_chunk_fn is None:
            src_chunk_fn = lambda k, tb: mk.SA[:, k * 2048 + tb * 512:k * 2048 + (tb + 1) * 512]
        C = wh.shape[1]
        mm_d, ev_d, wl_d, xl_d, st_d, cast_d = {}, {}, {}, {}, {}, {}
        dst_dram = mk.d_xout if last else mk.d_xres
        wbuf_w = self.wbuf_tail
        for m in range(16):
            par = m % 2

            def g(g, m=m, par=par):
                if m < 2:
                    g.wait_ge(wbuf_w[0].h, wbuf_w[1])
                    if attn_wait is not None:
                        g.wait_ge(attn_wait[0].h, attn_wait[1])
                else:
                    g.wait_ge(sT.h, mm_d[m - 2])
                if wb_w:
                    g.dma_start(
                        out=bass.AP(mk.WBUF, par * 2048,
                                    [[4096, 128], [1, n_in_chunks * 128]]),
                        in_=bass.AP(wh, m * n_in_chunks * 128,
                                    [[C, 128], [1, n_in_chunks * 128]]),
                    ).then_inc(sW.h, 16)
                else:
                    g.dma_start(
                        out=bass.AP(mk.WBUF, par * 2048, [[4096, 128], [128, n_in_chunks], [1, 128]]),
                        in_=bass.AP(wh, m * 128, [[C, 128], [128 * C, n_in_chunks], [1, 128]]),
                    ).then_inc(sW.h, 16)
            self.G(g)
            wl_d[m] = sW.inc(16)

            def y(y, m=m):
                if m == 0 and attn_wait is not None:
                    y.wait_ge(attn_wait[0].h, attn_wait[1])
                if m >= 1:
                    y.wait_ge(sO.h, st_d[m - 1])
                    y.wait_ge(sG.h, cast_d[m - 1])
                for hh in range(2):
                    y.dma_start(
                        out=mk.SC[:, hh * 1024:(hh + 1) * 1024],
                        in_=bass.AP(mk.d_xres, m * 128 * S + hh * 1024,
                                    [[S, 128], [1, 1024]]),
                    ).then_inc(sX.h, 16)
            self.Y(y)
            xl_d[m] = sX.inc(32)

            def t(t, m=m, par=par):
                t.wait_ge(sW.h, wl_d[m])
                if m == 0 and attn_wait is not None:
                    t.wait_ge(attn_wait[0].h, attn_wait[1])
                if m >= 2:
                    t.wait_ge(sV.h, ev_d[m - 2])
                PB = mk.PB0 if par == 0 else mk.PB1
                ins = None
                for k in range(n_in_chunks):
                    for tb in range(4):
                        ins = t.matmul(PB[:, tb * 512:(tb + 1) * 512],
                                       mk.WBUF[:, par * 2048 + k * 128:par * 2048 + (k + 1) * 128],
                                       src_chunk_fn(k, tb),
                                       start=(k == 0), stop=(k == n_in_chunks - 1))
                ins.then_inc(sT.h, 1)
            self.T(t)
            mm_d[m] = sT.inc()

            def v(v, m=m, par=par):
                v.wait_ge(sT.h, mm_d[m])
                v.wait_ge(sX.h, xl_d[m])
                PB = mk.PB0 if par == 0 else mk.PB1
                ins = None
                if mult_tile is None:
                    for tb in range(4):
                        ins = v.scalar_tensor_tensor(
                            mk.SC[:, tb * 512:(tb + 1) * 512],
                            PB[:, tb * 512:(tb + 1) * 512], scale,
                            mk.SC[:, tb * 512:(tb + 1) * 512],
                            ALU.mult, ALU.add)
                else:
                    mt_t, mt_off = mult_tile
                    for tb in range(4):
                        v.scalar_tensor_tensor(
                            mk.SC[:, 2048 + (tb % 2) * 512:2048 + (tb % 2 + 1) * 512],
                            PB[:, tb * 512:(tb + 1) * 512], scale,
                            mt_t[:, mt_off + tb * 512:mt_off + (tb + 1) * 512],
                            ALU.mult, ALU.mult)
                        if tb >= 1:
                            v.tensor_tensor(
                                mk.SC[:, (tb - 1) * 512:tb * 512],
                                mk.SC[:, (tb - 1) * 512:tb * 512],
                                mk.SC[:, 2048 + ((tb - 1) % 2) * 512:2048 + ((tb - 1) % 2 + 1) * 512],
                                ALU.add)
                    v.memset(mk.ROWS[:, 12:13], 0.0)
                    ins = v.tensor_tensor(
                        mk.SC[:, 3 * 512:4 * 512], mk.SC[:, 3 * 512:4 * 512],
                        mk.SC[:, 2048 + 512:2048 + 1024], ALU.add)
                ins.then_inc(sV.h, 1)
            self.V(v)
            ev_d[m] = sV.inc()

            def gc(g, m=m):
                g.wait_ge(sV.h, ev_d[m])
                ins = None
                for tb in range(4):
                    ins = g.tensor_copy(mk.xbf(m, tb * 512, 512),
                                        mk.SC[:, tb * 512:(tb + 1) * 512])
                ins.then_inc(sG.h, 1)
            self.G(gc)
            cast_d[m] = sG.inc()

            def y2(y, m=m):
                y.wait_ge(sV.h, ev_d[m])
                for hh in range(2):
                    y.dma_start(
                        out=bass.AP(dst_dram, m * 128 * S + hh * 1024,
                                    [[S, 128], [1, 1024]]),
                        in_=mk.SC[:, hh * 1024:(hh + 1) * 1024],
                    ).then_inc(sO.h, 16)
            self.Y(y2)
            st_d[m] = sO.inc(32)
        self.xbf_ready = [(sG, cast_d[15])]
        self.wbuf_tail = (sT, mm_d[15])
        return (sO, st_d[15])

    def finish(self):
        mk = self

        def y(y):
            y.wait_ge(mk.sO.h, mk.sO.n)
        self.Y(y)
        self.es.close()


def sub_attn_b(mk, W, last):
    wq = mk.add_weight("L0b_wq", W["w_q_wb"])
    wkv = mk.add_weight("L0b_wkv", W["w_kv"])
    wkvb = mk.add_weight("L0b_wkv_wb", W["w_kv_wb"])
    wd = mk.add_weight("L0b_wd", W["w_dense_wb"])

    mk.emit_stats_center(rms2=False)

    kp = mk.emit_proj(wkvb, [0, 1, 2, 3],
                      lambda ci, tb: mk.SB[:, ci * 2048 + tb * 512:ci * 2048 + (tb + 1) * 512],
                      wb_w=True)

    def gv(g):
        g.wait_ge(mk.sT.h, mk.ln["stats_T"][1])
        g.dma_start(
            out=bass.AP(mk.SA, 0, [[32768, 128], [512, NK], [1, 512]]),
            in_=bass.AP(wkv, 512, [[1024, 128], [128 * 1024, NK], [1, 512]]),
        ).then_inc(mk.sW.h, 16)
    mk.G(gv)
    vw_loaded = mk.sW.inc(16)

    vp = mk.emit_vprod(0, 512, 8192,
                       extra_waits=[(mk.sW, vw_loaded), kp["ev"]])

    qp = mk.emit_proj(wq, list(range(16)),
                      lambda ci, tb: mk.SA[:, ci * 2048 + tb * 512:ci * 2048 + (tb + 1) * 512],
                      scale=1.0 / math.sqrt(128.0),
                      extra_waits=[vp], wb_w=True)

    att = mk.emit_attn_core(
        nheads=16,
        kt_off_fn=lambda h: (h // 4) * 2048,
        v_lhsT_fn=lambda h, kt: mk.SB[:, 8192 + kt * 512 + (h // 4) * 128:
                                      8192 + kt * 512 + (h // 4) * 128 + 128],
        qkv_wait=qp["ev"])

    return mk.emit_out_proj_residual(wd, last, attn_wait=att, wb_w=True)



def emit_streamed_attention(mk, wv_src, kt_cols, q_cols, wh_qkv, qscale,
                            window=None, nheads=16, kt_w=None, q_w=None,
                            m_wait=None):
    sW, sX, sT, sV = mk.sW, mk.sX, mk.sT, mk.sV
    QSP = (mk.d_spill, 0)
    VSP = (mk.d_spill, 2048 * 2048)

    wvh, wvoff = wv_src
    Cw = wvh.shape[1]

    def gwv(g):
        g.wait_ge(sT.h, mk.ln["stats_T"][1])
        for i in range(4):
            g.dma_start(
                out=bass.AP(mk.SA, i * 4 * 2048, [[32768, 128], [2048, 4], [1, 2048]]),
                in_=bass.AP(wvh, wvoff + i * 4 * 128 * Cw,
                            [[Cw, 128], [128 * Cw, 4], [1, 2048]]),
            ).then_inc(sW.h, 16)
    mk.G(gwv)
    wv_ld = sW.inc(64)
    vw = [(sW, wv_ld)] + ([m_wait] if m_wait is not None else [])
    vp = mk.emit_vprod(0, 2048, 0, extra_waits=vw,
                       spill=VSP, w_tensor=mk.SA, stg_off=0)
    kp = mk.emit_proj(kt_w if kt_w is not None else wh_qkv, kt_cols,
                      lambda ci, tb: mk.SA[:, ci * 2048 + tb * 512:ci * 2048 + (tb + 1) * 512],
                      extra_waits=[vp], wb_w=True)
    qp = mk.emit_proj(q_w if q_w is not None else wh_qkv, q_cols, scale=qscale,
                      spill=QSP, stg_off=4096, extra_waits=[kp["ev"]], wb_w=True)

    qv_ld = {}

    def emit_qv_load(hh, waits):
        def y0(y, hh=hh, waits=waits):
            for sm, val in waits:
                y.wait_ge(sm.h, val)
            y.dma_start(
                out=mk.SB[:, (hh % 2) * 2048:(hh % 2) * 2048 + 2048],
                in_=bass.AP(mk.d_spill, QSP[1] + hh * 128 * 2048,
                            [[2048, 128], [1, 2048]])).then_inc(sX.h, 16)
            y.dma_start(
                out=mk.SB[:, 4096 + (hh % 2) * 2048:4096 + (hh % 2) * 2048 + 2048],
                in_=bass.AP(mk.d_spill, VSP[1] + hh * 128,
                            [[2048, 128], [128 * 2048, 16], [1, 128]])
            ).then_inc(sX.h, 16)
        mk.Y(y0)
        qv_ld[hh] = sX.inc(32)

    def head_pre(h, prev_tail):
        if h == 0:
            emit_qv_load(0, [qp["sp"], vp])
        if h + 1 < nheads:
            emit_qv_load(h + 1, [prev_tail] if prev_tail is not None
                         else [qp["sp"], vp])
        return (sX, qv_ld[h])

    att = mk.emit_attn_core(
        nheads=nheads, kt_tensor=mk.SA, kt_off_fn=lambda h: h * 2048,
        q_src_fn=lambda h: (mk.SB, (h % 2) * 2048),
        v_lhsT_fn=lambda h, kt: mk.SB[:, 4096 + (h % 2) * 2048 + kt * 128:
                                      4096 + (h % 2) * 2048 + (kt + 1) * 128],
        ot_dst_fn=lambda h: (mk.SA, h * 2048),
        stage_ot=True, head_pre=head_pre, qkv_wait=qp["ev"], window=window)
    return att, qp


def sub_attn_a(mk, W, last):
    wqkv = mk.add_weight("L0a_wqkv", W["wqkv_l"])
    wqkvb = mk.add_weight("L0a_wqkv_wb", W["wqkv_wb"])
    wout = mk.add_weight("L0a_wout", W["wout_wb"])
    mk.emit_stats_center(rms2=False)
    att, qp = emit_streamed_attention(
        mk, (wqkv, 4096), list(range(16, 32)), list(range(16)),
        wqkvb, 1.0 / math.sqrt(128.0), window=512)
    return mk.emit_out_proj_residual(wout, last, attn_wait=att, wb_w=True)


def sub_ffn(mk, W, last, stage=99):
    nm = "L0f_"
    wq = mk.add_weight(nm + "wq", W["wq_l"])
    wk = mk.add_weight(nm + "wk", W["wk_l"])
    wv = mk.add_weight(nm + "wv", W["wv_l"])
    wo = mk.add_weight(nm + "wo", W["wo_raw"])
    wa = mk.add_weight(nm + "A", W["A"])
    wb = mk.add_weight(nm + "B", W["B"])
    win = mk.add_weight(nm + "win", W["win_l"])
    wup = mk.add_weight(nm + "wup", W["wup_l"])
    wdn = mk.add_weight(nm + "wdn", W["wdn_l"])
    lc1 = mk.add_weight(nm + "ln1c", W["ln1c"])
    lc2 = mk.add_weight(nm + "ln2c", W["ln2c"])
    sW, sX, sT, sV, sS, sG, sO = (mk.sW, mk.sX, mk.sT, mk.sV, mk.sS, mk.sG,
                                  mk.sO)
    QSP = (mk.d_spill, 0)
    VSP = (mk.d_spill, 2048 * 2048)

    ln = mk.emit_stats_center(rms2=True)
    if stage <= 0:
        return (mk.sO, mk.sO.n)

    def gld(g):
        g.dma_start(out=mk.SB[:, 8192:8448],
                    in_=bass.AP(wa, 0, [[16, 128], [128 * 16, 16], [1, 16]]),
                    ).then_inc(sW.h, 16)
        g.dma_start(out=mk.LNC[:, 0:16],
                    in_=lc1.ap()).then_inc(sW.h, 16)
        g.dma_start(out=mk.LNC[:, 16:32],
                    in_=lc2.ap()).then_inc(sW.h, 16)
    mk.G(gld)
    small_ld = sW.inc(48)

    wbw = mk.wbuf_tail
    mb_mm, mb_ev, mb_wl = {}, {}, {}
    for ec in range(16):
        par = ec % 2

        def g(g, ec=ec, par=par):
            if ec < 2:
                g.wait_ge(wbw[0].h, wbw[1])
            else:
                g.wait_ge(sT.h, mb_mm[ec - 2])
            g.dma_start(
                out=bass.AP(mk.WBUF, par * 2048, [[4096, 128], [1, 2048]]),
                in_=bass.AP(wo, ec * 2048, [[32768, 128], [1, 2048]]),
            ).then_inc(sW.h, 16)
        mk.G(g)
        mb_wl[ec] = sW.inc(16)

        def t(t, ec=ec, par=par):
            t.wait_ge(sW.h, mb_wl[ec])
            if ec == 0:
                t.wait_ge(sW.h, small_ld)
            if ec >= 2:
                t.wait_ge(sV.h, mb_ev[ec - 2])
            PB = mk.PB0 if par == 0 else mk.PB1
            ins = None
            for fc in range(NK):
                ins = t.matmul(PB[:, 0:16],
                               mk.WBUF[:, par * 2048 + fc * 128:par * 2048 + (fc + 1) * 128],
                               mk.SB[:, 8192 + fc * 16:8192 + (fc + 1) * 16],
                               start=(fc == 0), stop=(fc == NK - 1))
            ins.then_inc(sT.h, 1)
        mk.T(t)
        mb_mm[ec] = sT.inc()

        def v(v, ec=ec, par=par):
            v.wait_ge(sT.h, mb_mm[ec])
            PB = mk.PB0 if par == 0 else mk.PB1
            ins = v.tensor_copy(mk.SB[:, 8448 + ec * 16:8448 + (ec + 1) * 16],
                                PB[:, 0:16])
            ins.then_inc(sV.h, 1)
        mk.V(v)
        mb_ev[ec] = sV.inc()
    mk.wbuf_tail = (sT, mb_mm[15])
    m_done = (sV, mb_ev[15])

    def gs1(g):
        g.wait_ge(sG.h, ln["centered"][1])
        g.wait_ge(sW.h, small_ld)
        ins = None
        for k in range(NK):
            ins = g.tensor_scalar(mk.xbf(k), mk.xbf(k), mk.LNC[:, k:k + 1],
                                  None, ALU.mult)
        ins.then_inc(sG.h, 1)
    mk.G(gs1)
    xscaled1 = (sG, sG.inc())
    mk.ln["centered"] = xscaled1

    if stage <= 3:
        return (mk.sO, mk.sO.n)
    att, qp = emit_streamed_attention(
        mk, (wv, 0), list(range(16)), list(range(16)), None,
        1.0 / math.sqrt(128.0), kt_w=wk, q_w=wq, m_wait=m_done)

    if stage <= 4:
        return (mk.sO, mk.sO.n)
    md_mm = {}
    for tb in range(4):
        def t(t, tb=tb):
            if tb == 0:
                t.wait_ge(att[0].h, att[1])
            ins = None
            for k in range(NK):
                ins = t.matmul(mk.PB0[0:16, tb * 512:(tb + 1) * 512],
                               mk.SB[:, 8448 + k * 16:8448 + (k + 1) * 16],
                               mk.SA[:, k * 2048 + tb * 512:k * 2048 + (tb + 1) * 512],
                               start=(k == 0), stop=(k == NK - 1))
            ins.then_inc(sT.h, 1)
        mk.T(t)
        md_mm[tb] = sT.inc()

    def smod(s):
        ins = None
        for tb in range(4):
            s.wait_ge(sT.h, md_mm[tb])
            ins = s.activation(mk.BCF[0:16, 2048 + tb * 512:2048 + (tb + 1) * 512],
                               mk.PB0[0:16, tb * 512:(tb + 1) * 512], AF.Tanh)
        ins.then_inc(sS.h, 1)
    mk.Sc(smod)
    mod_done = (sS, sS.inc())

    def gre(g):
        g.wait_ge(qp["mm"][0].h, qp["mm"][1])
        for i in range(4):
            g.dma_start(
                out=bass.AP(mk.XBF, i * 4 * S, [[NK * S, 128], [S, 4], [1, S]]),
                in_=bass.AP(mk.d_xres, i * 4 * 128 * S,
                            [[S, 128], [128 * S, 4], [1, S]]),
            ).then_inc(sX.h, 16)
    mk.G(gre)
    xre = sX.inc(64)
    mu_tile2 = mk.bcast_from_dram(BNC_MU, mk.BCF, 0, 2048, (sX, xre),
                                  extra_wait=att)

    def gc2(g):
        g.wait_ge(sX.h, mu_tile2[1])
        ins = None
        for k in range(NK):
            ins = g.tensor_tensor(mk.xbf(k), mk.xbf(k), mk.BCF[:, 0:2048],
                                  ALU.subtract)
        ins.then_inc(sG.h, 1)
        for k in range(NK):
            ins = g.tensor_scalar(mk.xbf(k), mk.xbf(k), mk.LNC[:, 16 + k:17 + k],
                                  None, ALU.mult)
        ins.then_inc(sG.h, 1)
    mk.G(gc2)
    sG.inc()
    x2ready = (sG, sG.inc())

    ha_mm = {}
    for tb in range(4):
        def t(t, tb=tb):
            if tb == 0:
                t.wait_ge(x2ready[0].h, x2ready[1])
            ins = None
            for k in range(NK):
                ins = t.matmul(mk.PB1[0:16, tb * 512:(tb + 1) * 512],
                               mk.SB[:, 8192 + k * 16:8192 + (k + 1) * 16],
                               mk.xbf(k, tb * 512, 512),
                               start=(k == 0), stop=(k == NK - 1))
            ins.then_inc(sT.h, 1)
        mk.T(t)
        ha_mm[tb] = sT.inc()

    def vha(v):
        ins = None
        for tb in range(4):
            v.wait_ge(sT.h, ha_mm[tb])
            ins = v.scalar_tensor_tensor(
                mk.SC[0:16, tb * 512:(tb + 1) * 512],
                mk.PB1[0:16, tb * 512:(tb + 1) * 512], 1.0,
                mk.BCR[0:16, tb * 512:(tb + 1) * 512], ALU.mult, ALU.mult)
        ins.then_inc(sV.h, 1)
    mk.V(vha)
    ha_done = (sV, sV.inc())

    rrow = mk.bcast_from_dram(BNC_R, mk.BCF, 0, 2048, x2ready)

    def vinv(v):
        v.wait_ge(sX.h, rrow[1])
        ins = v.reciprocal(mk.BCF[0:16, 0:2048], mk.BCF[0:16, 0:2048])
        ins.then_inc(sV.h, 1)
    mk.V(vinv)
    inv_done = (sV, sV.inc())

    def vgp(v):
        v.wait_ge(mod_done[0].h, mod_done[1])
        v.wait_ge(ha_done[0].h, ha_done[1])
        v.wait_ge(inv_done[0].h, inv_done[1])
        v.tensor_tensor(mk.SC[0:16, 0:2048], mk.SC[0:16, 0:2048],
                        mk.BCF[0:16, 2048:4096], ALU.mult)
        v.memset(mk.ROWS[:, 12:13], 0.0)
        ins = v.scalar_tensor_tensor(mk.SB[0:16, 12288:14336],
                                     mk.SC[0:16, 0:2048], 0.1,
                                     mk.BCF[0:16, 0:2048], ALU.mult, ALU.mult)
        ins.then_inc(sV.h, 1)
    mk.V(vgp)
    gp_done = (sV, sV.inc())

    def gB(g):
        g.wait_ge(att[0].h, att[1])
        g.dma_start(out=mk.SB[0:16, 0:8192], in_=wb.ap()).then_inc(sW.h, 16)
    mk.G(gB)
    b_ld = sW.inc(16)

    if stage <= 5:
        return (mk.sO, mk.sO.n)
    import os
    NMCH = int(os.environ.get("FFN_NM", "64"))
    gl_mm, up_mm, pr_ev, pr_sp, wl_w, wl_u, g_ev = {}, {}, {}, {}, {}, {}, {}
    silu_d = {}
    wbw2 = mk.wbuf_tail
    for m in range(NMCH):
        par = m % 2

        def g(g, m=m, par=par):
            if m < 2:
                g.wait_ge(wbw2[0].h, wbw2[1])
                g.wait_ge(att[0].h, att[1])
            else:
                g.wait_ge(sT.h, gl_mm[m - 2])
            g.dma_start(
                out=bass.AP(mk.WBUF, par * 2048, [[4096, 128], [1, 2048]]),
                in_=bass.AP(win, m * 2048, [[131072, 128], [1, 2048]]),
            ).then_inc(sW.h, 16)
        mk.G(g)
        wl_w[m] = sW.inc(16)

        def g2(g, m=m):
            if m == 0:
                g.wait_ge(att[0].h, att[1])
            else:
                g.wait_ge(sT.h, up_mm[m - 1])
            g.dma_start(
                out=bass.AP(mk.SB, 14336, [[16384, 128], [1, 2048]]),
                in_=bass.AP(wup, m * 2048, [[131072, 128], [1, 2048]]),
            ).then_inc(sW.h, 16)
        mk.G(g2)
        wl_u[m] = sW.inc(16)

        def t(t, m=m, par=par):
            t.wait_ge(sW.h, wl_w[m])
            if m == 0:
                t.wait_ge(gp_done[0].h, gp_done[1])
                t.wait_ge(sW.h, b_ld)
            if m >= 1:
                t.wait_ge(sV.h, pr_ev[m - 1])
            PB = mk.PB0 if par == 0 else mk.PB1
            ins = None
            for k in range(NK):
                for tb in range(4):
                    ins = t.matmul(PB[:, tb * 512:(tb + 1) * 512],
                                   mk.WBUF[:, par * 2048 + k * 128:par * 2048 + (k + 1) * 128],
                                   mk.xbf(k, tb * 512, 512),
                                   start=(k == 0), stop=False)
            for tb in range(4):
                ins = t.matmul(PB[:, tb * 512:(tb + 1) * 512],
                               mk.SB[0:16, m * 128:(m + 1) * 128],
                               mk.SB[0:16, 12288 + tb * 512:12288 + (tb + 1) * 512],
                               start=False, stop=True)
            ins.then_inc(sT.h, 1)
        mk.T(t)
        gl_mm[m] = sT.inc()

        def v(v, m=m, par=par):
            v.wait_ge(sT.h, gl_mm[m])
            if m >= 1:
                v.wait_ge(sS.h, silu_d[m - 1])
            PB = mk.PB0 if par == 0 else mk.PB1
            ins = None
            for tb in range(4):
                ins = v.scalar_tensor_tensor(
                    mk.SC[:, tb * 512:(tb + 1) * 512],
                    PB[:, tb * 512:(tb + 1) * 512], 1.0,
                    mk.BCR[:, tb * 512:(tb + 1) * 512], ALU.mult, ALU.mult)
            ins.then_inc(sV.h, 1)
        mk.V(v)
        g_ev[m] = sV.inc()

        def s(s, m=m):
            s.wait_ge(sV.h, g_ev[m])
            if m >= 1:
                s.wait_ge(sV.h, pr_ev[m - 1])
            ins = s.activation(mk.SB[:, 8192:10240], mk.SC[:, 0:2048], AF.Silu)
            ins.then_inc(sS.h, 1)
        mk.Sc(s)
        silu_d[m] = sS.inc()

        def t2(t, m=m, par=par):
            t.wait_ge(sW.h, wl_u[m])
            if m == 0:
                t.wait_ge(ha_done[0].h, ha_done[1])
            if m >= 1:
                t.wait_ge(sV.h, g_ev[m - 1])
            PB = mk.PB1 if par == 0 else mk.PB0
            ins = None
            for k in range(NK):
                for tb in range(4):
                    ins = t.matmul(PB[:, tb * 512:(tb + 1) * 512],
                                   mk.SB[:, 14336 + k * 128:14336 + (k + 1) * 128],
                                   mk.xbf(k, tb * 512, 512),
                                   start=(k == 0), stop=(k == NK - 1))
            ins.then_inc(sT.h, 1)
        mk.T(t2)
        up_mm[m] = sT.inc()

        def v2(v, m=m, par=par):
            v.wait_ge(sT.h, up_mm[m])
            v.wait_ge(sS.h, silu_d[m])
            if m >= 1:
                v.wait_ge(sO.h, pr_sp[m - 1])
            PB = mk.PB1 if par == 0 else mk.PB0
            ins = None
            for tb in range(4):
                ins = v.tensor_tensor(
                    mk.SB[:, 10240 + tb * 512:10240 + (tb + 1) * 512],
                    mk.SB[:, 8192 + tb * 512:8192 + (tb + 1) * 512],
                    PB[:, tb * 512:(tb + 1) * 512], ALU.mult)
            ins.then_inc(sV.h, 1)
        mk.V(v2)
        pr_ev[m] = sV.inc()

        def y(y, m=m):
            y.wait_ge(sV.h, pr_ev[m])
            y.dma_start(
                out=bass.AP(mk.d_spill2, m * 128 * 2048, [[2048, 128], [1, 2048]]),
                in_=mk.SB[:, 10240:12288]).then_inc(sO.h, 16)
        mk.Y(y)
        pr_sp[m] = sO.inc(16)
    mk.wbuf_tail = (sT, gl_mm[NMCH - 1])

    if stage <= 6:
        return (mk.sO, mk.sO.n)
    r_r2 = mk.bcast_from_dram(BNC_R, mk.BCF, 0, 2048,
                              (sV, pr_ev[NMCH - 1]),
                              extra_wait=(sS, silu_d[NMCH - 1]))
    def vr1(v):
        v.wait_ge(sX.h, r_r2[1])
        ins = v.reciprocal(mk.BCF[0:1, 0:2048], mk.BCF[0:1, 0:2048])
        ins.then_inc(sV.h, 1)
    mk.V(vr1)
    invrow = (sV, sV.inc())
    r_rln = mk.bcast_from_dram(BNC_RLN, mk.BCF, 2048, 2048, invrow)
    r_mu2 = mk.bcast_from_dram(BNC_MU, mk.SC, 0, 2048, r_rln)
    def vrow(v):
        v.wait_ge(sX.h, r_mu2[1])
        v.wait_ge(sT.h, up_mm[NMCH - 1])
        v.tensor_tensor(mk.SC[0:1, 2048:3072][0:1, 0:1024], mk.SC[0:1, 0:1024],
                        mk.BCF[0:1, 2048:3072], ALU.mult)
        v.tensor_tensor(mk.SC[0:1, 1024:2048], mk.SC[0:1, 1024:2048],
                        mk.BCF[0:1, 3072:4096], ALU.mult)
        v.scalar_tensor_tensor(mk.SB[0:1, 14336:15360], mk.SC[0:1, 2048:3072][0:1, 0:1024],
                               -1.0, mk.BCF[0:1, 0:1024], ALU.mult, ALU.mult)
        ins = v.scalar_tensor_tensor(mk.SB[0:1, 15360:16384], mk.SC[0:1, 1024:2048],
                                     -1.0, mk.BCF[0:1, 1024:2048], ALU.mult, ALU.mult)
        v.memset(mk.ROWS[:, 12:13], 0.0)
        ins = v.tensor_scalar(mk.BCF[:, 2048:4096], mk.BCF[:, 2048:4096],
                              1.0, None, ALU.add)
        ins.then_inc(sV.h, 1)
    mk.V(vrow)
    rows_done = (sV, sV.inc())

    dst_dram = mk.d_xout if last else mk.d_xres
    dn_wl, dn_mm, dn_mmg, dn_ev, dn_xl, dn_st, dn_cast, blk_ld = {}, {}, {}, {}, {}, {}, {}, {}
    for tb in range(4):
        def yb(y, tb=tb):
            if tb == 0:
                y.wait_ge(sO.h, pr_sp[NMCH - 1])
                y.wait_ge(att[0].h, att[1])
            else:
                y.wait_ge(sT.h, dn_mm[(tb - 1, 15)])
            for i in range(4):
                y.dma_start(
                    out=bass.AP(mk.SA, i * 16 * 512, [[32768, 128], [512, 16], [1, 512]]),
                    in_=bass.AP(mk.d_spill2, tb * 512 + i * 16 * 128 * 2048,
                                [[2048, 128], [128 * 2048, 16], [1, 512]]),
                ).then_inc(sX.h, 16)
        mk.Y(yb)
        blk_ld[tb] = sX.inc(64)

        for m in range(16):
            par = m % 2

            def yx(y, tb=tb, m=m, par=par):
                if tb == 0 and m < 2:
                    y.wait_ge(rows_done[0].h, rows_done[1])
                if (tb, m) not in [(0, 0), (0, 1)]:
                    pm = (tb, m - 2) if m >= 2 else (tb - 1, 14 + m)
                    y.wait_ge(sO.h, dn_st[pm])
                    y.wait_ge(sG.h, dn_cast[pm])
                y.dma_start(
                    out=mk.SC[:, 1024 + par * 512:1024 + (par + 1) * 512],
                    in_=bass.AP(mk.d_xres, m * 128 * S + tb * 512,
                                [[S, 128], [1, 512]]),
                ).then_inc(sX.h, 16)
            mk.Y(yx)
            dn_xl[(tb, m)] = sX.inc(16)

            for kg in range(4):
                u = tb * 64 + m * 4 + kg
                upar = u % 2

                def g(g, tb=tb, m=m, kg=kg, u=u, upar=upar):
                    if u < 2:
                        g.wait_ge(sT.h, up_mm[NMCH - 1])
                    else:
                        g.wait_ge(sT.h, dn_mmg[u - 2])
                    g.dma_start(
                        out=bass.AP(mk.WBUF, upar * 2048, [[4096, 128], [1, 2048]]),
                        in_=bass.AP(wdn, (m * 4 + kg) * 2048,
                                    [[131072, 128], [1, 2048]]),
                    ).then_inc(sW.h, 16)
                mk.G(g)
                dn_wl[u] = sW.inc(16)

                def t(t, tb=tb, m=m, kg=kg, u=u, upar=upar, par=par):
                    t.wait_ge(sW.h, dn_wl[u])
                    if kg == 0:
                        t.wait_ge(sX.h, blk_ld[tb])
                        if tb == 0 and m == 0:
                            t.wait_ge(rows_done[0].h, rows_done[1])
                        if (tb, m) not in [(0, 0), (0, 1)]:
                            pm = (tb, m - 2) if m >= 2 else (tb - 1, 14 + m)
                            t.wait_ge(sV.h, dn_ev[pm])
                    pt = (mk.PB0 if par == 0 else mk.PB1)[:, 0:512]
                    ins = None
                    for k in range(16):
                        ins = t.matmul(pt,
                                       mk.WBUF[:, upar * 2048 + k * 128:upar * 2048 + (k + 1) * 128],
                                       mk.SA[:, (kg * 16 + k) * 512:(kg * 16 + k + 1) * 512],
                                       start=(kg == 0 and k == 0), stop=False)
                    if kg == 3:
                        ins = t.matmul(pt, mk.CB["ones_row_b"][:],
                                       mk.SB[0:1, 14336 + tb * 512:14336 + (tb + 1) * 512],
                                       start=False, stop=True)
                    ins.then_inc(sT.h, 1)
                mk.T(t)
                dn_mmg[u] = sT.inc()
            dn_mm[(tb, m)] = dn_mmg[tb * 64 + m * 4 + 3]

            def v(v, tb=tb, m=m, par=par):
                v.wait_ge(sT.h, dn_mm[(tb, m)])
                v.wait_ge(sX.h, dn_xl[(tb, m)])
                pt = (mk.PB0 if par == 0 else mk.PB1)[:, 0:512]
                xs = mk.SC[:, 1024 + par * 512:1024 + (par + 1) * 512]
                v.scalar_tensor_tensor(mk.SC[:, 0:512], pt, 1.0,
                                       mk.BCR[:, tb * 512:(tb + 1) * 512],
                                       ALU.mult, ALU.mult)
                v.scalar_tensor_tensor(xs, mk.BCF[:, 2048 + tb * 512:2048 + (tb + 1) * 512],
                                       1.0, xs, ALU.mult, ALU.mult)
                v.memset(mk.ROWS[:, 12:13], 0.0)
                ins = v.tensor_tensor(xs, xs, mk.SC[:, 0:512], ALU.add)
                ins.then_inc(sV.h, 1)
            mk.V(v)
            dn_ev[(tb, m)] = sV.inc()

            def gc(g, tb=tb, m=m, par=par):
                g.wait_ge(sV.h, dn_ev[(tb, m)])
                ins = g.tensor_copy(mk.xbf(m, tb * 512, 512),
                                    mk.SC[:, 1024 + par * 512:1024 + (par + 1) * 512])
                ins.then_inc(sG.h, 1)
            mk.G(gc)
            dn_cast[(tb, m)] = sG.inc()

            def y2(y, tb=tb, m=m, par=par):
                y.wait_ge(sV.h, dn_ev[(tb, m)])
                y.dma_start(
                    out=bass.AP(dst_dram, m * 128 * S + tb * 512,
                                [[S, 128], [1, 512]]),
                    in_=mk.SC[:, 1024 + par * 512:1024 + (par + 1) * 512],
                ).then_inc(sO.h, 16)
            mk.Y(y2)
            dn_st[(tb, m)] = sO.inc(16)
    mk.xbf_ready = [(sG, dn_cast[(3, 15)])]
    mk.wbuf_tail = (sT, dn_mm[(3, 15)])
    return (sO, dn_st[(3, 15)])


def emit_rope_kappa(mk, tiles, cos_off, sin_off, pmat, kappa, waits):
    sT, sV, sS = mk.sT, mk.sV, mk.sS
    last_v = None
    for (tt, off) in tiles:
        def t(t, tt=tt, off=off, last_v=last_v):
            if last_v is None:
                for sm, val in waits:
                    t.wait_ge(sm.h, val)
            else:
                t.wait_ge(sV.h, last_v)
            ins = None
            for b in range(4):
                ins = t.matmul(mk.PB0[:, b * 512:(b + 1) * 512],
                               mk.CB[pmat][:],
                               tt[:, off + b * 512:off + (b + 1) * 512],
                               start=True, stop=True)
            ins.then_inc(sT.h, 1)
        mk.T(t)
        mm_d = mk.sT.inc()

        def v(v, tt=tt, off=off, mm_d=mm_d):
            v.wait_ge(sT.h, mm_d)
            v.scalar_tensor_tensor(mk.WBUF[:, 0:2048], mk.PB0[:, 0:2048],
                                   1.0, mk.BCF[:, sin_off:sin_off + 2048],
                                   ALU.mult, ALU.mult)
            v.scalar_tensor_tensor(mk.WBUF[:, 2048:4096],
                                   tt[:, off:off + 2048], 1.0,
                                   mk.BCF[:, cos_off:cos_off + 2048],
                                   ALU.mult, ALU.mult)
            v.memset(mk.ROWS[:, 12:13], 0.0)
            ins = v.tensor_tensor(mk.SC[:, 0:2048], mk.WBUF[:, 0:2048],
                                  mk.WBUF[:, 2048:4096], ALU.add)
            if not kappa:
                v.memset(mk.ROWS[:, 13:14], 0.0)
                ins = v.tensor_copy(tt[:, off:off + 2048], mk.SC[:, 0:2048])
            ins.then_inc(sV.h, 1)
        mk.V(v)
        rp_d = mk.sV.inc()

        if kappa:
            def v2(v, rp_d=rp_d):
                v.wait_ge(sV.h, rp_d)
                ins = v.tensor_scalar(mk.WBUF[:, 0:2048], mk.SC[:, 0:2048],
                                      0.0, None, ALU.min)
                ins.then_inc(sV.h, 1)
            mk.V(v2)
            m_d = mk.sV.inc()

            def s(s, m_d=m_d):
                s.wait_ge(sV.h, m_d)
                ins = s.activation(mk.WBUF[:, 2048:4096], mk.WBUF[:, 0:2048],
                                   AF.Exp)
                ins.then_inc(sS.h, 1)
            mk.Sc(s)
            e_d = mk.sS.inc()

            def v3(v, tt=tt, off=off, e_d=e_d):
                v.wait_ge(sS.h, e_d)
                v.tensor_scalar(mk.SC[:, 0:2048], mk.SC[:, 0:2048], 0.0,
                                None, ALU.max)
                v.memset(mk.ROWS[:, 12:13], 0.0)
                ins = v.tensor_tensor(tt[:, off:off + 2048], mk.SC[:, 0:2048],
                                      mk.WBUF[:, 2048:4096], ALU.add)
                ins.then_inc(sV.h, 1)
            mk.V(v3)
            last_v = mk.sV.inc()
        else:
            last_v = rp_d
    return (sV, last_v)


def sub_l1attna(mk, W, last):
    wq = mk.add_weight("L1a_wq", W["wq_l"])
    wk = mk.add_weight("L1a_wk", W["wk_l"])
    wv = mk.add_weight("L1a_wv", W["wv_l"])
    wph = mk.add_weight("L1a_wphi", W["wphi_l"])
    wo = mk.add_weight("L1a_wo", W["wo_l"])
    cos_h = mk.add_weight("L1a_cos", W["cos64"])
    sin_h = mk.add_weight("L1a_sin", W["sin64"])
    sW, sX, sT, sV, sS, sG, sO = (mk.sW, mk.sX, mk.sT, mk.sV, mk.sS, mk.sG,
                                  mk.sO)
    ln = mk.emit_stats_center(rms2=False, r32=True)

    def gcs(g):
        g.wait_ge(sG.h, ln["centered"][1])
        g.dma_start(out=mk.BCF[:, 0:2048], in_=cos_h.ap()).then_inc(sW.h, 16)
        g.dma_start(out=mk.BCF[:, 2048:4096], in_=sin_h.ap()).then_inc(sW.h, 16)
    mk.G(gcs)
    cs_ld = sW.inc(32)

    def gwv(g):
        g.wait_ge(sT.h, mk.ln["stats_T"][1])
        g.dma_start(
            out=bass.AP(mk.SA, 0, [[32768, 128], [512, NK], [1, 512]]),
            in_=bass.AP(wv, 0, [[512, 128], [128 * 512, NK], [1, 512]]),
        ).then_inc(sW.h, 16)
    mk.G(gwv)
    wv_ld = sW.inc(16)
    vp = mk.emit_vprod(0, 512, 8192, extra_waits=[(sW, wv_ld)])

    kp = mk.emit_proj(wk, [0, 1, 2, 3],
                      lambda ci, tb: mk.SB[:, ci * 2048 + tb * 512:ci * 2048 + (tb + 1) * 512],
                      extra_waits=[vp], r32=True, wb_w=True)
    qp = mk.emit_proj(wq, list(range(16)),
                      lambda ci, tb: mk.SA[:, ci * 2048 + tb * 512:ci * 2048 + (tb + 1) * 512],
                      extra_waits=[vp], wb_w=True)

    kr = emit_rope_kappa(mk, [(mk.SB, j * 2048) for j in range(4)],
                         0, 2048, "rope_p64", True,
                         waits=[kp["ev"], qp["ev"], (sW, cs_ld)])
    qr = emit_rope_kappa(mk, [(mk.SA, j * 2048) for j in range(16)],
                         0, 2048, "rope_p64", True, waits=[kr])

    def vqg(v):
        v.wait_ge(qr[0].h, qr[1])
        for j in range(16):
            v.reduce_sum(mk.ROWS[:, 224 + j:225 + j],
                         mk.SA[:, j * 2048:(j + 1) * 2048], axis=AX.X)
        v.memset(mk.ROWS[:, 12:13], 0.0)
        ins = v.tensor_copy(mk.WBUF[:, 0:16], mk.ROWS[:, 224:240])
        ins.then_inc(sV.h, 1)
    mk.V(vqg)
    qg_d = sV.inc()

    def tsw(t):
        t.wait_ge(sV.h, qg_d)
        t.matmul(mk.PB1[:, 1536:1552], mk.CB["pswap"][:], mk.WBUF[:, 0:16],
                 start=True, stop=True).then_inc(sT.h, 1)
    mk.T(tsw)
    sw_mm = sT.inc()

    def vsw(v):
        v.wait_ge(sT.h, sw_mm)
        ins = v.tensor_copy(mk.WBUF[:, 16:32], mk.PB1[:, 1536:1552])
        ins.then_inc(sV.h, 1)
    mk.V(vsw)
    qg_d = sV.inc()

    lg_mm, lg_ev, lg_st = {}, {}, {}
    for n in range(32):
        j = n // 2
        pq = (n % 2) * 64
        kvt = n // 8
        pk = ((n // 4) % 2) * 64

        def t(t, n=n, j=j, pq=pq, kvt=kvt, pk=pk):
            if n == 0:
                t.wait_ge(sV.h, qg_d)
            if n >= 1:
                t.wait_ge(sV.h, lg_ev[n - 1])
            ins = None
            for b in range(4):
                qcol = j if pq == pk else 16 + j
                ins = t.matmul(mk.PB1[0:1, b * 512:(b + 1) * 512],
                               mk.WBUF[pk:pk + 64, qcol:qcol + 1],
                               mk.SB[pk:pk + 64, kvt * 2048 + b * 512:kvt * 2048 + (b + 1) * 512],
                               start=True, stop=True)
            ins.then_inc(sT.h, 1)
        mk.T(t)
        lg_mm[n] = sT.inc()

        def v(v, n=n):
            v.wait_ge(sT.h, lg_mm[n])
            if n >= 1:
                v.wait_ge(sX.h, lg_st[n - 1])
            ins = v.tensor_scalar(mk.BCF[0:1, 0:2048], mk.PB1[0:1, 0:2048],
                                  1.0 / 2048.0, None, ALU.mult)
            ins.then_inc(sV.h, 1)
        mk.V(v)
        lg_ev[n] = sV.inc()

        def y(y, n=n):
            y.wait_ge(sV.h, lg_ev[n])
            y.dma_start(out=bass.AP(mk.d_lgm, n * 2048, [[1, 1], [1, 2048]]),
                        in_=mk.BCF[0:1, 0:2048]).then_inc(sX.h, 16)
        mk.Y(y)
        lg_st[n] = sX.inc(16)

    def ylg(y):
        y.wait_ge(sX.h, lg_st[31])
        y.dma_start(out=mk.SC[0:32, 0:2048],
                    in_=bass.AP(mk.d_lgm, 0, [[2048, 32], [1, 2048]])
                    ).then_inc(sX.h, 16)
    mk.Y(ylg)
    lgm_ld = sX.inc(16)

    def vsm(v):
        v.wait_ge(sX.h, lgm_ld)
        ins = v.reduce_max(mk.SC[0:32, 2048:2049], mk.SC[0:32, 0:2048],
                           axis=AX.X, negate=True)
        ins.then_inc(sV.h, 1)
    mk.V(vsm)
    sm1 = sV.inc()

    def ssm(s):
        s.wait_ge(sV.h, sm1)
        ins = s.activation(mk.BCF[0:32, 0:2048], mk.SC[0:32, 0:2048], AF.Exp,
                           bias=mk.SC[0:32, 2048:2049],
                           accum_out=mk.SC[0:32, 2052:2053])
        ins.then_inc(sS.h, 1)
    mk.Sc(ssm)
    sm2 = sS.inc()

    def vsm2(v):
        v.wait_ge(sS.h, sm2)
        v.reciprocal(mk.SC[0:32, 2056:2057], mk.SC[0:32, 2052:2053])
        v.memset(mk.ROWS[:, 12:13], 0.0)
        v.tensor_scalar(mk.SC[0:32, 2060:2061], mk.SC[0:32, 2056:2057],
                        2048.0, None, ALU.mult)
        v.memset(mk.ROWS[:, 13:14], 0.0)
        inv_b = bass.AP(mk.SC, 2060, [[3072, 32], [0, 2048]])
        ins = v.tensor_tensor(mk.SC[0:32, 0:2048], mk.BCF[0:32, 0:2048],
                              inv_b, ALU.mult)
        ins.then_inc(sV.h, 1)
    mk.V(vsm2)
    al_d = sV.inc()

    ac_d = {}
    for sc in range(16):
        def t(t, sc=sc):
            t.wait_ge(sV.h, al_d)
            if sc >= 2:
                t.wait_ge(sV.h, ac_d[sc - 2])
            t.transpose(mk.PB1[:, (sc % 2) * 512:(sc % 2) * 512 + 32],
                        mk.SC[0:32, sc * 128:(sc + 1) * 128],
                        mk.CB["ident"][0:32, 0:32]).then_inc(sT.h, 1)
        mk.T(t)
        tr = sT.inc()

        def v(v, sc=sc, tr=tr):
            v.wait_ge(sT.h, tr)
            ins = v.tensor_copy(mk.BCF[:, sc * 32:(sc + 1) * 32],
                                mk.PB1[:, (sc % 2) * 512:(sc % 2) * 512 + 32])
            ins.then_inc(sV.h, 1)
        mk.V(v)
        ac_d[sc] = sV.inc()
    acol_done = (sV, ac_d[15])

    kkt_d = {}
    kkt_tr = {}
    for j in range(4):
        for c in range(16):
            idx = j * 16 + c

            def vc(v, j=j, c=c, idx=idx):
                if idx == 0:
                    v.wait_ge(acol_done[0].h, acol_done[1])
                if idx >= 2:
                    v.wait_ge(sT.h, kkt_tr[idx - 2])
                ins = v.tensor_copy(
                    mk.SC[:, 2048 + (idx % 2) * 128:2176 + (idx % 2) * 128],
                    mk.SB[:, j * 2048 + c * 128:j * 2048 + (c + 1) * 128])
                ins.then_inc(sV.h, 1)
            mk.V(vc)
            cp = sV.inc()

            def t(t, j=j, c=c, idx=idx, cp=cp):
                t.wait_ge(sV.h, cp)
                if idx >= 2:
                    t.wait_ge(sV.h, kkt_d[idx - 2])
                t.transpose(mk.PB0[:, (idx % 2) * 512:(idx % 2) * 512 + 128],
                            mk.SC[:, 2048 + (idx % 2) * 128:2176 + (idx % 2) * 128],
                            mk.CB["ident"][:]).then_inc(sT.h, 1)
            mk.T(t)
            kkt_tr[idx] = sT.inc()

            def v(v, j=j, c=c, idx=idx):
                v.wait_ge(sT.h, kkt_tr[idx])
                ins = v.tensor_copy(
                    mk.SB[:, j * 2048 + c * 128:j * 2048 + (c + 1) * 128],
                    mk.PB0[:, (idx % 2) * 512:(idx % 2) * 512 + 128])
                ins.then_inc(sV.h, 1)
            mk.V(v)
            kkt_d[idx] = sV.inc()
    kkt_done = (sV, kkt_d[63])

    res_done = {}
    sc_cnt = {}
    for n in range(32):
        j = n // 2
        kvh = n // 4
        kvt = kvh // 2
        kc = (kvh % 2) * 64
        vc = kvh * 64
        nbase = (n % 2) * 64
        mm_c = {}
        for c in range(16):
            def vsc(v, n=n, c=c, kvt=kvt, kc=kc):
                if n == 0 and c == 0:
                    v.wait_ge(kkt_done[0].h, kkt_done[1])
                if (n, c) > (0, 3):
                    pc = (n, c - 4) if c >= 4 else (n - 1, 12 + c)
                    v.wait_ge(sT.h, sc_cnt[pc])
                ins = v.scalar_tensor_tensor(
                    mk.WBUF[:, 2048 + (c % 4) * 64:2048 + (c % 4) * 64 + 64],
                    mk.SB[:, kvt * 2048 + c * 128 + kc:kvt * 2048 + c * 128 + kc + 64],
                    1.0,
                    bass.AP(mk.BCF, c * 32 + n, [[4096, 128], [0, 64]]),
                    ALU.mult, ALU.mult)
                ins.then_inc(sV.h, 1)
            mk.V(vsc)
            scv = sV.inc()

            def t(t, n=n, c=c, vc=vc, scv=scv, nbase=nbase):
                t.wait_ge(sV.h, scv)
                if c == 0 and n >= 2:
                    t.wait_ge(sV.h, res_done[n - 2])
                t.matmul(mk.PB1[nbase:nbase + 64, 1536 + (n % 2) * 64:1600 + (n % 2) * 64],
                         mk.WBUF[:, 2048 + (c % 4) * 64:2048 + (c % 4) * 64 + 64],
                         mk.SB[:, 8192 + c * 512 + vc:8192 + c * 512 + vc + 64],
                         start=(c == 0), stop=(c == 15),
                         tile_position=(0, nbase)).then_inc(sT.h, 1)
            mk.T(t)
            sc_cnt[(n, c)] = sT.inc()
        ou_mm = sc_cnt[(n, 15)]

        def vpo(v, n=n, ou_mm=ou_mm, nbase=nbase):
            v.wait_ge(sT.h, ou_mm)
            v.memset(mk.WBUF[:, (n % 2) * 64:(n % 2) * 64 + 64], 0.0)
            v.memset(mk.ROWS[:, 12:13], 0.0)
            ins = v.tensor_copy(
                mk.WBUF[nbase:nbase + 64, (n % 2) * 64:(n % 2) * 64 + 64],
                mk.PB1[nbase:nbase + 64, 1536 + (n % 2) * 64:1600 + (n % 2) * 64])
            ins.then_inc(sV.h, 1)
        mk.V(vpo)
        po_d = sV.inc()

        def t2(t, n=n, j=j, po_d=po_d, nbase=nbase):
            t.wait_ge(sV.h, po_d)
            if n >= 2:
                t.wait_ge(sV.h, res_done[n - 2])
            for b in range(4):
                ins = t.matmul(mk.PB0[nbase:nbase + 64, b * 512:(b + 1) * 512],
                               mk.WBUF[:, (n % 2) * 64:(n % 2) * 64 + 64],
                               mk.SA[:, j * 2048 + b * 512:j * 2048 + (b + 1) * 512],
                               start=True, stop=True,
                               tile_position=(0, nbase))
            ins.then_inc(sT.h, 1)
        mk.T(t2)
        rs_mm = sT.inc()

        if n % 2 == 0:
            res_done[n] = None
            rs_even = rs_mm
        else:
            def vrs(v, n=n, j=j, rs_mm=rs_mm, rs_even=rs_even):
                v.wait_ge(sT.h, rs_mm)
                ins = v.tensor_copy(mk.SA[:, j * 2048:(j + 1) * 2048],
                                    mk.PB0[:, 0:2048])
                ins.then_inc(sV.h, 1)
            mk.V(vrs)
            res_done[n] = sV.inc()
            res_done[n - 1] = res_done[n]
    res_all = (sV, res_done[31])

    ph_mm, ph_ev, ph_wl = {}, {}, {}
    wbw = mk.wbuf_tail
    for ci in range(16):
        par = ci % 2

        def g(g, ci=ci, par=par):
            if ci < 2:
                g.wait_ge(res_all[0].h, res_all[1])
            else:
                g.wait_ge(sT.h, ph_mm[ci - 2])
            g.dma_start(
                out=bass.AP(mk.WBUF, par * 2048, [[4096, 128], [1, 2048]]),
                in_=bass.AP(wph, ci * 2048, [[32768, 128], [1, 2048]]),
            ).then_inc(sW.h, 16)
        mk.G(g)
        ph_wl[ci] = sW.inc(16)

        def t(t, ci=ci, par=par):
            t.wait_ge(sW.h, ph_wl[ci])
            if ci == 0:
                t.wait_ge(res_all[0].h, res_all[1])
            if ci >= 2:
                t.wait_ge(sV.h, ph_ev[ci - 2])
            PB = mk.PB0 if par == 0 else mk.PB1
            ins = None
            for k in range(NK):
                for tb in range(4):
                    ins = t.matmul(PB[:, tb * 512:(tb + 1) * 512],
                                   mk.WBUF[:, par * 2048 + k * 128:par * 2048 + (k + 1) * 128],
                                   mk.xbf(k, tb * 512, 512),
                                   start=(k == 0), stop=(k == NK - 1))
            ins.then_inc(sT.h, 1)
        mk.T(t)
        ph_mm[ci] = sT.inc()

        def v(v, ci=ci, par=par):
            v.wait_ge(sT.h, ph_mm[ci])
            PB = mk.PB0 if par == 0 else mk.PB1
            for tb in range(4):
                v.scalar_tensor_tensor(
                    mk.SC[:, 2048 + (tb % 2) * 512:2048 + (tb % 2 + 1) * 512],
                    PB[:, tb * 512:(tb + 1) * 512], 1.0,
                    mk.BCR[:, tb * 512:(tb + 1) * 512], ALU.mult, ALU.mult)
                if tb >= 1:
                    v.tensor_tensor(
                        mk.SA[:, ci * 2048 + (tb - 1) * 512:ci * 2048 + tb * 512],
                        mk.SA[:, ci * 2048 + (tb - 1) * 512:ci * 2048 + tb * 512],
                        mk.SC[:, 2048 + ((tb - 1) % 2) * 512:2048 + ((tb - 1) % 2 + 1) * 512],
                        ALU.mult)
            v.memset(mk.ROWS[:, 12:13], 0.0)
            ins = v.tensor_tensor(
                mk.SA[:, ci * 2048 + 3 * 512:ci * 2048 + 4 * 512],
                mk.SA[:, ci * 2048 + 3 * 512:ci * 2048 + 4 * 512],
                mk.SC[:, 2048 + 512:2048 + 1024], ALU.mult)
            ins.then_inc(sV.h, 1)
        mk.V(v)
        ph_ev[ci] = sV.inc()
    mk.wbuf_tail = (sT, ph_mm[15])

    return mk.emit_out_proj_residual(wo, last, attn_wait=(sV, ph_ev[15]),
                                     wb_w=True)


def sub_l1ffn(mk, W, last):
    wa = mk.add_weight("L1f_A", W["A_l"])
    wb = mk.add_weight("L1f_Bw", W["Bw_l"])
    wg = mk.add_weight("L1f_Wg", W["Wg_l"])
    sW, sX, sT, sV, sS, sG, sO = (mk.sW, mk.sX, mk.sT, mk.sV, mk.sS, mk.sG,
                                  mk.sO)
    ln = mk.emit_stats_center(rms2=False)

    hp = mk.emit_proj(wa, [0, 1, 2, 3],
                      lambda ci, tb: mk.SB[:, ci * 2048 + tb * 512:ci * 2048 + (tb + 1) * 512],
                      wb_w=True)

    def gwg(g):
        g.dma_start(out=bass.AP(mk.SB, 10240, [[16384, 128], [8, NK], [1, 8]]),
                    in_=bass.AP(wg, 0, [[8, 128], [128 * 8, NK], [1, 8]]),
                    ).then_inc(sW.h, 16)
    mk.G(gwg)
    wg_ld = sW.inc(16)

    gt_mm = {}
    for tt in range(16):
        def t(t, tt=tt):
            t.wait_ge(sW.h, wg_ld)
            if tt == 0:
                t.wait_ge(sG.h, ln["centered"][1])
                t.wait_ge(hp["ev"][0].h, hp["ev"][1])
            ins = None
            for k in range(NK):
                ins = t.matmul(mk.PB1[:, 1536 + tt * 8:1544 + tt * 8],
                               mk.xbf(k, tt * 128, 128),
                               mk.SB[:, 10240 + k * 8:10240 + (k + 1) * 8],
                               start=(k == 0), stop=(k == NK - 1))
            ins.then_inc(sT.h, 1)
        mk.T(t)
        gt_mm[tt] = sT.inc()

    def vg1(v):
        v.wait_ge(sT.h, gt_mm[15])
        v.wait_ge(sV.h, ln["rcol"][1])
        rcol_b = bass.AP(mk.ROWS, 32, [[256, 128], [1, 16], [0, 8]])
        v.tensor_tensor(bass.AP(mk.SC, 0, [[3072, 128], [8, 16], [1, 8]]),
                        bass.AP(mk.PB1, 1536, [[2048, 128], [8, 16], [1, 8]]),
                        rcol_b, ALU.mult)
        v.memset(mk.ROWS[:, 12:13], 0.0)
        v.reduce_max(mk.SC[:, 128:144],
                     bass.AP(mk.SC, 0, [[3072, 128], [8, 16], [1, 8]]),
                     axis=AX.X, negate=True)
        v.memset(mk.ROWS[:, 13:14], 0.0)
        nm_b = bass.AP(mk.SC, 128, [[3072, 128], [1, 16], [0, 8]])
        ins = v.tensor_tensor(bass.AP(mk.SC, 256, [[3072, 128], [8, 16], [1, 8]]),
                              bass.AP(mk.SC, 0, [[3072, 128], [8, 16], [1, 8]]),
                              nm_b, ALU.add)
        ins.then_inc(sV.h, 1)
    mk.V(vg1)
    g1_d = sV.inc()

    def sg1(s):
        s.wait_ge(sV.h, g1_d)
        ins = s.activation(mk.SC[:, 384:512], mk.SC[:, 256:384], AF.Exp)
        ins.then_inc(sS.h, 1)
    mk.Sc(sg1)
    ge_d = sS.inc()

    def vg2(v):
        v.wait_ge(sS.h, ge_d)
        v.reduce_sum(mk.SC[:, 144:160],
                     bass.AP(mk.SC, 384, [[3072, 128], [8, 16], [1, 8]]),
                     axis=AX.X)
        v.tensor_scalar(mk.SC[:, 512:640], mk.SC[:, 256:384], 0.0,
                        None, ALU.is_ge)
        ins = v.reciprocal(mk.SC[:, 160:176], mk.SC[:, 144:160])
        ins.then_inc(sV.h, 1)
    mk.V(vg2)
    g2_d = sV.inc()

    gt_sel = {}
    for tt in range(16):
        def t2(t, tt=tt):
            t.wait_ge(sV.h, g2_d)
            if tt >= 1:
                t.wait_ge(sV.h, gt_sel[tt - 1])
            t.transpose(mk.PB0[0:8, 1024:1152], mk.SC[:, 512 + tt * 8:520 + tt * 8],
                        mk.CB["ident"][:]).then_inc(sT.h, 1)
        mk.T(t2)
        tr_d = sT.inc()

        def v3(v, tt=tt, tr_d=tr_d):
            v.wait_ge(sT.h, tr_d)
            ins = v.tensor_copy(mk.SB[0:8, 10496 + tt * 128:10496 + (tt + 1) * 128],
                                mk.PB0[0:8, 1024:1152])
            ins.then_inc(sV.h, 1)
        mk.V(v3)
        gt_sel[tt] = sV.inc()

    def t3(t):
        t.wait_ge(sV.h, gt_sel[15])
        t.transpose(mk.PB0[0:16, 1024:1152], mk.SC[:, 160:176],
                    mk.CB["ident"][:]).then_inc(sT.h, 1)
    mk.T(t3)
    pst = sT.inc()

    def v4(v):
        v.wait_ge(sT.h, pst)
        v.tensor_copy(mk.ROWS[0:16, 64:192], mk.PB0[0:16, 1024:1152]
                      ).then_inc(sV.h, 1)
    mk.V(v4)
    psev = sV.inc()
    wps = mk.row_to_dram(mk.ROWS, 64, BNC_AUX, (sV, psev), part16=True)
    ps_tile = mk.bcast_from_dram(BNC_AUX, mk.BCF, 0, 2048, wps)

    msk_mm, msk_d = {}, {}
    for c in range(4):
        for tb in range(4):
            def t4(t, c=c, tb=tb):
                if c == 0 and tb == 0:
                    t.wait_ge(hp["ev"][0].h, hp["ev"][1])
                t.wait_ge(sV.h, gt_sel[min(15, tb * 4 + 3)])
                idx = c * 4 + tb
                if idx >= 2:
                    pc = ((idx - 2) // 4, (idx - 2) % 4)
                    t.wait_ge(sV.h, msk_d[pc])
                t.matmul(mk.PB0[:, (tb % 2) * 512:(tb % 2) * 512 + 512],
                         mk.CB["epat"][:, c * 128:(c + 1) * 128],
                         mk.SB[0:8, 10496 + tb * 512:10496 + (tb + 1) * 512],
                         start=True, stop=True).then_inc(sT.h, 1)
            mk.T(t4)
            msk_mm[(c, tb)] = sT.inc()

            def v5(v, c=c, tb=tb):
                v.wait_ge(sT.h, msk_mm[(c, tb)])
                ins = v.tensor_tensor(
                    mk.SB[:, c * 2048 + tb * 512:c * 2048 + (tb + 1) * 512],
                    mk.SB[:, c * 2048 + tb * 512:c * 2048 + (tb + 1) * 512],
                    mk.PB0[:, (tb % 2) * 512:(tb % 2) * 512 + 512], ALU.mult)
                ins.then_inc(sV.h, 1)
            mk.V(v5)
            msk_d[(c, tb)] = sV.inc()

    st = mk.emit_out_proj_residual(
        wb, last, scale=4.0,
        src_chunk_fn=lambda k, tb: mk.SB[:, k * 2048 + tb * 512:k * 2048 + (tb + 1) * 512],
        n_in_chunks=4, attn_wait=(sV, msk_d[(3, 3)]),
        mult_tile=(mk.BCF, 0), wb_w=True)
    return st


SELCFG = True
SELCFG2 = True
NH_DBG = 16


def sub_l1attnb(mk, W, last, stage=99):
    wq = mk.add_weight("L1b_wq", W["wq_l"])
    wk = mk.add_weight("L1b_wk", W["wk_l"])
    wv = mk.add_weight("L1b_wv", W["wv_l"])
    wo = mk.add_weight("L1b_wo", W["wo_l"])
    cos_h = mk.add_weight("L1b_cos", W["cos128"])
    sin_h = mk.add_weight("L1b_sin", W["sin128"])
    onw_h = mk.add_weight("L1b_onw", W["onw"])
    sW, sX, sT, sV, sS, sG, sO = (mk.sW, mk.sX, mk.sT, mk.sV, mk.sS, mk.sG,
                                  mk.sO)
    ln = mk.emit_stats_center(rms2=False)
    VSP_off = 2048 * 2048

    def gcs(g):
        g.wait_ge(sG.h, ln["centered"][1])
        g.dma_start(out=mk.BCF[:, 0:2048], in_=cos_h.ap()).then_inc(sW.h, 16)
        g.dma_start(out=mk.BCF[:, 2048:4096], in_=sin_h.ap()).then_inc(sW.h, 16)
    mk.G(gcs)
    cs_ld = sW.inc(32)

    def yonw(y):
        y.wait_ge(sG.h, ln["centered"][1])
        y.dma_start(out=mk.LNC[:, 0:1], in_=onw_h.ap()).then_inc(sX.h, 16)
    mk.Y(yonw)
    onw_ld = sX.inc(16)

    def gwv(g):
        g.wait_ge(sT.h, mk.ln["stats_T"][1])
        for i in range(4):
            g.dma_start(
                out=bass.AP(mk.SA, i * 4 * 2048, [[32768, 128], [2048, 4], [1, 2048]]),
                in_=bass.AP(wv, i * 4 * 128 * 2048,
                            [[2048, 128], [128 * 2048, 4], [1, 2048]]),
            ).then_inc(sW.h, 16)
    mk.G(gwv)
    wv_ld = sW.inc(64)
    vp = mk.emit_vprod(0, 2048, 0, extra_waits=[(sW, wv_ld)],
                       spill=(mk.d_spill, VSP_off), w_tensor=mk.SA, stg_off=0)

    kp = mk.emit_proj(wk, list(range(16)),
                      lambda ci, tb: mk.SA[:, ci * 2048 + tb * 512:ci * 2048 + (tb + 1) * 512],
                      extra_waits=[vp], wb_w=True)
    kr = emit_rope_kappa(mk, [(mk.SA, j * 2048) for j in range(16)],
                         0, 2048, "rope_p128", False,
                         waits=[kp["ev"], (sW, cs_ld)])

    def vkm(v):
        v.wait_ge(kr[0].h, kr[1])
        for h in range(16):
            v.reduce_sum(mk.SC[:, 2048 + h * 8:2048 + h * 8 + 8],
                         bass.AP(mk.SA, h * 2048, [[32768, 128], [256, 8], [1, 256]]),
                         axis=AX.X)
        v.memset(mk.ROWS[:, 12:13], 0.0)
        ins = v.tensor_copy(mk.SB[:, 14336:14464], mk.SC[:, 2048:2176])
        ins.then_inc(sV.h, 1)
    mk.V(vkm)
    km_d = sV.inc()

    ksp_d = {}
    for h in range(16):
        def yk(y, h=h):
            if h == 0:
                y.wait_ge(kr[0].h, kr[1])
            y.dma_start(out=bass.AP(mk.d_spill, h * 128 * 2048,
                                    [[2048, 128], [1, 2048]]),
                        in_=mk.SA[:, h * 2048:(h + 1) * 2048]).then_inc(sO.h, 16)
        mk.Y(yk)
        ksp_d[h] = sO.inc(16)
    ksp = (sO, ksp_d[15])

    mk.wbuf_tail = kr
    qp = mk.emit_proj(wq, list(range(16)),
                      lambda ci, tb: mk.SA[:, ci * 2048 + tb * 512:ci * 2048 + (tb + 1) * 512],
                      scale=1.0 / math.sqrt(128.0),
                      extra_waits=[ksp, (sV, km_d)], wb_w=True)
    mk.dbg_qp = qp["ev"]
    qr = emit_rope_kappa(mk, [(mk.SA, j * 2048) for j in range(16)],
                         0, 2048, "rope_p128", False, waits=[qp["ev"]])
    mk.dbg_qr = qr

    def vfsm(v):
        v.wait_ge(sG.h, ln["centered"][1])
        v.memset(mk.SB[:, 14464:14592], 0.0)
        v.memset(mk.ROWS[:, 12:13], 0.0)
        ins = None
        for tc in range(16):
            bq = tc // 2
            if bq + 1 < 8:
                ins = v.memset(mk.SB[:, 14464 + tc * 8 + bq + 1:
                                     14464 + (tc + 1) * 8], NEG)
            ins = v.memset(mk.SB[:, 14464 + tc * 8 + bq:
                                 14464 + tc * 8 + bq + 1], -NEG)
        ins.then_inc(sV.h, 1)
    mk.V(vfsm)
    fsm_d = sV.inc()

    g_mm, g_ev = {}, {}
    for h in range(16):
        def tg(t, h=h):
            if h == 0:
                t.wait_ge(qr[0].h, qr[1])
            if h >= 2:
                t.wait_ge(sV.h, g_ev[h - 2])
            ins = None
            for tc in range(16):
                ins = t.matmul(
                    mk.PB0[:, (h % 2) * 512 + tc * 8:(h % 2) * 512 + tc * 8 + 8],
                    mk.SA[:, h * 2048 + tc * 128:h * 2048 + (tc + 1) * 128],
                    mk.SB[:, 14336 + h * 8:14336 + h * 8 + 8],
                    start=True, stop=True)
            ins.then_inc(sT.h, 1)
        mk.T(tg)
        g_mm[h] = sT.inc()

        def vg(v, h=h):
            v.wait_ge(sT.h, g_mm[h])
            if h == 0:
                v.wait_ge(sV.h, fsm_d)
            ins = v.scalar_tensor_tensor(
                bass.AP(mk.SC, h * 8, [[3072, 128], [128, 16], [1, 8]]),
                mk.PB0[:, (h % 2) * 512:(h % 2) * 512 + 128], 1.0,
                mk.SB[:, 14464:14592], ALU.mult, ALU.add)
            ins.then_inc(sV.h, 1)
        mk.V(vg)
        g_ev[h] = sV.inc()

    def vsel(v):
        v.wait_ge(sV.h, g_ev[15])
        v.tensor_copy(mk.BCF[:, 0:2048], mk.SC[:, 0:2048])
        v.memset(mk.ROWS[:, 12:13], 0.0)
        for it in range(4):
            v.reduce_max(mk.BCF[:, 2048 + it * 256:2048 + (it + 1) * 256],
                         bass.AP(mk.BCF, 0, [[4096, 128], [8, 256], [1, 8]]),
                         axis=AX.X)
            v.memset(mk.ROWS[:, 12:13], 0.0)
            if it < 3:
                v.tensor_tensor(mk.WBUF[:, 0:2048], mk.BCF[:, 0:2048],
                                bass.AP(mk.BCF, 2048 + it * 256,
                                        [[4096, 128], [1, 256], [0, 8]]),
                                ALU.is_equal)
                v.memset(mk.ROWS[:, 13:14], 0.0)
                v.scalar_tensor_tensor(mk.BCF[:, 0:2048], mk.WBUF[:, 0:2048],
                                       -1e32, mk.BCF[:, 0:2048],
                                       ALU.mult, ALU.add)
                v.memset(mk.ROWS[:, 12:13], 0.0)
        v.tensor_tensor(mk.BCF[:, 0:2048], mk.SC[:, 0:2048],
                        bass.AP(mk.BCF, 2048 + 3 * 256,
                                [[4096, 128], [1, 256], [0, 8]]),
                        ALU.is_ge)
        v.memset(mk.ROWS[:, 13:14], 0.0)
        v.tensor_scalar(mk.WBUF[:, 0:2048], mk.SC[:, 0:2048], -5e29,
                        None, ALU.is_gt)
        v.memset(mk.ROWS[:, 12:13], 0.0)
        v.tensor_tensor(mk.SC[:, 0:2048], mk.BCF[:, 0:2048],
                        mk.WBUF[:, 0:2048], ALU.mult)
        v.memset(mk.ROWS[:, 13:14], 0.0)
        ins = v.tensor_scalar(mk.SB[:, 10240:12288], mk.SC[:, 0:2048],
                              1e15, -1e15, ALU.mult, ALU.add)
        ins.then_inc(sV.h, 1)
    mk.V(vsel)
    sel_d = sV.inc()
    mk.dbg_sel = sel_d
    if stage <= 1:
        return mk.emit_out_proj_residual(wo, last, attn_wait=(sV, sel_d))

    qv_ld = {}

    def emit_kv_load(hh, waits):
        def y0(y, hh=hh, waits=waits):
            for sm, val in waits:
                y.wait_ge(sm.h, val)
            y.dma_start(
                out=mk.SB[:, (hh % 2) * 2048:(hh % 2) * 2048 + 2048],
                in_=bass.AP(mk.d_spill, hh * 128 * 2048,
                            [[2048, 128], [1, 2048]])).then_inc(sX.h, 16)
            y.dma_start(
                out=mk.SB[:, 4096 + (hh % 2) * 2048:4096 + (hh % 2) * 2048 + 2048],
                in_=bass.AP(mk.d_spill, VSP_off + hh * 128,
                            [[2048, 128], [128 * 2048, 16], [1, 128]])
            ).then_inc(sX.h, 16)
        mk.Y(y0)
        qv_ld[hh] = sX.inc(32)

    def emit_selt8(h, prev_tail):
        tr_d, ev_d = {}, {}
        for tc in range(16):
            def t5(t, h=h, tc=tc, prev_tail=prev_tail):
                if tc == 0:
                    t.wait_ge(sV.h, sel_d)
                    if prev_tail is not None:
                        t.wait_ge(prev_tail[0].h, prev_tail[1])
                if tc >= 2:
                    t.wait_ge(sV.h, ev_d[tc - 2])
                t.matmul(mk.PB0[0:8, 1024 + (tc % 2) * 512:1024 + (tc % 2) * 512 + 128],
                         mk.SC[:, tc * 128 + h * 8:tc * 128 + h * 8 + 8],
                         mk.CB["ident"][:],
                         start=True, stop=True).then_inc(sT.h, 1)
            mk.T(t5)
            tr_d[tc] = sT.inc()

            def v5(v, h=h, tc=tc):
                v.wait_ge(sT.h, tr_d[tc])
                if tc == 15:
                    v.wait_ge(sX.h, qv_ld[h])
                ins = v.tensor_copy(
                    mk.SB[0:8, 8192 + tc * 128:8192 + (tc + 1) * 128],
                    mk.PB0[0:8, 1024 + (tc % 2) * 512:1024 + (tc % 2) * 512 + 128])
                ins.then_inc(sV.h, 1)
            mk.V(v5)
            ev_d[tc] = sV.inc()
        return (sV, ev_d[15])

    def head_pre(h, prev_tail):
        if h == 0:
            emit_kv_load(0, [ksp, vp, (sV, sel_d)])
        if h + 1 < 16:
            emit_kv_load(h + 1, [prev_tail] if prev_tail is not None
                         else [ksp, vp, (sV, sel_d)])
        if SELCFG:
            return emit_selt8(h, prev_tail)
        return (sX, qv_ld[h])

    att = mk.emit_attn_core(
        nheads=NH_DBG,
        kt_tensor=mk.SB, kt_off_fn=lambda h: (h % 2) * 2048,
        v_lhsT_fn=lambda h, kt: mk.SB[:, 4096 + (h % 2) * 2048 + kt * 128:
                                      4096 + (h % 2) * 2048 + (kt + 1) * 128],
        qkv_wait=(sV, sel_d), head_pre=head_pre, sel_cfg=(SELCFG and SELCFG2))
    if stage <= 2:
        return mk.emit_out_proj_residual(wo, last, attn_wait=att)

    on_d = {}
    for h in range(16):
        def s5(s, h=h):
            if h == 0:
                s.wait_ge(att[0].h, att[1])
            else:
                s.wait_ge(sV.h, on_d[h - 1])
            ins = s.activation(mk.WBUF[:, (h % 2) * 2048:(h % 2) * 2048 + 2048],
                               mk.SA[:, h * 2048:(h + 1) * 2048], AF.Square)
            ins.then_inc(sS.h, 1)
        mk.Sc(s5)
        sq_d = sS.inc()

        def t6(t, h=h, sq_d=sq_d):
            t.wait_ge(sS.h, sq_d)
            ins = None
            for b in range(4):
                ins = t.matmul(mk.PB1[0:1, b * 512:(b + 1) * 512],
                               mk.CB["ones_col"][:],
                               mk.WBUF[:, (h % 2) * 2048 + b * 512:
                                       (h % 2) * 2048 + (b + 1) * 512],
                               start=True, stop=True)
            ins.then_inc(sT.h, 1)
        mk.T(t6)
        mmo = sT.inc()

        def v7(v, h=h, mmo=mmo):
            v.wait_ge(sT.h, mmo)
            ins = v.tensor_scalar(mk.SC[0:1, 0:2048], mk.PB1[0:1, 0:2048],
                                  1.0 / 128.0, None, ALU.mult)
            ins.then_inc(sV.h, 1)
        mk.V(v7)
        mean_d = sV.inc()

        def s6(s, h=h, mean_d=mean_d):
            s.wait_ge(sV.h, mean_d)
            s.activation(mk.BCF[0:1, (h % 2) * 2048:(h % 2) * 2048 + 2048],
                         mk.SC[0:1, 0:2048], AF.Ln,
                         bias=mk.CB["eps6"][0:1, 0:1])
            ins = s.activation(mk.SC[0:1, 0:2048],
                               mk.BCF[0:1, (h % 2) * 2048:(h % 2) * 2048 + 2048],
                               AF.Exp, scale=-0.5)
            ins.then_inc(sS.h, 1)
        mk.Sc(s6)
        rso_d = sS.inc()

        w_on = mk.row_to_dram(mk.SC, 0, 6 + h % 2, (sS, rso_d))
        on_tile = mk.bcast_from_dram(6 + h % 2, mk.BCF, (h % 2) * 2048, 2048,
                                     w_on)

        def v8(v, h=h, on_tile=on_tile):
            v.wait_ge(sX.h, on_tile[1])
            v.wait_ge(sX.h, onw_ld)
            v.tensor_scalar(mk.BCF[:, (h % 2) * 2048:(h % 2) * 2048 + 2048],
                            mk.BCF[:, (h % 2) * 2048:(h % 2) * 2048 + 2048],
                            mk.LNC[:, 0:1], None, ALU.mult)
            v.memset(mk.ROWS[:, 12:13], 0.0)
            ins = v.scalar_tensor_tensor(
                mk.SA[:, h * 2048:(h + 1) * 2048],
                mk.SA[:, h * 2048:(h + 1) * 2048], 1.0,
                mk.BCF[:, (h % 2) * 2048:(h % 2) * 2048 + 2048],
                ALU.mult, ALU.mult)
            ins.then_inc(sV.h, 1)
        mk.V(v8)
        on_d[h] = sV.inc()

    return mk.emit_out_proj_residual(wo, last, attn_wait=(sV, on_d[15]),
                                     wb_w=True)


def build_ffn_only(weights_np):
    mk = MK()
    sub_ffn(mk, weights_np, last=True)
    mk.finish()
    return mk


def build_attn_b_only(weights_np):
    mk = MK()
    sub_attn_b(mk, weights_np, last=True)
    mk.finish()
    return mk


def _wb(w, n_in=16):
    w = np.asarray(w, dtype=np.float32)
    r, c = w.shape
    assert r == n_in * 128
    return np.ascontiguousarray(
        w.reshape(n_in, 128, c // 128, 128).transpose(1, 2, 0, 3)
         .reshape(128, (c // 128) * n_in * 128))


def kernel(**inputs):
    T = lambda a: np.ascontiguousarray(np.asarray(a, dtype=np.float32).T)
    A_ = lambda a: np.ascontiguousarray(np.asarray(a, dtype=np.float32))
    x = np.asarray(inputs["L0_attn_a_x"], dtype=np.float32)

    Wa = dict(wqkv_l=T(inputs["L0_attn_a_w_qkv"]),
              wqkv_wb=_wb(T(inputs["L0_attn_a_w_qkv"])),
              wout_wb=_wb(T(inputs["L0_attn_a_w_out"])))
    Wb = dict(w_q_wb=_wb(A_(inputs["L0_attn_b_w_q"])),
              w_kv=A_(inputs["L0_attn_b_w_kv"]),
              w_kv_wb=_wb(A_(inputs["L0_attn_b_w_kv"])),
              w_dense_wb=_wb(A_(inputs["L0_attn_b_w_dense"])))
    ln1 = np.asarray(inputs["L0_ffn_ln1_w"], dtype=np.float32)
    ln2 = np.asarray(inputs["L0_ffn_ln2_w"], dtype=np.float32)
    wdn_t = T(inputs["L0_ffn_W_down"])
    wdn3 = np.ascontiguousarray(
        wdn_t.reshape(4, 16, 128, 16, 128).transpose(2, 3, 0, 1, 4)
             .reshape(128, 131072))
    Wf = dict(wq_l=_wb(T(inputs["L0_ffn_Wq"])), wk_l=_wb(T(inputs["L0_ffn_Wk"])),
              wv_l=T(inputs["L0_ffn_Wv"]), wo_raw=_wb(A_(inputs["L0_ffn_Wo"])),
              A=A_(inputs["L0_ffn_A"]), B=A_(inputs["L0_ffn_B"]),
              win_l=_wb(T(inputs["L0_ffn_W_in_base"])),
              wup_l=_wb(T(inputs["L0_ffn_W_up_base"])),
              wdn_l=wdn3,
              ln1c=np.ascontiguousarray(ln1.reshape(16, 128).T),
              ln2c=np.ascontiguousarray(ln2.reshape(16, 128).T))
    s = np.arange(2048)
    inv64 = 1.0 / 10000.0 ** (np.arange(0, 64, 2) / 64.0)
    fr64 = s[None, :] * inv64[(np.arange(128) % 32)][:, None]
    W1a = dict(wq_l=_wb(A_(inputs["L1_attn_a_Wq"])),
               wk_l=_wb(A_(inputs["L1_attn_a_Wk"])),
               wv_l=A_(inputs["L1_attn_a_Wv"]),
               wphi_l=_wb(A_(inputs["L1_attn_a_Wphi"])),
               wo_l=_wb(A_(inputs["L1_attn_a_Wo"])),
               cos64=np.cos(fr64).astype(np.float32),
               sin64=np.sin(fr64).astype(np.float32))
    W1f = dict(
        A_l=_wb(np.asarray(inputs["L1_ffn_A"], dtype=np.float32
                           ).transpose(2, 0, 1).reshape(2048, 512)),
        Bw_l=_wb(np.asarray(inputs["L1_ffn_Bw"], dtype=np.float32
                            ).transpose(0, 2, 1).reshape(512, 2048), n_in=4),
        Wg_l=T(inputs["L1_ffn_Wg"]))
    inv128 = 1.0 / 10000.0 ** (np.arange(0, 128, 2) / 128.0)
    fr128 = s[None, :] * inv128[(np.arange(128) % 64)][:, None]
    onw = np.asarray(inputs["L1_attn_b_o_norm_w"], dtype=np.float32).reshape(128, 1)
    W1b = dict(wq_l=_wb(T(inputs["L1_attn_b_Wq"])),
               wk_l=_wb(T(inputs["L1_attn_b_Wk"])),
               wv_l=T(inputs["L1_attn_b_Wv"]),
               wo_l=_wb(T(inputs["L1_attn_b_Wo"])),
               cos128=np.cos(fr128).astype(np.float32),
               sin128=np.sin(fr128).astype(np.float32),
               onw=np.ascontiguousarray(onw))

    mk = MK()
    sub_attn_a(mk, Wa, last=False)
    sub_ffn(mk, Wf, last=False)
    sub_attn_b(mk, Wb, last=False)
    sub_ffn(mk, Wf, last=False)
    sub_l1attna(mk, W1a, last=False)
    sub_l1ffn(mk, W1f, last=False)
    sub_l1attnb(mk, W1b, last=False)
    sub_l1ffn(mk, W1f, last=True)
    mk.finish()

    in_maps = []
    for b in range(8):
        ins = dict(mk.inputs)
        ins["x_in"] = np.ascontiguousarray(x[b].T)
        in_maps.append(ins)
    res = run_bass_kernel_spmd(mk.nc, in_maps, core_ids=list(range(8)))
    out = np.stack([np.ascontiguousarray(res.results[b]["x_out"].T)
                    for b in range(8)], axis=0)
    return out.astype(np.float32)


# version 07
# speedup vs baseline: 1.0626x; 1.0068x over previous
import math
from contextlib import ExitStack

import numpy as np
import ml_dtypes
import concourse.bass as bass
import concourse.mybir as mybir
from concourse.bass_utils import run_bass_kernel_spmd

F32 = mybir.dt.float32
BF16 = mybir.dt.bfloat16
AF = mybir.ActivationFunctionType
ALU = mybir.AluOpType
AX = mybir.AxisListType
BF = ml_dtypes.bfloat16

D = 2048
S = 2048
NK = D // 128
NEG = -1e30


def host_constants():
    c = {}
    i = np.arange(128)
    c["ident"] = np.eye(128, dtype=np.float32)
    c["ones_col"] = np.ones((128, 1), dtype=np.float32)
    c["ones_row_b"] = np.ones((1, 128), dtype=np.float32)
    c["mask_qdiag"] = np.where(i[:, None] >= i[None, :], 0.0, NEG).astype(np.float32)
    c["mask_qlow"] = np.where(i[None, :] >= i[:, None], 0.0, NEG).astype(np.float32)
    j = np.arange(128)[:, None]
    u = np.arange(-384, 512)[None, :]
    c["bin_kdiag_m"] = (u >= j).astype(np.float32)
    c["bin_klow_m"] = (u <= j).astype(np.float32)
    p64 = np.zeros((128, 128), dtype=np.float32)
    for r in range(128):
        b = (r // 64) * 64
        o = r - b
        if o < 32:
            p64[r, b + o + 32] = -1.0
        else:
            p64[r, b + o - 32] = 1.0
    c["rope_p64"] = np.ascontiguousarray(p64.T)
    psw = np.zeros((128, 128), dtype=np.float32)
    for r in range(128):
        psw[r, r ^ 64] = 1.0
    c["pswap"] = psw
    p128 = np.zeros((128, 128), dtype=np.float32)
    for r in range(128):
        if r < 64:
            p128[r, r + 64] = -1.0
        else:
            p128[r, r - 64] = 1.0
    c["rope_p128"] = np.ascontiguousarray(p128.T)
    ep = np.zeros((8, 512), dtype=np.float32)
    for e in range(8):
        ep[e, e * 64:(e + 1) * 64] = 1.0
    c["epat"] = ep
    c["eps5"] = np.full((128, 1), 1e-5, dtype=np.float32)
    c["eps6"] = np.full((128, 1), 1e-6, dtype=np.float32)
    return c


CONST_DT = {"ident": F32, "ones_col": BF16, "ones_row_b": BF16,
            "mask_qdiag": BF16, "mask_qlow": BF16,
            "eps5": F32, "eps6": F32,
            "bin_kdiag_m": BF16, "bin_klow_m": BF16, "epat": BF16,
            "rope_p64": BF16, "rope_p128": BF16, "pswap": BF16}

BNC_MU, BNC_R, BNC_RLN, BNC_AUX, BNC_C0, BNC_C1 = 0, 1, 2, 3, 4, 5
BNC_N = 8


class Sem:
    def __init__(self, h):
        self.h = h
        self.n = 0

    def inc(self, k=1):
        self.n += k
        return self.n


class MK:
    def __init__(self):
        self.nc = nc = bass.Bass("TRN2", target_bir_lowering=False)
        self.es = ExitStack()
        self.inputs = {}
        self.consts = host_constants()

        self.d_xin = nc.dram_tensor("x_in", [D, S], F32, kind="ExternalInput")
        self.d_xout = nc.dram_tensor("x_out", [D, S], F32, kind="ExternalOutput")
        self.d_xres = nc.dram_tensor("x_res", [D, S], F32)
        self.d_bnc = nc.dram_tensor("bnc", [BNC_N, 2048], F32)
        self.d_spill = nc.dram_tensor("spill", [2 * 2048 * 2048], BF16)
        self.d_spill2 = nc.dram_tensor("spill2", [8192 * 2048], BF16)
        self.d_lgm = nc.dram_tensor("lgm", [32 * 2048], F32)
        self.weight_tensors = {}

        self.c_handles = {}
        for k, v in self.consts.items():
            self.c_handles[k] = nc.dram_tensor("c_" + k, list(v.shape), F32,
                                               kind="ExternalInput")
            self.inputs["c_" + k] = np.ascontiguousarray(v)

        sb = lambda n, sh, dt: self.es.enter_context(nc.sbuf_tensor(n, sh, dt))
        ps = lambda n, sh: self.es.enter_context(nc.psum_tensor(n, sh, F32))
        self.XBF = sb("XBF", [128, NK * S], BF16)
        self.SA = sb("SA", [128, 32768], BF16)
        self.SB = sb("SBX", [128, 16384], BF16)
        self.SC = sb("SC", [128, 3072], F32)
        self.WBUF = sb("WBUF", [128, 4096], BF16)
        self.BCR = sb("BCR", [128, 2048], BF16)
        self.BCF = sb("BCF", [128, 4096], F32)
        self.ROWS = sb("ROWS", [128, 256], F32)
        self.LNC = sb("LNC", [128, 32], F32)
        self.CB = {}
        for k, v in self.consts.items():
            self.CB[k] = sb("cb_" + k, list(v.shape), CONST_DT[k])

        self.PB0 = ps("PB0", [128, 2048])
        self.PB1 = ps("PB1", [128, 2048])

        mksem = lambda n: Sem(self.es.enter_context(nc.semaphore(n)))
        self.sW = mksem("sW")
        self.sX = mksem("sX")
        self.sT = mksem("sT")
        self.sV = mksem("sV")
        self.sS = mksem("sS")
        self.sG = mksem("sG")
        self.sO = mksem("sO")

        self.block = self.es.enter_context(nc.Block())
        self.wbuf_tail = (self.sT, 0)
        self._prologue()

    def T(self, fn):
        self.block.tensor(fn)

    def V(self, fn):
        self.block.vector(fn)

    def Sc(self, fn):
        self.block.scalar(fn)

    def G(self, fn):
        self.block.gpsimd(fn)

    def Y(self, fn):
        self.block.sync(fn)

    def xbf(self, k, c0=0, w=S):
        return self.XBF[:, k * S + c0:k * S + c0 + w]

    def add_weight(self, name, arr):
        if name not in self.weight_tensors:
            h = self.nc.dram_tensor(name, list(arr.shape), F32,
                                    kind="ExternalInput")
            self.weight_tensors[name] = h
            self.inputs[name] = np.ascontiguousarray(arr, dtype=np.float32)
        return self.weight_tensors[name]

    def _prologue(self):
        mk, sW, sX = self, self.sW, self.sX

        def g(g):
            for k, h in mk.c_handles.items():
                g.dma_start(out=mk.CB[k][:], in_=h.ap()).then_inc(sW.h, 16)
            for i in range(4):
                g.dma_start(
                    out=bass.AP(mk.XBF, i * 4 * S, [[NK * S, 128], [S, 4], [1, S]]),
                    in_=bass.AP(mk.d_xin, i * 4 * 128 * S,
                                [[S, 128], [128 * S, 4], [1, S]]),
                ).then_inc(sW.h, 16)
        self.G(g)
        sW.inc(16 * len(mk.c_handles) + 64)

        def y(y):
            for i in range(4):
                y.dma_start(
                    out=bass.AP(mk.d_xres, i * 4 * 128 * S,
                                [[S, 128], [128 * S, 4], [1, S]]),
                    in_=bass.AP(mk.d_xin, i * 4 * 128 * S,
                                [[S, 128], [128 * S, 4], [1, S]]),
                ).then_inc(sX.h, 16)
        self.Y(y)
        sX.inc(64)
        self.xbf_ready = [(sW, sW.n), (sX, sX.n)]
        self.consts_ready = sW.n

    def row_to_dram(self, src_t, src_off, slot, wait, width=2048, part16=False):
        mk, sX = self, self.sX

        def y(y):
            y.wait_ge(wait[0].h, wait[1])
            if part16:
                y.dma_start(out=bass.AP(mk.d_bnc, slot * 2048, [[128, 16], [1, 128]]),
                            in_=src_t[0:16, src_off:src_off + 128]).then_inc(sX.h, 16)
            else:
                y.dma_start(out=bass.AP(mk.d_bnc, slot * 2048, [[1, 1], [1, width]]),
                            in_=src_t[0:1, src_off:src_off + width]).then_inc(sX.h, 16)
        self.Y(y)
        return (sX, sX.inc(16))

    def bcast_from_dram(self, slot, dst_t, dst_off, width, wait, use_g=False,
                        extra_wait=None):
        mk = self
        sem = self.sW if use_g else self.sX

        def f(e):
            e.wait_ge(wait[0].h, wait[1])
            if extra_wait is not None:
                e.wait_ge(extra_wait[0].h, extra_wait[1])
            e.dma_start(out=dst_t[:, dst_off:dst_off + width],
                        in_=bass.AP(mk.d_bnc, slot * 2048, [[0, 128], [1, width]])
                        ).then_inc(sem.h, 16)
        (self.G if use_g else self.Y)(f)
        return (sem, sem.inc(16))

    def emit_stats_center(self, rms2=False, r32=False):
        mk = self
        sT, sV, sS, sG, sX = self.sT, self.sV, self.sS, self.sG, self.sX
        BCF, SC = self.BCF, self.SC

        def t0(t):
            for sm, val in self.xbf_ready:
                t.wait_ge(sm.h, val)
            t.wait_ge(mk.sW.h, self.consts_ready)
            ins = None
            for k in range(NK):
                for b in range(4):
                    ins = t.matmul(mk.PB0[0:1, b * 512:(b + 1) * 512],
                                   mk.CB["ones_col"][:], mk.xbf(k, b * 512, 512),
                                   start=(k == 0), stop=(k == NK - 1))
            ins.then_inc(sT.h, 1)
        self.T(t0)
        sum_done = sT.inc()

        sq_s = {}
        sq_t = {}
        for kp in range(1, NK, 2):
            def s(s, kp=kp):
                if kp < 2:
                    for sm, val in self.xbf_ready:
                        s.wait_ge(sm.h, val)
                else:
                    s.wait_ge(sT.h, sq_t[kp - 2])
                s.activation(mk.SA[:, 0:S], mk.xbf(kp - 1), AF.Square)
                s.activation(mk.SA[:, 2048:2048 + S], mk.xbf(kp),
                             AF.Square).then_inc(sS.h, 1)
            self.Sc(s)
            sq_s[kp - 1] = sq_s[kp] = sS.inc()

            for k in (kp - 1, kp):
                buf = (k % 2) * 2048

                def t(t, k=k, buf=buf):
                    t.wait_ge(sS.h, sq_s[k])
                    ins = None
                    for b in range(4):
                        ins = t.matmul(mk.PB1[0:1, b * 512:(b + 1) * 512],
                                       mk.CB["ones_col"][:],
                                       mk.SA[:, buf + b * 512:buf + (b + 1) * 512],
                                       start=(k == 0), stop=(k == NK - 1))
                    ins.then_inc(sT.h, 1)
                self.T(t)
                sq_t[k] = sT.inc()
        mm_done = sq_t[NK - 1]

        def v(v):
            v.wait_ge(sT.h, mm_done)
            ins = v.tensor_scalar(BCF[0:1, 0:2048], mk.PB0[0:1, 0:2048],
                                  1.0 / D, None, ALU.mult)
            ins.then_inc(sV.h, 1)
        self.V(v)
        mu_done = sV.inc()
        w_mu = self.row_to_dram(BCF, 0, BNC_MU, (sV, mu_done))

        def v2(v):
            v.wait_ge(sX.h, w_mu[1])
            v.tensor_tensor(SC[0:1, 0:2048], BCF[0:1, 0:2048],
                            BCF[0:1, 0:2048], ALU.mult)
            v.memset(mk.ROWS[:, 12:13], 0.0)
            ins = v.scalar_tensor_tensor(BCF[0:1, 0:2048], mk.PB1[0:1, 0:2048],
                                         1.0 / D, SC[0:1, 0:2048],
                                         ALU.mult, ALU.subtract)
            ins.then_inc(sV.h, 1)
        self.V(v2)
        var_done = sV.inc()

        def s2(s, rms2=rms2):
            s.wait_ge(sV.h, var_done)
            s.activation(SC[0:1, 0:2048], BCF[0:1, 0:2048], AF.Ln, bias=mk.CB["eps5"][0:1, 0:1])
            ins = s.activation(BCF[0:1, 2048:4096], SC[0:1, 0:2048],
                               AF.Exp, scale=-0.5)
            if rms2:
                ins = s.activation(BCF[0:1, 0:2048], SC[0:1, 0:2048],
                                   AF.Exp, scale=-1.0)
            ins.then_inc(sS.h, 1)
        self.Sc(s2)
        rln_done = sS.inc()

        if not rms2:
            r_final = (sS, rln_done)
        else:
            w_rln = self.row_to_dram(BCF, 2048, BNC_RLN, (sS, rln_done))
            e2_done = rln_done

            def v3(v):
                v.wait_ge(sS.h, e2_done)
                ins = v.tensor_scalar(BCF[0:1, 0:2048], BCF[0:1, 0:2048],
                                      -1e-5, 1.0, ALU.mult, ALU.add)
                ins.then_inc(sV.h, 1)
            self.V(v3)
            u_done = sV.inc()

            def s4(s):
                s.wait_ge(sV.h, u_done)
                s.activation(SC[0:1, 0:2048], BCF[0:1, 0:2048], AF.Ln, bias=mk.CB["eps6"][0:1, 0:1])
                ins = s.activation(BCF[0:1, 0:2048], SC[0:1, 0:2048],
                                   AF.Exp, scale=-0.5)
                ins.then_inc(sS.h, 1)
            self.Sc(s4)
            rms_done = sS.inc()

            def v4(v):
                v.wait_ge(sS.h, rms_done)
                v.wait_ge(sX.h, w_rln[1])
                ins = v.tensor_tensor(BCF[0:1, 2048:4096], BCF[0:1, 2048:4096],
                                      BCF[0:1, 0:2048], ALU.mult)
                ins.then_inc(sV.h, 1)
            self.V(v4)
            r_final = (sV, sV.inc())

        w_r = self.row_to_dram(BCF, 2048, BNC_R, r_final)
        r_tile = self.bcast_from_dram(BNC_R, self.BCR, 0, 2048, w_r, use_g=True)
        mu_tile = self.bcast_from_dram(BNC_MU, self.BCF, 0, 2048, w_r)
        r32_tile = None
        if r32:
            r32_tile = self.bcast_from_dram(BNC_R, self.SC, 0, 2048, w_r)

        rc_last = None
        for jj in range(16):
            def t2(t, jj=jj, rc_last=rc_last):
                if jj == 0:
                    t.wait_ge(r_final[0].h, r_final[1])
                else:
                    t.wait_ge(sV.h, rc_last)
                t.transpose(mk.PB1[:, 1536:1537],
                            BCF[0:1, 2048 + jj * 128:2048 + (jj + 1) * 128],
                            mk.CB["ident"][0:1, 0:1]).then_inc(sT.h, 1)
            self.T(t2)
            td = sT.inc()

            def v5(v, jj=jj, td=td):
                v.wait_ge(sT.h, td)
                v.tensor_copy(mk.ROWS[:, 32 + jj:33 + jj],
                              mk.PB1[:, 1536:1537]).then_inc(sV.h, 1)
            self.V(v5)
            rc_last = sV.inc()
        rcol = (sV, rc_last)

        def gch(g):
            g.wait_ge(sX.h, mu_tile[1])
            ins = None
            for k in range(NK):
                ins = g.tensor_tensor(mk.xbf(k), mk.xbf(k),
                                      BCF[:, 0:2048], ALU.subtract)
            ins.then_inc(self.sG.h, 1)
        self.G(gch)
        centered = (self.sG, self.sG.inc())

        self.ln = dict(r_tile=r_tile, rcol=rcol, centered=centered,
                       stats_T=(sT, mm_done), r32_tile=r32_tile)
        return self.ln

    def emit_proj(self, wh, cols, dest_fn=None, scale=1.0, use_rtile=True,
                  extra_waits=(), n_in_chunks=NK, src_fn=None,
                  spill=None, stg_off=4096, r32=False, wb_w=False):
        mk = self
        sW, sT, sV, sO = self.sW, self.sT, self.sV, self.sO
        sp_d = {}
        if spill is not None:
            dest_fn = lambda ci, tb: mk.SB[:, stg_off + (ci % 2) * 2048 + tb * 512:
                                           stg_off + (ci % 2) * 2048 + (tb + 1) * 512]
        if src_fn is None:
            src_fn = lambda k, tb: mk.xbf(k, tb * 512, 512)
        C = wh.shape[1]
        mm_d, ev_d, wl_d = {}, {}, {}
        wbuf_w = self.wbuf_tail
        n = len(cols)
        for ci in range(n):
            col = cols[ci]
            par = ci % 2

            def g(g, ci=ci, col=col, par=par):
                if ci < 2:
                    g.wait_ge(wbuf_w[0].h, wbuf_w[1])
                else:
                    g.wait_ge(sT.h, mm_d[ci - 2])
                if wb_w:
                    g.dma_start(
                        out=bass.AP(mk.WBUF, par * 2048,
                                    [[4096, 128], [1, n_in_chunks * 128]]),
                        in_=bass.AP(wh, col * n_in_chunks * 128,
                                    [[C, 128], [1, n_in_chunks * 128]]),
                    ).then_inc(sW.h, 16)
                else:
                    g.dma_start(
                        out=bass.AP(mk.WBUF, par * 2048, [[4096, 128], [128, n_in_chunks], [1, 128]]),
                        in_=bass.AP(wh, col * 128, [[C, 128], [128 * C, n_in_chunks], [1, 128]]),
                    ).then_inc(sW.h, 16)
            self.G(g)
            wl_d[ci] = sW.inc(16)

            def t(t, ci=ci, par=par):
                t.wait_ge(sW.h, wl_d[ci])
                if ci == 0:
                    t.wait_ge(self.sG.h, self.ln["centered"][1])
                    for sm, val in extra_waits:
                        t.wait_ge(sm.h, val)
                if ci >= 2:
                    t.wait_ge(sV.h, ev_d[ci - 2])
                PB = mk.PB0 if par == 0 else mk.PB1
                ins = None
                for k in range(n_in_chunks):
                    for tb in range(4):
                        ins = t.matmul(PB[:, tb * 512:(tb + 1) * 512],
                                       mk.WBUF[:, par * 2048 + k * 128:par * 2048 + (k + 1) * 128],
                                       src_fn(k, tb),
                                       start=(k == 0), stop=(k == n_in_chunks - 1))
                ins.then_inc(sT.h, 1)
            self.T(t)
            mm_d[ci] = sT.inc()

            def v(v, ci=ci, par=par):
                v.wait_ge(sT.h, mm_d[ci])
                if spill is not None and ci >= 2:
                    v.wait_ge(sO.h, sp_d[ci - 2])
                if ci == 0 and use_rtile:
                    if r32:
                        v.wait_ge(self.ln["r32_tile"][0].h,
                                  self.ln["r32_tile"][1])
                    else:
                        v.wait_ge(self.ln["r_tile"][0].h,
                                  self.ln["r_tile"][1])
                PB = mk.PB0 if par == 0 else mk.PB1
                ins = None
                for tb in range(4):
                    pt = PB[:, tb * 512:(tb + 1) * 512]
                    dst = dest_fn(ci, tb)
                    if use_rtile:
                        rt = (mk.SC[:, tb * 512:(tb + 1) * 512] if r32
                              else mk.BCR[:, tb * 512:(tb + 1) * 512])
                        ins = v.scalar_tensor_tensor(
                            dst, pt, scale, rt, ALU.mult, ALU.mult)
                    elif scale != 1.0:
                        ins = v.tensor_scalar(dst, pt, scale, None, ALU.mult)
                    else:
                        ins = v.tensor_copy(dst, pt)
                ins.then_inc(sV.h, 1)
            self.V(v)
            ev_d[ci] = sV.inc()

            if spill is not None:
                def ysp(y, ci=ci):
                    y.wait_ge(sV.h, ev_d[ci])
                    y.dma_start(
                        out=bass.AP(spill[0], spill[1] + ci * 128 * 2048,
                                    [[2048, 128], [1, 2048]]),
                        in_=mk.SB[:, stg_off + (ci % 2) * 2048:
                                  stg_off + (ci % 2) * 2048 + 2048],
                    ).then_inc(sO.h, 16)
                self.Y(ysp)
                sp_d[ci] = sO.inc(16)
        self.wbuf_tail = (sT, mm_d[n - 1])
        return dict(ev=(sV, ev_d[n - 1]), mm=(sT, mm_d[n - 1]),
                    sp=(sO, sp_d[n - 1]) if spill is not None else None)

    def emit_vprod(self, w_sb_off, W, vdst_off, extra_waits=(), spill=None,
                   w_tensor=None, stg_off=0):
        mk = self
        sT, sV, sO = self.sT, self.sV, self.sO
        if w_tensor is None:
            w_tensor = mk.SA
        mm_d, ev_d, sp_d = {}, {}, {}
        for sc in range(16):
            par = sc % 2

            def t(t, sc=sc, par=par):
                if sc == 0:
                    t.wait_ge(self.sG.h, self.ln["centered"][1])
                    for sm, val in extra_waits:
                        t.wait_ge(sm.h, val)
                if sc >= 2:
                    t.wait_ge(sV.h, ev_d[sc - 2])
                PB = mk.PB0 if par == 0 else mk.PB1
                ins = None
                for k in range(NK):
                    for w0 in range(0, W, 512):
                        ww = min(512, W - w0)
                        ins = t.matmul(PB[:, w0:w0 + ww],
                                       mk.xbf(k, sc * 128, 128),
                                       w_tensor[:, w_sb_off + k * W + w0:
                                                w_sb_off + k * W + w0 + ww],
                                       start=(k == 0), stop=(k == NK - 1))
                ins.then_inc(sT.h, 1)
            self.T(t)
            mm_d[sc] = sT.inc()

            def v(v, sc=sc, par=par):
                v.wait_ge(sT.h, mm_d[sc])
                if sc == 0:
                    v.wait_ge(sV.h, self.ln["rcol"][1])
                if spill is not None and sc >= 2:
                    v.wait_ge(sO.h, sp_d[sc - 2])
                PB = mk.PB0 if par == 0 else mk.PB1
                off = (stg_off + (sc % 2) * W) if spill is not None                     else (vdst_off + sc * W)
                ins = v.tensor_scalar(mk.SB[:, off:off + W],
                                      PB[:, 0:W], mk.ROWS[:, 32 + sc:33 + sc],
                                      None, ALU.mult)
                ins.then_inc(sV.h, 1)
            self.V(v)
            ev_d[sc] = sV.inc()

            if spill is not None:
                def ysp(y, sc=sc):
                    y.wait_ge(sV.h, ev_d[sc])
                    y.dma_start(
                        out=bass.AP(spill[0], spill[1] + sc * 128 * W,
                                    [[W, 128], [1, W]]),
                        in_=mk.SB[:, stg_off + (sc % 2) * W:
                                  stg_off + (sc % 2) * W + W],
                    ).then_inc(sO.h, 16)
                self.Y(ysp)
                sp_d[sc] = sO.inc(16)
        return (sV, ev_d[15]) if spill is None else (sO, sp_d[15])

    def emit_attn_core(self, nheads, kt_off_fn, v_lhsT_fn, qkv_wait,
                       kt_tensor=None, q_src_fn=None, ot_dst_fn=None,
                       stage_ot=False, window=None, head_pre=None,
                       sel_cfg=False):
        mk = self
        sT, sV, sS, sX = self.sT, self.sV, self.sS, self.sX
        if kt_tensor is None:
            kt_tensor = mk.SB
        if q_src_fn is None:
            q_src_fn = lambda h: (mk.SA, h * 2048)
        if ot_dst_fn is None:
            ot_dst_fn = lambda h: (mk.SA, h * 2048)
        OT_STAGE = 14336
        exp_done = {}
        prev_tail = None
        pass2_stt_tail = {}
        for h in range(nheads):
            pre_wait = head_pre(h, prev_tail) if head_pre is not None else None
            kt0 = kt_off_fn(h)
            qsrc_t, qoff = q_src_fn(h)
            mm1 = {}
            for qt in range(16):
                if window is None:
                    klo = 0
                else:
                    klo = max(0, qt * 128 - window)
                khi = (qt + 1) * 128
                ext = khi - klo
                PB = mk.PB0 if qt % 2 == 0 else mk.PB1

                def t(t, h=h, qt=qt, klo=klo, ext=ext, PB=PB, kt0=kt0,
                      qoff=qoff, qsrc_t=qsrc_t, prev_tail=prev_tail,
                      pre_wait=pre_wait):
                    if qt < 2:
                        if h == 0:
                            t.wait_ge(qkv_wait[0].h, qkv_wait[1])
                        else:
                            t.wait_ge(prev_tail[0].h, prev_tail[1])
                            t.wait_ge(sS.h, exp_done[(h - 1, 14 + qt)])
                        if pre_wait is not None:
                            t.wait_ge(pre_wait[0].h, pre_wait[1])
                    else:
                        t.wait_ge(sS.h, exp_done[(h, qt - 2)])
                    ins = None
                    for b in range((ext + 511) // 512):
                        w = min(512, ext - b * 512)
                        ins = t.matmul(PB[:, b * 512:b * 512 + w],
                                       qsrc_t[:, qoff + qt * 128:qoff + (qt + 1) * 128],
                                       kt_tensor[:, kt0 + klo + b * 512:kt0 + klo + b * 512 + w],
                                       start=True, stop=True)
                    ins.then_inc(sT.h, 1)
                self.T(t)
                mm1[qt] = sT.inc()

                def v(v, qt=qt, klo=klo, ext=ext, PB=PB, h=h):
                    v.wait_ge(sT.h, mm1[qt])
                    v.reduce_max(mk.ROWS[:, 16 + qt:17 + qt], PB[:, 0:ext],
                                 axis=AX.X, negate=True)
                    if sel_cfg:
                        nb = (qt + 1) // 2
                        if nb >= 1:
                            v.tensor_tensor(
                                PB[:, 0:nb * 256], PB[:, 0:nb * 256],
                                bass.AP(mk.SB, 10240 + qt * 128 + h * 8,
                                        [[16384, 128], [1, nb], [0, 256]]),
                                ALU.add)
                    doff = qt * 128 - klo
                    ins = v.tensor_tensor(PB[:, doff:ext],
                                          PB[:, doff:ext],
                                          mk.CB["mask_qdiag"][:], ALU.add)
                    if window is not None and qt * 128 - window >= 0:
                        ins = v.tensor_tensor(PB[:, 0:128], PB[:, 0:128],
                                              mk.CB["mask_qlow"][:], ALU.add)
                    ins.then_inc(sV.h, 1)
                self.V(v)
                max_d = sV.inc()

                if qt % 2 == 1:
                    klo_p = 0 if window is None else max(0, (qt - 1) * 128 - window)
                    ext_p = qt * 128 - klo_p
                    PB_p = mk.PB0

                    def s(s, qt=qt, ext=ext, PB=PB, max_d=max_d,
                          ext_p=ext_p, PB_p=PB_p):
                        s.wait_ge(sV.h, max_d)
                        s.activation(mk.WBUF[:, 2048:2048 + ext_p],
                                     PB_p[:, 0:ext_p], AF.Exp,
                                     bias=mk.ROWS[:, 15 + qt:16 + qt],
                                     accum_out=mk.ROWS[:, 47 + qt:48 + qt])
                        ins = s.activation(mk.WBUF[:, 2048:2048 + ext],
                                           PB[:, 0:ext], AF.Exp,
                                           bias=mk.ROWS[:, 16 + qt:17 + qt],
                                           accum_out=mk.ROWS[:, 48 + qt:49 + qt])
                        ins.then_inc(sS.h, 1)
                    self.Sc(s)
                    exp_done[(h, qt - 1)] = exp_done[(h, qt)] = sS.inc()

            def sln(s, h=h):
                s.wait_ge(sS.h, exp_done[(h, 15)])
                ins = s.activation(mk.ROWS[:, 208:224], mk.ROWS[:, 48:64],
                                   AF.Ln)
                ins.then_inc(sS.h, 1)
            self.Sc(sln)
            ln_d = sS.inc()

            def vc2(v, ln_d=ln_d):
                v.wait_ge(sS.h, ln_d)
                ins = v.tensor_tensor(mk.ROWS[:, 16:32], mk.ROWS[:, 208:224],
                                      mk.ROWS[:, 16:32], ALU.subtract)
                ins.then_inc(sV.h, 1)
            self.V(vc2)
            sV.inc()
            cmat_done = (sV, sV.n)

            def t2(t, cm=cmat_done):
                t.wait_ge(sV.h, cm[1])
                t.transpose(mk.PB1[0:16, 1536:1664], mk.ROWS[:, 16:32],
                            mk.CB["ident"][:]).then_inc(sT.h, 1)
            self.T(t2)
            ctp = sT.inc()

            def v3(v, ctp=ctp):
                v.wait_ge(sT.h, ctp)
                v.tensor_copy(mk.ROWS[0:16, 64:192],
                              mk.PB1[0:16, 1536:1664]).then_inc(sV.h, 1)
            self.V(v3)
            cev = sV.inc()
            wrow = self.row_to_dram(mk.ROWS, 64, BNC_C0 + h % 2, (sV, cev),
                                    part16=True)
            ew = None
            if h >= 2:
                ew = (sV, pass2_stt_tail[h - 2])
            ctile = self.bcast_from_dram(BNC_C0 + h % 2, mk.BCF,
                                         (h % 2) * 2048, 2048, wrow,
                                         extra_wait=ew)

            coff = (h % 2) * 2048
            ktg = 0
            stt_done = {}
            pexp_done = {}
            selmul_done = {}
            pv_mm = {}
            otv = {}
            NS = 3 if sel_cfg else 4

            def stt_ap(i):
                if i < 2:
                    return mk.SC[:, 2048 + i * 512:2048 + (i + 1) * 512]
                if sel_cfg:
                    return mk.PB0[:, 1536:2048]
                return mk.PB0[:, 512 + (i - 2) * 512:1024 + (i - 2) * 512]
            for qb in range(4):
                if window is None:
                    kts = list(range(qb * 4 + 4))
                else:
                    kts = list(range(max(0, qb * 4 - window // 128),
                                     qb * 4 + 4))
                nkt = len(kts)
                gi0 = 0
                while gi0 < nkt:
                    gn = 2 if gi0 + 1 < nkt else 1
                    grp = [(gi0 + j, kts[gi0 + j], ktg + j) for j in range(gn)]
                    for ki, kt, kg in grp:
                        pslot = kg % 4
                        sslot = kg % NS

                        def t3(t, qb=qb, kt=kt, ktg=kg, pslot=pslot, kt0=kt0,
                               qoff=qoff, qsrc_t=qsrc_t, ct=ctile):
                            if ktg == 0:
                                t.wait_ge(sX.h, ct[1])
                            if ktg >= 4:
                                t.wait_ge(sV.h, stt_done[ktg - 4])
                            t.matmul(mk.PB1[:, pslot * 512:(pslot + 1) * 512],
                                     kt_tensor[:, kt0 + kt * 128:kt0 + (kt + 1) * 128],
                                     qsrc_t[:, qoff + qb * 512:qoff + (qb + 1) * 512],
                                     start=True, stop=True).then_inc(sT.h, 1)
                        self.T(t3)
                        st_mm = sT.inc()

                        def v4(v, qb=qb, kt=kt, ktg=kg, pslot=pslot, sslot=sslot,
                               st_mm=st_mm, coff=coff):
                            v.wait_ge(sT.h, st_mm)
                            if ktg >= NS:
                                v.wait_ge(sS.h, pexp_done[ktg - NS])
                            ins = v.scalar_tensor_tensor(
                                stt_ap(sslot),
                                mk.PB1[:, pslot * 512:(pslot + 1) * 512], 1.0,
                                mk.BCF[:, coff + qb * 512:coff + (qb + 1) * 512],
                                ALU.mult, ALU.subtract)
                            ins.then_inc(sV.h, 1)
                        self.V(v4)
                        stt_done[kg] = sV.inc()

                    def s2(s, grp=grp):
                        kg_last = grp[-1][2]
                        s.wait_ge(sV.h, stt_done[kg_last])
                        if kg_last >= 4:
                            s.wait_ge(sT.h, pv_mm[kg_last - 4])
                        ins = None
                        for _, _, kg in grp:
                            ins = s.activation(
                                mk.WBUF[:, (kg % 4) * 512:(kg % 4 + 1) * 512],
                                stt_ap(kg % NS),
                                AF.Exp)
                        ins.then_inc(sS.h, 1)
                    self.Sc(s2)
                    pe_val = sS.inc()
                    for _, _, kg in grp:
                        pexp_done[kg] = pe_val

                    for ki, kt, kg in grp:
                        ptile = kg % 4
                        self._attn_tile_tail(h, qb, ki, kt, kg, nkt, ptile,
                                             sel_cfg, window, exp_done,
                                             selmul_done, pexp_done, pv_mm,
                                             otv, v_lhsT_fn)
                    ktg += gn
                    gi0 += gn
                pv_done = (sT, sT.n)

                ot_t, ot_off = (mk.SB, OT_STAGE) if stage_ot else ot_dst_fn(h)

                def v5(v, qb=qb, pv=pv_done, ot_t=ot_t, ot_off=ot_off):
                    v.wait_ge(pv[0].h, pv[1])
                    ins = v.tensor_copy(
                        ot_t[:, ot_off + qb * 512:ot_off + (qb + 1) * 512],
                        mk.PB0[:, 0:512])
                    ins.then_inc(sV.h, 1)
                self.V(v5)
                otv[qb] = sV.inc()
            if stage_ot:
                dt_t, dt_off = ot_dst_fn(h)

                def v6(v, dt_t=dt_t, dt_off=dt_off, w=otv[3]):
                    v.wait_ge(sV.h, w)
                    ins = v.tensor_copy(dt_t[:, dt_off:dt_off + 2048],
                                        mk.SB[:, OT_STAGE:OT_STAGE + 2048])
                    ins.then_inc(sV.h, 1)
                self.V(v6)
                sV.inc()
            pass2_stt_tail[h] = stt_done[ktg - 1]
            prev_tail = (sV, sV.n)
        return (sV, sV.n)

    def _attn_tile_tail(self, h, qb, ki, kt, ktg, nkt, ptile, sel_cfg, window,
                        exp_done, selmul_done, pexp_done, pv_mm, otv,
                        v_lhsT_fn):
        mk = self
        sT, sV, sS = self.sT, self.sV, self.sS
        if True:
            if True:
                if True:
                    if sel_cfg:
                        blk = kt // 2
                        mslot = ktg % 2

                        def tmsk(t, h=h, qb=qb, ktg=ktg, blk=blk, mslot=mslot):
                            if ktg < 2:
                                t.wait_ge(sS.h, exp_done[(h, 14)])
                            else:
                                t.wait_ge(sV.h, selmul_done[ktg - 2])
                            moff = 512 + mslot * 512
                            t.matmul(mk.PB0[0:64, moff:moff + 512],
                                     mk.CB["epat"][0:8, blk * 64:blk * 64 + 64],
                                     mk.SB[0:8,
                                           8192 + qb * 512:8192 + (qb + 1) * 512],
                                     start=True, stop=True)
                            t.matmul(mk.PB0[64:128, moff:moff + 512],
                                     mk.CB["epat"][0:8, blk * 64:blk * 64 + 64],
                                     mk.SB[0:8,
                                           8192 + qb * 512:8192 + (qb + 1) * 512],
                                     start=True, stop=True,
                                     tile_position=(0, 64)).then_inc(sT.h, 1)
                        self.T(tmsk)
                        tm = sT.inc()

                        def v4s(v, ktg=ktg, ptile=ptile, mslot=mslot, tm=tm):
                            v.wait_ge(sT.h, tm)
                            v.wait_ge(sS.h, pexp_done[ktg])
                            ins = v.tensor_tensor(
                                mk.WBUF[:, ptile * 512:(ptile + 1) * 512],
                                mk.WBUF[:, ptile * 512:(ptile + 1) * 512],
                                mk.PB0[:, 512 + mslot * 512:1024 + mslot * 512],
                                ALU.mult)
                            ins.then_inc(sV.h, 1)
                            v.memset(mk.ROWS[:, 12:13], 0.0)
                        self.V(v4s)
                        selmul_done[ktg] = sV.inc()

                    pmask_done = None
                    if kt >= qb * 4:
                        vv = kt - qb * 4

                        def v4b(v, ktg=ktg, ptile=ptile, vv=vv):
                            if sel_cfg:
                                v.wait_ge(sV.h, selmul_done[ktg])
                            else:
                                v.wait_ge(sS.h, pexp_done[ktg])
                            ins = v.tensor_tensor(
                                mk.WBUF[:, ptile * 512:(ptile + 1) * 512],
                                mk.WBUF[:, ptile * 512:(ptile + 1) * 512],
                                mk.CB["bin_kdiag_m"][:, 384 - vv * 128:896 - vv * 128],
                                ALU.mult)
                            ins.then_inc(sV.h, 1)
                        self.V(v4b)
                        pmask_done = sV.inc()
                    elif window is not None and kt * 128 < qb * 512 - window + 512:
                        vv = kt - (qb * 4 - window // 128)

                        def v4c(v, ktg=ktg, ptile=ptile, vv=vv):
                            if sel_cfg:
                                v.wait_ge(sV.h, selmul_done[ktg])
                            else:
                                v.wait_ge(sS.h, pexp_done[ktg])
                            ins = v.tensor_tensor(
                                mk.WBUF[:, ptile * 512:(ptile + 1) * 512],
                                mk.WBUF[:, ptile * 512:(ptile + 1) * 512],
                                mk.CB["bin_klow_m"][:, 384 - vv * 128:896 - vv * 128],
                                ALU.mult)
                            ins.then_inc(sV.h, 1)
                        self.V(v4c)
                        pmask_done = sV.inc()

                    def t4(t, h=h, qb=qb, ki=ki, ktg=ktg, kt=kt, ptile=ptile,
                           nkt=nkt, pmask_done=pmask_done):
                        if pmask_done is not None:
                            t.wait_ge(sV.h, pmask_done)
                        elif sel_cfg:
                            t.wait_ge(sV.h, selmul_done[ktg])
                        else:
                            t.wait_ge(sS.h, pexp_done[ktg])
                        if ki == 0 and qb >= 1:
                            t.wait_ge(sV.h, otv[qb - 1])
                        t.matmul(mk.PB0[:, 0:512], v_lhsT_fn(h, kt),
                                 mk.WBUF[:, ptile * 512:(ptile + 1) * 512],
                                 start=(ki == 0), stop=(ki == nkt - 1)
                                 ).then_inc(sT.h, 1)
                    self.T(t4)
                    pv_mm[ktg] = sT.inc()

    def emit_out_proj_residual(self, wh, last, scale=1.0, src_chunk_fn=None,
                               n_in_chunks=NK, attn_wait=None, mult_tile=None,
                               wb_w=False):
        mk = self
        sW, sT, sV, sG, sX, sO = self.sW, self.sT, self.sV, self.sG, self.sX, self.sO
        if src_chunk_fn is None:
            src_chunk_fn = lambda k, tb: mk.SA[:, k * 2048 + tb * 512:k * 2048 + (tb + 1) * 512]
        C = wh.shape[1]
        mm_d, ev_d, wl_d, xl_d, st_d, cast_d = {}, {}, {}, {}, {}, {}
        dst_dram = mk.d_xout if last else mk.d_xres
        wbuf_w = self.wbuf_tail
        for m in range(16):
            par = m % 2

            def g(g, m=m, par=par):
                if m < 2:
                    g.wait_ge(wbuf_w[0].h, wbuf_w[1])
                    if attn_wait is not None:
                        g.wait_ge(attn_wait[0].h, attn_wait[1])
                else:
                    g.wait_ge(sT.h, mm_d[m - 2])
                if wb_w:
                    g.dma_start(
                        out=bass.AP(mk.WBUF, par * 2048,
                                    [[4096, 128], [1, n_in_chunks * 128]]),
                        in_=bass.AP(wh, m * n_in_chunks * 128,
                                    [[C, 128], [1, n_in_chunks * 128]]),
                    ).then_inc(sW.h, 16)
                else:
                    g.dma_start(
                        out=bass.AP(mk.WBUF, par * 2048, [[4096, 128], [128, n_in_chunks], [1, 128]]),
                        in_=bass.AP(wh, m * 128, [[C, 128], [128 * C, n_in_chunks], [1, 128]]),
                    ).then_inc(sW.h, 16)
            self.G(g)
            wl_d[m] = sW.inc(16)

            def y(y, m=m):
                if m == 0 and attn_wait is not None:
                    y.wait_ge(attn_wait[0].h, attn_wait[1])
                if m >= 1:
                    y.wait_ge(sO.h, st_d[m - 1])
                    y.wait_ge(sG.h, cast_d[m - 1])
                for hh in range(2):
                    y.dma_start(
                        out=mk.SC[:, hh * 1024:(hh + 1) * 1024],
                        in_=bass.AP(mk.d_xres, m * 128 * S + hh * 1024,
                                    [[S, 128], [1, 1024]]),
                    ).then_inc(sX.h, 16)
            self.Y(y)
            xl_d[m] = sX.inc(32)

            def t(t, m=m, par=par):
                t.wait_ge(sW.h, wl_d[m])
                if m == 0 and attn_wait is not None:
                    t.wait_ge(attn_wait[0].h, attn_wait[1])
                if m >= 2:
                    t.wait_ge(sV.h, ev_d[m - 2])
                PB = mk.PB0 if par == 0 else mk.PB1
                ins = None
                for k in range(n_in_chunks):
                    for tb in range(4):
                        ins = t.matmul(PB[:, tb * 512:(tb + 1) * 512],
                                       mk.WBUF[:, par * 2048 + k * 128:par * 2048 + (k + 1) * 128],
                                       src_chunk_fn(k, tb),
                                       start=(k == 0), stop=(k == n_in_chunks - 1))
                ins.then_inc(sT.h, 1)
            self.T(t)
            mm_d[m] = sT.inc()

            def v(v, m=m, par=par):
                v.wait_ge(sT.h, mm_d[m])
                v.wait_ge(sX.h, xl_d[m])
                PB = mk.PB0 if par == 0 else mk.PB1
                ins = None
                if mult_tile is None:
                    for tb in range(4):
                        ins = v.scalar_tensor_tensor(
                            mk.SC[:, tb * 512:(tb + 1) * 512],
                            PB[:, tb * 512:(tb + 1) * 512], scale,
                            mk.SC[:, tb * 512:(tb + 1) * 512],
                            ALU.mult, ALU.add)
                else:
                    mt_t, mt_off = mult_tile
                    for tb in range(4):
                        v.scalar_tensor_tensor(
                            mk.SC[:, 2048 + (tb % 2) * 512:2048 + (tb % 2 + 1) * 512],
                            PB[:, tb * 512:(tb + 1) * 512], scale,
                            mt_t[:, mt_off + tb * 512:mt_off + (tb + 1) * 512],
                            ALU.mult, ALU.mult)
                        if tb >= 1:
                            v.tensor_tensor(
                                mk.SC[:, (tb - 1) * 512:tb * 512],
                                mk.SC[:, (tb - 1) * 512:tb * 512],
                                mk.SC[:, 2048 + ((tb - 1) % 2) * 512:2048 + ((tb - 1) % 2 + 1) * 512],
                                ALU.add)
                    v.memset(mk.ROWS[:, 12:13], 0.0)
                    ins = v.tensor_tensor(
                        mk.SC[:, 3 * 512:4 * 512], mk.SC[:, 3 * 512:4 * 512],
                        mk.SC[:, 2048 + 512:2048 + 1024], ALU.add)
                ins.then_inc(sV.h, 1)
            self.V(v)
            ev_d[m] = sV.inc()

            def gc(g, m=m):
                g.wait_ge(sV.h, ev_d[m])
                ins = None
                for tb in range(4):
                    ins = g.tensor_copy(mk.xbf(m, tb * 512, 512),
                                        mk.SC[:, tb * 512:(tb + 1) * 512])
                ins.then_inc(sG.h, 1)
            self.G(gc)
            cast_d[m] = sG.inc()

            def y2(y, m=m):
                y.wait_ge(sV.h, ev_d[m])
                for hh in range(2):
                    y.dma_start(
                        out=bass.AP(dst_dram, m * 128 * S + hh * 1024,
                                    [[S, 128], [1, 1024]]),
                        in_=mk.SC[:, hh * 1024:(hh + 1) * 1024],
                    ).then_inc(sO.h, 16)
            self.Y(y2)
            st_d[m] = sO.inc(32)
        self.xbf_ready = [(sG, cast_d[15])]
        self.wbuf_tail = (sT, mm_d[15])
        return (sO, st_d[15])

    def finish(self):
        mk = self

        def y(y):
            y.wait_ge(mk.sO.h, mk.sO.n)
        self.Y(y)
        self.es.close()


def sub_attn_b(mk, W, last):
    wq = mk.add_weight("L0b_wq", W["w_q_wb"])
    wkv = mk.add_weight("L0b_wkv", W["w_kv"])
    wkvb = mk.add_weight("L0b_wkv_wb", W["w_kv_wb"])
    wd = mk.add_weight("L0b_wd", W["w_dense_wb"])

    mk.emit_stats_center(rms2=False)

    kp = mk.emit_proj(wkvb, [0, 1, 2, 3],
                      lambda ci, tb: mk.SB[:, ci * 2048 + tb * 512:ci * 2048 + (tb + 1) * 512],
                      wb_w=True)

    def gv(g):
        g.wait_ge(mk.sT.h, mk.ln["stats_T"][1])
        g.dma_start(
            out=bass.AP(mk.SA, 0, [[32768, 128], [512, NK], [1, 512]]),
            in_=bass.AP(wkv, 512, [[1024, 128], [128 * 1024, NK], [1, 512]]),
        ).then_inc(mk.sW.h, 16)
    mk.G(gv)
    vw_loaded = mk.sW.inc(16)

    vp = mk.emit_vprod(0, 512, 8192,
                       extra_waits=[(mk.sW, vw_loaded), kp["ev"]])

    qp = mk.emit_proj(wq, list(range(16)),
                      lambda ci, tb: mk.SA[:, ci * 2048 + tb * 512:ci * 2048 + (tb + 1) * 512],
                      scale=1.0 / math.sqrt(128.0),
                      extra_waits=[vp], wb_w=True)

    att = mk.emit_attn_core(
        nheads=16,
        kt_off_fn=lambda h: (h // 4) * 2048,
        v_lhsT_fn=lambda h, kt: mk.SB[:, 8192 + kt * 512 + (h // 4) * 128:
                                      8192 + kt * 512 + (h // 4) * 128 + 128],
        qkv_wait=qp["ev"])

    return mk.emit_out_proj_residual(wd, last, attn_wait=att, wb_w=True)



def emit_streamed_attention(mk, wv_src, kt_cols, q_cols, wh_qkv, qscale,
                            window=None, nheads=16, kt_w=None, q_w=None,
                            m_wait=None):
    sW, sX, sT, sV = mk.sW, mk.sX, mk.sT, mk.sV
    QSP = (mk.d_spill, 0)
    VSP = (mk.d_spill, 2048 * 2048)

    wvh, wvoff = wv_src
    Cw = wvh.shape[1]

    def gwv(g):
        g.wait_ge(sT.h, mk.ln["stats_T"][1])
        for i in range(4):
            g.dma_start(
                out=bass.AP(mk.SA, i * 4 * 2048, [[32768, 128], [2048, 4], [1, 2048]]),
                in_=bass.AP(wvh, wvoff + i * 4 * 128 * Cw,
                            [[Cw, 128], [128 * Cw, 4], [1, 2048]]),
            ).then_inc(sW.h, 16)
    mk.G(gwv)
    wv_ld = sW.inc(64)
    vw = [(sW, wv_ld)] + ([m_wait] if m_wait is not None else [])
    vp = mk.emit_vprod(0, 2048, 0, extra_waits=vw,
                       spill=VSP, w_tensor=mk.SA, stg_off=0)
    kp = mk.emit_proj(kt_w if kt_w is not None else wh_qkv, kt_cols,
                      lambda ci, tb: mk.SA[:, ci * 2048 + tb * 512:ci * 2048 + (tb + 1) * 512],
                      extra_waits=[vp], wb_w=True)
    qp = mk.emit_proj(q_w if q_w is not None else wh_qkv, q_cols, scale=qscale,
                      spill=QSP, stg_off=4096, extra_waits=[kp["ev"]], wb_w=True)

    qv_ld = {}

    def emit_qv_load(hh, waits):
        def y0(y, hh=hh, waits=waits):
            for sm, val in waits:
                y.wait_ge(sm.h, val)
            y.dma_start(
                out=mk.SB[:, (hh % 2) * 2048:(hh % 2) * 2048 + 2048],
                in_=bass.AP(mk.d_spill, QSP[1] + hh * 128 * 2048,
                            [[2048, 128], [1, 2048]])).then_inc(sX.h, 16)
            y.dma_start(
                out=mk.SB[:, 4096 + (hh % 2) * 2048:4096 + (hh % 2) * 2048 + 2048],
                in_=bass.AP(mk.d_spill, VSP[1] + hh * 128,
                            [[2048, 128], [128 * 2048, 16], [1, 128]])
            ).then_inc(sX.h, 16)
        mk.Y(y0)
        qv_ld[hh] = sX.inc(32)

    def head_pre(h, prev_tail):
        if h == 0:
            emit_qv_load(0, [qp["sp"], vp])
        if h + 1 < nheads:
            emit_qv_load(h + 1, [prev_tail] if prev_tail is not None
                         else [qp["sp"], vp])
        return (sX, qv_ld[h])

    att = mk.emit_attn_core(
        nheads=nheads, kt_tensor=mk.SA, kt_off_fn=lambda h: h * 2048,
        q_src_fn=lambda h: (mk.SB, (h % 2) * 2048),
        v_lhsT_fn=lambda h, kt: mk.SB[:, 4096 + (h % 2) * 2048 + kt * 128:
                                      4096 + (h % 2) * 2048 + (kt + 1) * 128],
        ot_dst_fn=lambda h: (mk.SA, h * 2048),
        stage_ot=True, head_pre=head_pre, qkv_wait=qp["ev"], window=window)
    return att, qp


def sub_attn_a(mk, W, last):
    wqkv = mk.add_weight("L0a_wqkv", W["wqkv_l"])
    wqkvb = mk.add_weight("L0a_wqkv_wb", W["wqkv_wb"])
    wout = mk.add_weight("L0a_wout", W["wout_wb"])
    mk.emit_stats_center(rms2=False)
    att, qp = emit_streamed_attention(
        mk, (wqkv, 4096), list(range(16, 32)), list(range(16)),
        wqkvb, 1.0 / math.sqrt(128.0), window=512)
    return mk.emit_out_proj_residual(wout, last, attn_wait=att, wb_w=True)


def sub_ffn(mk, W, last, stage=99):
    nm = "L0f_"
    wq = mk.add_weight(nm + "wq", W["wq_l"])
    wk = mk.add_weight(nm + "wk", W["wk_l"])
    wv = mk.add_weight(nm + "wv", W["wv_l"])
    wo = mk.add_weight(nm + "wo", W["wo_raw"])
    wa = mk.add_weight(nm + "A", W["A"])
    wb = mk.add_weight(nm + "B", W["B"])
    win = mk.add_weight(nm + "win", W["win_l"])
    wup = mk.add_weight(nm + "wup", W["wup_l"])
    wdn = mk.add_weight(nm + "wdn", W["wdn_l"])
    lc1 = mk.add_weight(nm + "ln1c", W["ln1c"])
    lc2 = mk.add_weight(nm + "ln2c", W["ln2c"])
    sW, sX, sT, sV, sS, sG, sO = (mk.sW, mk.sX, mk.sT, mk.sV, mk.sS, mk.sG,
                                  mk.sO)
    QSP = (mk.d_spill, 0)
    VSP = (mk.d_spill, 2048 * 2048)

    ln = mk.emit_stats_center(rms2=True)
    if stage <= 0:
        return (mk.sO, mk.sO.n)

    def gld(g):
        g.dma_start(out=mk.SB[:, 8192:8448],
                    in_=bass.AP(wa, 0, [[16, 128], [128 * 16, 16], [1, 16]]),
                    ).then_inc(sW.h, 16)
        g.dma_start(out=mk.LNC[:, 0:16],
                    in_=lc1.ap()).then_inc(sW.h, 16)
        g.dma_start(out=mk.LNC[:, 16:32],
                    in_=lc2.ap()).then_inc(sW.h, 16)
    mk.G(gld)
    small_ld = sW.inc(48)

    wbw = mk.wbuf_tail
    mb_mm, mb_ev, mb_wl = {}, {}, {}
    for ec in range(16):
        par = ec % 2

        def g(g, ec=ec, par=par):
            if ec < 2:
                g.wait_ge(wbw[0].h, wbw[1])
            else:
                g.wait_ge(sT.h, mb_mm[ec - 2])
            g.dma_start(
                out=bass.AP(mk.WBUF, par * 2048, [[4096, 128], [1, 2048]]),
                in_=bass.AP(wo, ec * 2048, [[32768, 128], [1, 2048]]),
            ).then_inc(sW.h, 16)
        mk.G(g)
        mb_wl[ec] = sW.inc(16)

        def t(t, ec=ec, par=par):
            t.wait_ge(sW.h, mb_wl[ec])
            if ec == 0:
                t.wait_ge(sW.h, small_ld)
            if ec >= 2:
                t.wait_ge(sV.h, mb_ev[ec - 2])
            PB = mk.PB0 if par == 0 else mk.PB1
            ins = None
            for fc in range(NK):
                ins = t.matmul(PB[:, 0:16],
                               mk.WBUF[:, par * 2048 + fc * 128:par * 2048 + (fc + 1) * 128],
                               mk.SB[:, 8192 + fc * 16:8192 + (fc + 1) * 16],
                               start=(fc == 0), stop=(fc == NK - 1))
            ins.then_inc(sT.h, 1)
        mk.T(t)
        mb_mm[ec] = sT.inc()

        def v(v, ec=ec, par=par):
            v.wait_ge(sT.h, mb_mm[ec])
            PB = mk.PB0 if par == 0 else mk.PB1
            ins = v.tensor_copy(mk.SB[:, 8448 + ec * 16:8448 + (ec + 1) * 16],
                                PB[:, 0:16])
            ins.then_inc(sV.h, 1)
        mk.V(v)
        mb_ev[ec] = sV.inc()
    mk.wbuf_tail = (sT, mb_mm[15])
    m_done = (sV, mb_ev[15])

    def gs1(g):
        g.wait_ge(sG.h, ln["centered"][1])
        g.wait_ge(sW.h, small_ld)
        ins = None
        for k in range(NK):
            ins = g.tensor_scalar(mk.xbf(k), mk.xbf(k), mk.LNC[:, k:k + 1],
                                  None, ALU.mult)
        ins.then_inc(sG.h, 1)
    mk.G(gs1)
    xscaled1 = (sG, sG.inc())
    mk.ln["centered"] = xscaled1

    if stage <= 3:
        return (mk.sO, mk.sO.n)
    att, qp = emit_streamed_attention(
        mk, (wv, 0), list(range(16)), list(range(16)), None,
        1.0 / math.sqrt(128.0), kt_w=wk, q_w=wq, m_wait=m_done)

    if stage <= 4:
        return (mk.sO, mk.sO.n)
    md_mm = {}
    for tb in range(4):
        def t(t, tb=tb):
            if tb == 0:
                t.wait_ge(att[0].h, att[1])
            ins = None
            for k in range(NK):
                ins = t.matmul(mk.PB0[0:16, tb * 512:(tb + 1) * 512],
                               mk.SB[:, 8448 + k * 16:8448 + (k + 1) * 16],
                               mk.SA[:, k * 2048 + tb * 512:k * 2048 + (tb + 1) * 512],
                               start=(k == 0), stop=(k == NK - 1))
            ins.then_inc(sT.h, 1)
        mk.T(t)
        md_mm[tb] = sT.inc()

    def smod(s):
        ins = None
        for tb in range(4):
            s.wait_ge(sT.h, md_mm[tb])
            ins = s.activation(mk.BCF[0:16, 2048 + tb * 512:2048 + (tb + 1) * 512],
                               mk.PB0[0:16, tb * 512:(tb + 1) * 512], AF.Tanh)
        ins.then_inc(sS.h, 1)
    mk.Sc(smod)
    mod_done = (sS, sS.inc())

    def gre(g):
        g.wait_ge(qp["mm"][0].h, qp["mm"][1])
        for i in range(4):
            g.dma_start(
                out=bass.AP(mk.XBF, i * 4 * S, [[NK * S, 128], [S, 4], [1, S]]),
                in_=bass.AP(mk.d_xres, i * 4 * 128 * S,
                            [[S, 128], [128 * S, 4], [1, S]]),
            ).then_inc(sX.h, 16)
    mk.G(gre)
    xre = sX.inc(64)
    mu_tile2 = mk.bcast_from_dram(BNC_MU, mk.BCF, 0, 2048, (sX, xre),
                                  extra_wait=att)

    def gc2(g):
        g.wait_ge(sX.h, mu_tile2[1])
        ins = None
        for k in range(NK):
            ins = g.tensor_tensor(mk.xbf(k), mk.xbf(k), mk.BCF[:, 0:2048],
                                  ALU.subtract)
        ins.then_inc(sG.h, 1)
        for k in range(NK):
            ins = g.tensor_scalar(mk.xbf(k), mk.xbf(k), mk.LNC[:, 16 + k:17 + k],
                                  None, ALU.mult)
        ins.then_inc(sG.h, 1)
    mk.G(gc2)
    sG.inc()
    x2ready = (sG, sG.inc())

    ha_mm = {}
    for tb in range(4):
        def t(t, tb=tb):
            if tb == 0:
                t.wait_ge(x2ready[0].h, x2ready[1])
            ins = None
            for k in range(NK):
                ins = t.matmul(mk.PB1[0:16, tb * 512:(tb + 1) * 512],
                               mk.SB[:, 8192 + k * 16:8192 + (k + 1) * 16],
                               mk.xbf(k, tb * 512, 512),
                               start=(k == 0), stop=(k == NK - 1))
            ins.then_inc(sT.h, 1)
        mk.T(t)
        ha_mm[tb] = sT.inc()

    def vha(v):
        ins = None
        for tb in range(4):
            v.wait_ge(sT.h, ha_mm[tb])
            ins = v.scalar_tensor_tensor(
                mk.SC[0:16, tb * 512:(tb + 1) * 512],
                mk.PB1[0:16, tb * 512:(tb + 1) * 512], 1.0,
                mk.BCR[0:16, tb * 512:(tb + 1) * 512], ALU.mult, ALU.mult)
        ins.then_inc(sV.h, 1)
    mk.V(vha)
    ha_done = (sV, sV.inc())

    rrow = mk.bcast_from_dram(BNC_R, mk.BCF, 0, 2048, x2ready)

    def vinv(v):
        v.wait_ge(sX.h, rrow[1])
        ins = v.reciprocal(mk.BCF[0:16, 0:2048], mk.BCF[0:16, 0:2048])
        ins.then_inc(sV.h, 1)
    mk.V(vinv)
    inv_done = (sV, sV.inc())

    def vgp(v):
        v.wait_ge(mod_done[0].h, mod_done[1])
        v.wait_ge(ha_done[0].h, ha_done[1])
        v.wait_ge(inv_done[0].h, inv_done[1])
        v.tensor_tensor(mk.SC[0:16, 0:2048], mk.SC[0:16, 0:2048],
                        mk.BCF[0:16, 2048:4096], ALU.mult)
        v.memset(mk.ROWS[:, 12:13], 0.0)
        ins = v.scalar_tensor_tensor(mk.SB[0:16, 12288:14336],
                                     mk.SC[0:16, 0:2048], 0.1,
                                     mk.BCF[0:16, 0:2048], ALU.mult, ALU.mult)
        ins.then_inc(sV.h, 1)
    mk.V(vgp)
    gp_done = (sV, sV.inc())

    def gB(g):
        g.wait_ge(att[0].h, att[1])
        g.dma_start(out=mk.SB[0:16, 0:8192], in_=wb.ap()).then_inc(sW.h, 16)
    mk.G(gB)
    b_ld = sW.inc(16)

    if stage <= 5:
        return (mk.sO, mk.sO.n)
    import os
    NMCH = int(os.environ.get("FFN_NM", "64"))
    gl_mm, up_mm, pr_ev, pr_sp, wl_w, wl_u, g_ev = {}, {}, {}, {}, {}, {}, {}
    silu_d = {}
    wbw2 = mk.wbuf_tail
    for m in range(NMCH):
        par = m % 2

        def g(g, m=m, par=par):
            if m < 2:
                g.wait_ge(wbw2[0].h, wbw2[1])
                g.wait_ge(att[0].h, att[1])
            else:
                g.wait_ge(sT.h, gl_mm[m - 2])
            g.dma_start(
                out=bass.AP(mk.WBUF, par * 2048, [[4096, 128], [1, 2048]]),
                in_=bass.AP(win, m * 2048, [[131072, 128], [1, 2048]]),
            ).then_inc(sW.h, 16)
        mk.G(g)
        wl_w[m] = sW.inc(16)

        def g2(g, m=m):
            if m == 0:
                g.wait_ge(att[0].h, att[1])
            else:
                g.wait_ge(sT.h, up_mm[m - 1])
            g.dma_start(
                out=bass.AP(mk.SB, 14336, [[16384, 128], [1, 2048]]),
                in_=bass.AP(wup, m * 2048, [[131072, 128], [1, 2048]]),
            ).then_inc(sW.h, 16)
        mk.G(g2)
        wl_u[m] = sW.inc(16)

        def t(t, m=m, par=par):
            t.wait_ge(sW.h, wl_w[m])
            if m == 0:
                t.wait_ge(gp_done[0].h, gp_done[1])
                t.wait_ge(sW.h, b_ld)
            if m >= 1:
                t.wait_ge(sV.h, pr_ev[m - 1])
            PB = mk.PB0 if par == 0 else mk.PB1
            ins = None
            for k in range(NK):
                for tb in range(4):
                    ins = t.matmul(PB[:, tb * 512:(tb + 1) * 512],
                                   mk.WBUF[:, par * 2048 + k * 128:par * 2048 + (k + 1) * 128],
                                   mk.xbf(k, tb * 512, 512),
                                   start=(k == 0), stop=False)
            for tb in range(4):
                ins = t.matmul(PB[:, tb * 512:(tb + 1) * 512],
                               mk.SB[0:16, m * 128:(m + 1) * 128],
                               mk.SB[0:16, 12288 + tb * 512:12288 + (tb + 1) * 512],
                               start=False, stop=True)
            ins.then_inc(sT.h, 1)
        mk.T(t)
        gl_mm[m] = sT.inc()

        def v(v, m=m, par=par):
            v.wait_ge(sT.h, gl_mm[m])
            if m >= 1:
                v.wait_ge(sS.h, silu_d[m - 1])
            PB = mk.PB0 if par == 0 else mk.PB1
            ins = None
            for tb in range(4):
                ins = v.scalar_tensor_tensor(
                    mk.SC[:, tb * 512:(tb + 1) * 512],
                    PB[:, tb * 512:(tb + 1) * 512], 1.0,
                    mk.BCR[:, tb * 512:(tb + 1) * 512], ALU.mult, ALU.mult)
            ins.then_inc(sV.h, 1)
        mk.V(v)
        g_ev[m] = sV.inc()

        def s(s, m=m):
            s.wait_ge(sV.h, g_ev[m])
            if m >= 1:
                s.wait_ge(sV.h, pr_ev[m - 1])
            ins = s.activation(mk.SB[:, 8192:10240], mk.SC[:, 0:2048], AF.Silu)
            ins.then_inc(sS.h, 1)
        mk.Sc(s)
        silu_d[m] = sS.inc()

        def t2(t, m=m, par=par):
            t.wait_ge(sW.h, wl_u[m])
            if m == 0:
                t.wait_ge(ha_done[0].h, ha_done[1])
            if m >= 1:
                t.wait_ge(sV.h, g_ev[m - 1])
            PB = mk.PB1 if par == 0 else mk.PB0
            ins = None
            for k in range(NK):
                for tb in range(4):
                    ins = t.matmul(PB[:, tb * 512:(tb + 1) * 512],
                                   mk.SB[:, 14336 + k * 128:14336 + (k + 1) * 128],
                                   mk.xbf(k, tb * 512, 512),
                                   start=(k == 0), stop=(k == NK - 1))
            ins.then_inc(sT.h, 1)
        mk.T(t2)
        up_mm[m] = sT.inc()

        def v2(v, m=m, par=par):
            v.wait_ge(sT.h, up_mm[m])
            v.wait_ge(sS.h, silu_d[m])
            if m >= 1:
                v.wait_ge(sO.h, pr_sp[m - 1])
            PB = mk.PB1 if par == 0 else mk.PB0
            ins = None
            for tb in range(4):
                ins = v.tensor_tensor(
                    mk.SB[:, 10240 + tb * 512:10240 + (tb + 1) * 512],
                    mk.SB[:, 8192 + tb * 512:8192 + (tb + 1) * 512],
                    PB[:, tb * 512:(tb + 1) * 512], ALU.mult)
            ins.then_inc(sV.h, 1)
        mk.V(v2)
        pr_ev[m] = sV.inc()

        def y(y, m=m):
            y.wait_ge(sV.h, pr_ev[m])
            y.dma_start(
                out=bass.AP(mk.d_spill2, m * 128 * 2048, [[2048, 128], [1, 2048]]),
                in_=mk.SB[:, 10240:12288]).then_inc(sO.h, 16)
        mk.Y(y)
        pr_sp[m] = sO.inc(16)
    mk.wbuf_tail = (sT, gl_mm[NMCH - 1])

    if stage <= 6:
        return (mk.sO, mk.sO.n)
    r_r2 = mk.bcast_from_dram(BNC_R, mk.BCF, 0, 2048,
                              (sV, pr_ev[NMCH - 1]),
                              extra_wait=(sS, silu_d[NMCH - 1]))
    def vr1(v):
        v.wait_ge(sX.h, r_r2[1])
        ins = v.reciprocal(mk.BCF[0:1, 0:2048], mk.BCF[0:1, 0:2048])
        ins.then_inc(sV.h, 1)
    mk.V(vr1)
    invrow = (sV, sV.inc())
    r_rln = mk.bcast_from_dram(BNC_RLN, mk.BCF, 2048, 2048, invrow)
    r_mu2 = mk.bcast_from_dram(BNC_MU, mk.SC, 0, 2048, r_rln)
    def vrow(v):
        v.wait_ge(sX.h, r_mu2[1])
        v.wait_ge(sT.h, up_mm[NMCH - 1])
        v.tensor_tensor(mk.SC[0:1, 2048:3072][0:1, 0:1024], mk.SC[0:1, 0:1024],
                        mk.BCF[0:1, 2048:3072], ALU.mult)
        v.tensor_tensor(mk.SC[0:1, 1024:2048], mk.SC[0:1, 1024:2048],
                        mk.BCF[0:1, 3072:4096], ALU.mult)
        v.scalar_tensor_tensor(mk.SB[0:1, 14336:15360], mk.SC[0:1, 2048:3072][0:1, 0:1024],
                               -1.0, mk.BCF[0:1, 0:1024], ALU.mult, ALU.mult)
        ins = v.scalar_tensor_tensor(mk.SB[0:1, 15360:16384], mk.SC[0:1, 1024:2048],
                                     -1.0, mk.BCF[0:1, 1024:2048], ALU.mult, ALU.mult)
        v.memset(mk.ROWS[:, 12:13], 0.0)
        ins = v.tensor_scalar(mk.BCF[:, 2048:4096], mk.BCF[:, 2048:4096],
                              1.0, None, ALU.add)
        ins.then_inc(sV.h, 1)
    mk.V(vrow)
    rows_done = (sV, sV.inc())

    dst_dram = mk.d_xout if last else mk.d_xres
    dn_wl, dn_mm, dn_mmg, dn_ev, dn_xl, dn_st, dn_cast, blk_ld = {}, {}, {}, {}, {}, {}, {}, {}
    for tb in range(4):
        def yb(y, tb=tb):
            if tb == 0:
                y.wait_ge(sO.h, pr_sp[NMCH - 1])
                y.wait_ge(att[0].h, att[1])
            else:
                y.wait_ge(sT.h, dn_mm[(tb - 1, 15)])
            for i in range(4):
                y.dma_start(
                    out=bass.AP(mk.SA, i * 16 * 512, [[32768, 128], [512, 16], [1, 512]]),
                    in_=bass.AP(mk.d_spill2, tb * 512 + i * 16 * 128 * 2048,
                                [[2048, 128], [128 * 2048, 16], [1, 512]]),
                ).then_inc(sX.h, 16)
        mk.Y(yb)
        blk_ld[tb] = sX.inc(64)

        for m in range(16):
            par = m % 2

            def yx(y, tb=tb, m=m, par=par):
                if tb == 0 and m < 2:
                    y.wait_ge(rows_done[0].h, rows_done[1])
                if (tb, m) not in [(0, 0), (0, 1)]:
                    pm = (tb, m - 2) if m >= 2 else (tb - 1, 14 + m)
                    y.wait_ge(sO.h, dn_st[pm])
                    y.wait_ge(sG.h, dn_cast[pm])
                y.dma_start(
                    out=mk.SC[:, 1024 + par * 512:1024 + (par + 1) * 512],
                    in_=bass.AP(mk.d_xres, m * 128 * S + tb * 512,
                                [[S, 128], [1, 512]]),
                ).then_inc(sX.h, 16)
            mk.Y(yx)
            dn_xl[(tb, m)] = sX.inc(16)

            for kg in range(4):
                u = tb * 64 + m * 4 + kg
                upar = u % 2

                def g(g, tb=tb, m=m, kg=kg, u=u, upar=upar):
                    if u < 2:
                        g.wait_ge(sT.h, up_mm[NMCH - 1])
                    else:
                        g.wait_ge(sT.h, dn_mmg[u - 2])
                    g.dma_start(
                        out=bass.AP(mk.WBUF, upar * 2048, [[4096, 128], [1, 2048]]),
                        in_=bass.AP(wdn, (m * 4 + kg) * 2048,
                                    [[131072, 128], [1, 2048]]),
                    ).then_inc(sW.h, 16)
                mk.G(g)
                dn_wl[u] = sW.inc(16)

                def t(t, tb=tb, m=m, kg=kg, u=u, upar=upar, par=par):
                    t.wait_ge(sW.h, dn_wl[u])
                    if kg == 0:
                        t.wait_ge(sX.h, blk_ld[tb])
                        if tb == 0 and m == 0:
                            t.wait_ge(rows_done[0].h, rows_done[1])
                        if (tb, m) not in [(0, 0), (0, 1)]:
                            pm = (tb, m - 2) if m >= 2 else (tb - 1, 14 + m)
                            t.wait_ge(sV.h, dn_ev[pm])
                    pt = (mk.PB0 if par == 0 else mk.PB1)[:, 0:512]
                    ins = None
                    for k in range(16):
                        ins = t.matmul(pt,
                                       mk.WBUF[:, upar * 2048 + k * 128:upar * 2048 + (k + 1) * 128],
                                       mk.SA[:, (kg * 16 + k) * 512:(kg * 16 + k + 1) * 512],
                                       start=(kg == 0 and k == 0), stop=False)
                    if kg == 3:
                        ins = t.matmul(pt, mk.CB["ones_row_b"][:],
                                       mk.SB[0:1, 14336 + tb * 512:14336 + (tb + 1) * 512],
                                       start=False, stop=True)
                    ins.then_inc(sT.h, 1)
                mk.T(t)
                dn_mmg[u] = sT.inc()
            dn_mm[(tb, m)] = dn_mmg[tb * 64 + m * 4 + 3]

            def v(v, tb=tb, m=m, par=par):
                v.wait_ge(sT.h, dn_mm[(tb, m)])
                v.wait_ge(sX.h, dn_xl[(tb, m)])
                pt = (mk.PB0 if par == 0 else mk.PB1)[:, 0:512]
                xs = mk.SC[:, 1024 + par * 512:1024 + (par + 1) * 512]
                v.scalar_tensor_tensor(mk.SC[:, 0:512], pt, 1.0,
                                       mk.BCR[:, tb * 512:(tb + 1) * 512],
                                       ALU.mult, ALU.mult)
                v.scalar_tensor_tensor(xs, mk.BCF[:, 2048 + tb * 512:2048 + (tb + 1) * 512],
                                       1.0, xs, ALU.mult, ALU.mult)
                v.memset(mk.ROWS[:, 12:13], 0.0)
                ins = v.tensor_tensor(xs, xs, mk.SC[:, 0:512], ALU.add)
                ins.then_inc(sV.h, 1)
            mk.V(v)
            dn_ev[(tb, m)] = sV.inc()

            def gc(g, tb=tb, m=m, par=par):
                g.wait_ge(sV.h, dn_ev[(tb, m)])
                ins = g.tensor_copy(mk.xbf(m, tb * 512, 512),
                                    mk.SC[:, 1024 + par * 512:1024 + (par + 1) * 512])
                ins.then_inc(sG.h, 1)
            mk.G(gc)
            dn_cast[(tb, m)] = sG.inc()

            def y2(y, tb=tb, m=m, par=par):
                y.wait_ge(sV.h, dn_ev[(tb, m)])
                y.dma_start(
                    out=bass.AP(dst_dram, m * 128 * S + tb * 512,
                                [[S, 128], [1, 512]]),
                    in_=mk.SC[:, 1024 + par * 512:1024 + (par + 1) * 512],
                ).then_inc(sO.h, 16)
            mk.Y(y2)
            dn_st[(tb, m)] = sO.inc(16)
    mk.xbf_ready = [(sG, dn_cast[(3, 15)])]
    mk.wbuf_tail = (sT, dn_mm[(3, 15)])
    return (sO, dn_st[(3, 15)])


def emit_rope_kappa(mk, tiles, cos_off, sin_off, pmat, kappa, waits):
    sT, sV, sS = mk.sT, mk.sV, mk.sS
    last_v = None
    for (tt, off) in tiles:
        def t(t, tt=tt, off=off, last_v=last_v):
            if last_v is None:
                for sm, val in waits:
                    t.wait_ge(sm.h, val)
            else:
                t.wait_ge(sV.h, last_v)
            ins = None
            for b in range(4):
                ins = t.matmul(mk.PB0[:, b * 512:(b + 1) * 512],
                               mk.CB[pmat][:],
                               tt[:, off + b * 512:off + (b + 1) * 512],
                               start=True, stop=True)
            ins.then_inc(sT.h, 1)
        mk.T(t)
        mm_d = mk.sT.inc()

        def v(v, tt=tt, off=off, mm_d=mm_d):
            v.wait_ge(sT.h, mm_d)
            v.scalar_tensor_tensor(mk.WBUF[:, 0:2048], mk.PB0[:, 0:2048],
                                   1.0, mk.BCF[:, sin_off:sin_off + 2048],
                                   ALU.mult, ALU.mult)
            v.scalar_tensor_tensor(mk.WBUF[:, 2048:4096],
                                   tt[:, off:off + 2048], 1.0,
                                   mk.BCF[:, cos_off:cos_off + 2048],
                                   ALU.mult, ALU.mult)
            v.memset(mk.ROWS[:, 12:13], 0.0)
            ins = v.tensor_tensor(mk.SC[:, 0:2048], mk.WBUF[:, 0:2048],
                                  mk.WBUF[:, 2048:4096], ALU.add)
            if not kappa:
                v.memset(mk.ROWS[:, 13:14], 0.0)
                ins = v.tensor_copy(tt[:, off:off + 2048], mk.SC[:, 0:2048])
            ins.then_inc(sV.h, 1)
        mk.V(v)
        rp_d = mk.sV.inc()

        if kappa:
            def v2(v, rp_d=rp_d):
                v.wait_ge(sV.h, rp_d)
                ins = v.tensor_scalar(mk.WBUF[:, 0:2048], mk.SC[:, 0:2048],
                                      0.0, None, ALU.min)
                ins.then_inc(sV.h, 1)
            mk.V(v2)
            m_d = mk.sV.inc()

            def s(s, m_d=m_d):
                s.wait_ge(sV.h, m_d)
                ins = s.activation(mk.WBUF[:, 2048:4096], mk.WBUF[:, 0:2048],
                                   AF.Exp)
                ins.then_inc(sS.h, 1)
            mk.Sc(s)
            e_d = mk.sS.inc()

            def v3(v, tt=tt, off=off, e_d=e_d):
                v.wait_ge(sS.h, e_d)
                v.tensor_scalar(mk.SC[:, 0:2048], mk.SC[:, 0:2048], 0.0,
                                None, ALU.max)
                v.memset(mk.ROWS[:, 12:13], 0.0)
                ins = v.tensor_tensor(tt[:, off:off + 2048], mk.SC[:, 0:2048],
                                      mk.WBUF[:, 2048:4096], ALU.add)
                ins.then_inc(sV.h, 1)
            mk.V(v3)
            last_v = mk.sV.inc()
        else:
            last_v = rp_d
    return (sV, last_v)


def sub_l1attna(mk, W, last):
    wq = mk.add_weight("L1a_wq", W["wq_l"])
    wk = mk.add_weight("L1a_wk", W["wk_l"])
    wv = mk.add_weight("L1a_wv", W["wv_l"])
    wph = mk.add_weight("L1a_wphi", W["wphi_l"])
    wo = mk.add_weight("L1a_wo", W["wo_l"])
    cos_h = mk.add_weight("L1a_cos", W["cos64"])
    sin_h = mk.add_weight("L1a_sin", W["sin64"])
    sW, sX, sT, sV, sS, sG, sO = (mk.sW, mk.sX, mk.sT, mk.sV, mk.sS, mk.sG,
                                  mk.sO)
    ln = mk.emit_stats_center(rms2=False, r32=True)

    def gcs(g):
        g.wait_ge(sG.h, ln["centered"][1])
        g.dma_start(out=mk.BCF[:, 0:2048], in_=cos_h.ap()).then_inc(sW.h, 16)
        g.dma_start(out=mk.BCF[:, 2048:4096], in_=sin_h.ap()).then_inc(sW.h, 16)
    mk.G(gcs)
    cs_ld = sW.inc(32)

    def gwv(g):
        g.wait_ge(sT.h, mk.ln["stats_T"][1])
        g.dma_start(
            out=bass.AP(mk.SA, 0, [[32768, 128], [512, NK], [1, 512]]),
            in_=bass.AP(wv, 0, [[512, 128], [128 * 512, NK], [1, 512]]),
        ).then_inc(sW.h, 16)
    mk.G(gwv)
    wv_ld = sW.inc(16)
    vp = mk.emit_vprod(0, 512, 8192, extra_waits=[(sW, wv_ld)])

    kp = mk.emit_proj(wk, [0, 1, 2, 3],
                      lambda ci, tb: mk.SB[:, ci * 2048 + tb * 512:ci * 2048 + (tb + 1) * 512],
                      extra_waits=[vp], r32=True, wb_w=True)
    qp = mk.emit_proj(wq, list(range(16)),
                      lambda ci, tb: mk.SA[:, ci * 2048 + tb * 512:ci * 2048 + (tb + 1) * 512],
                      extra_waits=[vp], wb_w=True)

    kr = emit_rope_kappa(mk, [(mk.SB, j * 2048) for j in range(4)],
                         0, 2048, "rope_p64", True,
                         waits=[kp["ev"], qp["ev"], (sW, cs_ld)])
    qr = emit_rope_kappa(mk, [(mk.SA, j * 2048) for j in range(16)],
                         0, 2048, "rope_p64", True, waits=[kr])

    def vqg(v):
        v.wait_ge(qr[0].h, qr[1])
        for j in range(16):
            v.reduce_sum(mk.ROWS[:, 224 + j:225 + j],
                         mk.SA[:, j * 2048:(j + 1) * 2048], axis=AX.X)
        v.memset(mk.ROWS[:, 12:13], 0.0)
        ins = v.tensor_copy(mk.WBUF[:, 0:16], mk.ROWS[:, 224:240])
        ins.then_inc(sV.h, 1)
    mk.V(vqg)
    qg_d = sV.inc()

    def tsw(t):
        t.wait_ge(sV.h, qg_d)
        t.matmul(mk.PB1[:, 1536:1552], mk.CB["pswap"][:], mk.WBUF[:, 0:16],
                 start=True, stop=True).then_inc(sT.h, 1)
    mk.T(tsw)
    sw_mm = sT.inc()

    def vsw(v):
        v.wait_ge(sT.h, sw_mm)
        ins = v.tensor_copy(mk.WBUF[:, 16:32], mk.PB1[:, 1536:1552])
        ins.then_inc(sV.h, 1)
    mk.V(vsw)
    qg_d = sV.inc()

    lg_mm, lg_ev, lg_st = {}, {}, {}
    for n in range(32):
        j = n // 2
        pq = (n % 2) * 64
        kvt = n // 8
        pk = ((n // 4) % 2) * 64

        def t(t, n=n, j=j, pq=pq, kvt=kvt, pk=pk):
            if n == 0:
                t.wait_ge(sV.h, qg_d)
            if n >= 1:
                t.wait_ge(sV.h, lg_ev[n - 1])
            ins = None
            for b in range(4):
                qcol = j if pq == pk else 16 + j
                ins = t.matmul(mk.PB1[0:1, b * 512:(b + 1) * 512],
                               mk.WBUF[pk:pk + 64, qcol:qcol + 1],
                               mk.SB[pk:pk + 64, kvt * 2048 + b * 512:kvt * 2048 + (b + 1) * 512],
                               start=True, stop=True)
            ins.then_inc(sT.h, 1)
        mk.T(t)
        lg_mm[n] = sT.inc()

        def v(v, n=n):
            v.wait_ge(sT.h, lg_mm[n])
            if n >= 1:
                v.wait_ge(sX.h, lg_st[n - 1])
            ins = v.tensor_scalar(mk.BCF[0:1, 0:2048], mk.PB1[0:1, 0:2048],
                                  1.0 / 2048.0, None, ALU.mult)
            ins.then_inc(sV.h, 1)
        mk.V(v)
        lg_ev[n] = sV.inc()

        def y(y, n=n):
            y.wait_ge(sV.h, lg_ev[n])
            y.dma_start(out=bass.AP(mk.d_lgm, n * 2048, [[1, 1], [1, 2048]]),
                        in_=mk.BCF[0:1, 0:2048]).then_inc(sX.h, 16)
        mk.Y(y)
        lg_st[n] = sX.inc(16)

    def ylg(y):
        y.wait_ge(sX.h, lg_st[31])
        y.dma_start(out=mk.SC[0:32, 0:2048],
                    in_=bass.AP(mk.d_lgm, 0, [[2048, 32], [1, 2048]])
                    ).then_inc(sX.h, 16)
    mk.Y(ylg)
    lgm_ld = sX.inc(16)

    def vsm(v):
        v.wait_ge(sX.h, lgm_ld)
        ins = v.reduce_max(mk.SC[0:32, 2048:2049], mk.SC[0:32, 0:2048],
                           axis=AX.X, negate=True)
        ins.then_inc(sV.h, 1)
    mk.V(vsm)
    sm1 = sV.inc()

    def ssm(s):
        s.wait_ge(sV.h, sm1)
        ins = s.activation(mk.BCF[0:32, 0:2048], mk.SC[0:32, 0:2048], AF.Exp,
                           bias=mk.SC[0:32, 2048:2049],
                           accum_out=mk.SC[0:32, 2052:2053])
        ins.then_inc(sS.h, 1)
    mk.Sc(ssm)
    sm2 = sS.inc()

    def vsm2(v):
        v.wait_ge(sS.h, sm2)
        v.reciprocal(mk.SC[0:32, 2056:2057], mk.SC[0:32, 2052:2053])
        v.memset(mk.ROWS[:, 12:13], 0.0)
        v.tensor_scalar(mk.SC[0:32, 2060:2061], mk.SC[0:32, 2056:2057],
                        2048.0, None, ALU.mult)
        v.memset(mk.ROWS[:, 13:14], 0.0)
        inv_b = bass.AP(mk.SC, 2060, [[3072, 32], [0, 2048]])
        ins = v.tensor_tensor(mk.SC[0:32, 0:2048], mk.BCF[0:32, 0:2048],
                              inv_b, ALU.mult)
        ins.then_inc(sV.h, 1)
    mk.V(vsm2)
    al_d = sV.inc()

    ac_d = {}
    for sc in range(16):
        def t(t, sc=sc):
            t.wait_ge(sV.h, al_d)
            if sc >= 2:
                t.wait_ge(sV.h, ac_d[sc - 2])
            t.transpose(mk.PB1[:, (sc % 2) * 512:(sc % 2) * 512 + 32],
                        mk.SC[0:32, sc * 128:(sc + 1) * 128],
                        mk.CB["ident"][0:32, 0:32]).then_inc(sT.h, 1)
        mk.T(t)
        tr = sT.inc()

        def v(v, sc=sc, tr=tr):
            v.wait_ge(sT.h, tr)
            ins = v.tensor_copy(mk.BCF[:, sc * 32:(sc + 1) * 32],
                                mk.PB1[:, (sc % 2) * 512:(sc % 2) * 512 + 32])
            ins.then_inc(sV.h, 1)
        mk.V(v)
        ac_d[sc] = sV.inc()
    acol_done = (sV, ac_d[15])

    kkt_d = {}
    kkt_tr = {}
    for j in range(4):
        for c in range(16):
            idx = j * 16 + c

            def vc(v, j=j, c=c, idx=idx):
                if idx == 0:
                    v.wait_ge(acol_done[0].h, acol_done[1])
                if idx >= 2:
                    v.wait_ge(sT.h, kkt_tr[idx - 2])
                ins = v.tensor_copy(
                    mk.SC[:, 2048 + (idx % 2) * 128:2176 + (idx % 2) * 128],
                    mk.SB[:, j * 2048 + c * 128:j * 2048 + (c + 1) * 128])
                ins.then_inc(sV.h, 1)
            mk.V(vc)
            cp = sV.inc()

            def t(t, j=j, c=c, idx=idx, cp=cp):
                t.wait_ge(sV.h, cp)
                if idx >= 2:
                    t.wait_ge(sV.h, kkt_d[idx - 2])
                t.transpose(mk.PB0[:, (idx % 2) * 512:(idx % 2) * 512 + 128],
                            mk.SC[:, 2048 + (idx % 2) * 128:2176 + (idx % 2) * 128],
                            mk.CB["ident"][:]).then_inc(sT.h, 1)
            mk.T(t)
            kkt_tr[idx] = sT.inc()

            def v(v, j=j, c=c, idx=idx):
                v.wait_ge(sT.h, kkt_tr[idx])
                ins = v.tensor_copy(
                    mk.SB[:, j * 2048 + c * 128:j * 2048 + (c + 1) * 128],
                    mk.PB0[:, (idx % 2) * 512:(idx % 2) * 512 + 128])
                ins.then_inc(sV.h, 1)
            mk.V(v)
            kkt_d[idx] = sV.inc()
    kkt_done = (sV, kkt_d[63])

    res_done = {}
    sc_cnt = {}
    for n in range(32):
        j = n // 2
        kvh = n // 4
        kvt = kvh // 2
        kc = (kvh % 2) * 64
        vc = kvh * 64
        nbase = (n % 2) * 64
        mm_c = {}
        for c in range(16):
            def vsc(v, n=n, c=c, kvt=kvt, kc=kc):
                if n == 0 and c == 0:
                    v.wait_ge(kkt_done[0].h, kkt_done[1])
                if (n, c) > (0, 3):
                    pc = (n, c - 4) if c >= 4 else (n - 1, 12 + c)
                    v.wait_ge(sT.h, sc_cnt[pc])
                ins = v.scalar_tensor_tensor(
                    mk.WBUF[:, 2048 + (c % 4) * 64:2048 + (c % 4) * 64 + 64],
                    mk.SB[:, kvt * 2048 + c * 128 + kc:kvt * 2048 + c * 128 + kc + 64],
                    1.0,
                    bass.AP(mk.BCF, c * 32 + n, [[4096, 128], [0, 64]]),
                    ALU.mult, ALU.mult)
                ins.then_inc(sV.h, 1)
            mk.V(vsc)
            scv = sV.inc()

            def t(t, n=n, c=c, vc=vc, scv=scv, nbase=nbase):
                t.wait_ge(sV.h, scv)
                if c == 0 and n >= 2:
                    t.wait_ge(sV.h, res_done[n - 2])
                t.matmul(mk.PB1[nbase:nbase + 64, 1536 + (n % 2) * 64:1600 + (n % 2) * 64],
                         mk.WBUF[:, 2048 + (c % 4) * 64:2048 + (c % 4) * 64 + 64],
                         mk.SB[:, 8192 + c * 512 + vc:8192 + c * 512 + vc + 64],
                         start=(c == 0), stop=(c == 15),
                         tile_position=(0, nbase)).then_inc(sT.h, 1)
            mk.T(t)
            sc_cnt[(n, c)] = sT.inc()
        ou_mm = sc_cnt[(n, 15)]

        def vpo(v, n=n, ou_mm=ou_mm, nbase=nbase):
            v.wait_ge(sT.h, ou_mm)
            v.memset(mk.WBUF[:, (n % 2) * 64:(n % 2) * 64 + 64], 0.0)
            v.memset(mk.ROWS[:, 12:13], 0.0)
            ins = v.tensor_copy(
                mk.WBUF[nbase:nbase + 64, (n % 2) * 64:(n % 2) * 64 + 64],
                mk.PB1[nbase:nbase + 64, 1536 + (n % 2) * 64:1600 + (n % 2) * 64])
            ins.then_inc(sV.h, 1)
        mk.V(vpo)
        po_d = sV.inc()

        def t2(t, n=n, j=j, po_d=po_d, nbase=nbase):
            t.wait_ge(sV.h, po_d)
            if n >= 2:
                t.wait_ge(sV.h, res_done[n - 2])
            for b in range(4):
                ins = t.matmul(mk.PB0[nbase:nbase + 64, b * 512:(b + 1) * 512],
                               mk.WBUF[:, (n % 2) * 64:(n % 2) * 64 + 64],
                               mk.SA[:, j * 2048 + b * 512:j * 2048 + (b + 1) * 512],
                               start=True, stop=True,
                               tile_position=(0, nbase))
            ins.then_inc(sT.h, 1)
        mk.T(t2)
        rs_mm = sT.inc()

        if n % 2 == 0:
            res_done[n] = None
            rs_even = rs_mm
        else:
            def vrs(v, n=n, j=j, rs_mm=rs_mm, rs_even=rs_even):
                v.wait_ge(sT.h, rs_mm)
                ins = v.tensor_copy(mk.SA[:, j * 2048:(j + 1) * 2048],
                                    mk.PB0[:, 0:2048])
                ins.then_inc(sV.h, 1)
            mk.V(vrs)
            res_done[n] = sV.inc()
            res_done[n - 1] = res_done[n]
    res_all = (sV, res_done[31])

    ph_mm, ph_ev, ph_wl = {}, {}, {}
    wbw = mk.wbuf_tail
    for ci in range(16):
        par = ci % 2

        def g(g, ci=ci, par=par):
            if ci < 2:
                g.wait_ge(res_all[0].h, res_all[1])
            else:
                g.wait_ge(sT.h, ph_mm[ci - 2])
            g.dma_start(
                out=bass.AP(mk.WBUF, par * 2048, [[4096, 128], [1, 2048]]),
                in_=bass.AP(wph, ci * 2048, [[32768, 128], [1, 2048]]),
            ).then_inc(sW.h, 16)
        mk.G(g)
        ph_wl[ci] = sW.inc(16)

        def t(t, ci=ci, par=par):
            t.wait_ge(sW.h, ph_wl[ci])
            if ci == 0:
                t.wait_ge(res_all[0].h, res_all[1])
            if ci >= 2:
                t.wait_ge(sV.h, ph_ev[ci - 2])
            PB = mk.PB0 if par == 0 else mk.PB1
            ins = None
            for k in range(NK):
                for tb in range(4):
                    ins = t.matmul(PB[:, tb * 512:(tb + 1) * 512],
                                   mk.WBUF[:, par * 2048 + k * 128:par * 2048 + (k + 1) * 128],
                                   mk.xbf(k, tb * 512, 512),
                                   start=(k == 0), stop=(k == NK - 1))
            ins.then_inc(sT.h, 1)
        mk.T(t)
        ph_mm[ci] = sT.inc()

        def v(v, ci=ci, par=par):
            v.wait_ge(sT.h, ph_mm[ci])
            PB = mk.PB0 if par == 0 else mk.PB1
            for tb in range(4):
                v.scalar_tensor_tensor(
                    mk.SC[:, 2048 + (tb % 2) * 512:2048 + (tb % 2 + 1) * 512],
                    PB[:, tb * 512:(tb + 1) * 512], 1.0,
                    mk.BCR[:, tb * 512:(tb + 1) * 512], ALU.mult, ALU.mult)
                if tb >= 1:
                    v.tensor_tensor(
                        mk.SA[:, ci * 2048 + (tb - 1) * 512:ci * 2048 + tb * 512],
                        mk.SA[:, ci * 2048 + (tb - 1) * 512:ci * 2048 + tb * 512],
                        mk.SC[:, 2048 + ((tb - 1) % 2) * 512:2048 + ((tb - 1) % 2 + 1) * 512],
                        ALU.mult)
            v.memset(mk.ROWS[:, 12:13], 0.0)
            ins = v.tensor_tensor(
                mk.SA[:, ci * 2048 + 3 * 512:ci * 2048 + 4 * 512],
                mk.SA[:, ci * 2048 + 3 * 512:ci * 2048 + 4 * 512],
                mk.SC[:, 2048 + 512:2048 + 1024], ALU.mult)
            ins.then_inc(sV.h, 1)
        mk.V(v)
        ph_ev[ci] = sV.inc()
    mk.wbuf_tail = (sT, ph_mm[15])

    return mk.emit_out_proj_residual(wo, last, attn_wait=(sV, ph_ev[15]),
                                     wb_w=True)


def sub_l1ffn(mk, W, last):
    wa = mk.add_weight("L1f_A", W["A_l"])
    wb = mk.add_weight("L1f_Bw", W["Bw_l"])
    wg = mk.add_weight("L1f_Wg", W["Wg_l"])
    sW, sX, sT, sV, sS, sG, sO = (mk.sW, mk.sX, mk.sT, mk.sV, mk.sS, mk.sG,
                                  mk.sO)
    ln = mk.emit_stats_center(rms2=False)

    hp = mk.emit_proj(wa, [0, 1, 2, 3],
                      lambda ci, tb: mk.SB[:, ci * 2048 + tb * 512:ci * 2048 + (tb + 1) * 512],
                      wb_w=True)

    def gwg(g):
        g.dma_start(out=bass.AP(mk.SB, 10240, [[16384, 128], [8, NK], [1, 8]]),
                    in_=bass.AP(wg, 0, [[8, 128], [128 * 8, NK], [1, 8]]),
                    ).then_inc(sW.h, 16)
    mk.G(gwg)
    wg_ld = sW.inc(16)

    gt_mm = {}
    for tt in range(16):
        def t(t, tt=tt):
            t.wait_ge(sW.h, wg_ld)
            if tt == 0:
                t.wait_ge(sG.h, ln["centered"][1])
                t.wait_ge(hp["ev"][0].h, hp["ev"][1])
            ins = None
            for k in range(NK):
                ins = t.matmul(mk.PB1[:, 1536 + tt * 8:1544 + tt * 8],
                               mk.xbf(k, tt * 128, 128),
                               mk.SB[:, 10240 + k * 8:10240 + (k + 1) * 8],
                               start=(k == 0), stop=(k == NK - 1))
            ins.then_inc(sT.h, 1)
        mk.T(t)
        gt_mm[tt] = sT.inc()

    def vg1(v):
        v.wait_ge(sT.h, gt_mm[15])
        v.wait_ge(sV.h, ln["rcol"][1])
        rcol_b = bass.AP(mk.ROWS, 32, [[256, 128], [1, 16], [0, 8]])
        v.tensor_tensor(bass.AP(mk.SC, 0, [[3072, 128], [8, 16], [1, 8]]),
                        bass.AP(mk.PB1, 1536, [[2048, 128], [8, 16], [1, 8]]),
                        rcol_b, ALU.mult)
        v.memset(mk.ROWS[:, 12:13], 0.0)
        v.reduce_max(mk.SC[:, 128:144],
                     bass.AP(mk.SC, 0, [[3072, 128], [8, 16], [1, 8]]),
                     axis=AX.X, negate=True)
        v.memset(mk.ROWS[:, 13:14], 0.0)
        nm_b = bass.AP(mk.SC, 128, [[3072, 128], [1, 16], [0, 8]])
        ins = v.tensor_tensor(bass.AP(mk.SC, 256, [[3072, 128], [8, 16], [1, 8]]),
                              bass.AP(mk.SC, 0, [[3072, 128], [8, 16], [1, 8]]),
                              nm_b, ALU.add)
        ins.then_inc(sV.h, 1)
    mk.V(vg1)
    g1_d = sV.inc()

    def sg1(s):
        s.wait_ge(sV.h, g1_d)
        ins = s.activation(mk.SC[:, 384:512], mk.SC[:, 256:384], AF.Exp)
        ins.then_inc(sS.h, 1)
    mk.Sc(sg1)
    ge_d = sS.inc()

    def vg2(v):
        v.wait_ge(sS.h, ge_d)
        v.reduce_sum(mk.SC[:, 144:160],
                     bass.AP(mk.SC, 384, [[3072, 128], [8, 16], [1, 8]]),
                     axis=AX.X)
        v.tensor_scalar(mk.SC[:, 512:640], mk.SC[:, 256:384], 0.0,
                        None, ALU.is_ge)
        ins = v.reciprocal(mk.SC[:, 160:176], mk.SC[:, 144:160])
        ins.then_inc(sV.h, 1)
    mk.V(vg2)
    g2_d = sV.inc()

    gt_sel = {}
    for tt in range(16):
        def t2(t, tt=tt):
            t.wait_ge(sV.h, g2_d)
            if tt >= 1:
                t.wait_ge(sV.h, gt_sel[tt - 1])
            t.transpose(mk.PB0[0:8, 1024:1152], mk.SC[:, 512 + tt * 8:520 + tt * 8],
                        mk.CB["ident"][:]).then_inc(sT.h, 1)
        mk.T(t2)
        tr_d = sT.inc()

        def v3(v, tt=tt, tr_d=tr_d):
            v.wait_ge(sT.h, tr_d)
            ins = v.tensor_copy(mk.SB[0:8, 10496 + tt * 128:10496 + (tt + 1) * 128],
                                mk.PB0[0:8, 1024:1152])
            ins.then_inc(sV.h, 1)
        mk.V(v3)
        gt_sel[tt] = sV.inc()

    def t3(t):
        t.wait_ge(sV.h, gt_sel[15])
        t.transpose(mk.PB0[0:16, 1024:1152], mk.SC[:, 160:176],
                    mk.CB["ident"][:]).then_inc(sT.h, 1)
    mk.T(t3)
    pst = sT.inc()

    def v4(v):
        v.wait_ge(sT.h, pst)
        v.tensor_copy(mk.ROWS[0:16, 64:192], mk.PB0[0:16, 1024:1152]
                      ).then_inc(sV.h, 1)
    mk.V(v4)
    psev = sV.inc()
    wps = mk.row_to_dram(mk.ROWS, 64, BNC_AUX, (sV, psev), part16=True)
    ps_tile = mk.bcast_from_dram(BNC_AUX, mk.BCF, 0, 2048, wps)

    msk_mm, msk_d = {}, {}
    for c in range(4):
        for tb in range(4):
            def t4(t, c=c, tb=tb):
                if c == 0 and tb == 0:
                    t.wait_ge(hp["ev"][0].h, hp["ev"][1])
                t.wait_ge(sV.h, gt_sel[min(15, tb * 4 + 3)])
                idx = c * 4 + tb
                if idx >= 2:
                    pc = ((idx - 2) // 4, (idx - 2) % 4)
                    t.wait_ge(sV.h, msk_d[pc])
                t.matmul(mk.PB0[:, (tb % 2) * 512:(tb % 2) * 512 + 512],
                         mk.CB["epat"][:, c * 128:(c + 1) * 128],
                         mk.SB[0:8, 10496 + tb * 512:10496 + (tb + 1) * 512],
                         start=True, stop=True).then_inc(sT.h, 1)
            mk.T(t4)
            msk_mm[(c, tb)] = sT.inc()

            def v5(v, c=c, tb=tb):
                v.wait_ge(sT.h, msk_mm[(c, tb)])
                ins = v.tensor_tensor(
                    mk.SB[:, c * 2048 + tb * 512:c * 2048 + (tb + 1) * 512],
                    mk.SB[:, c * 2048 + tb * 512:c * 2048 + (tb + 1) * 512],
                    mk.PB0[:, (tb % 2) * 512:(tb % 2) * 512 + 512], ALU.mult)
                ins.then_inc(sV.h, 1)
            mk.V(v5)
            msk_d[(c, tb)] = sV.inc()

    st = mk.emit_out_proj_residual(
        wb, last, scale=4.0,
        src_chunk_fn=lambda k, tb: mk.SB[:, k * 2048 + tb * 512:k * 2048 + (tb + 1) * 512],
        n_in_chunks=4, attn_wait=(sV, msk_d[(3, 3)]),
        mult_tile=(mk.BCF, 0), wb_w=True)
    return st


SELCFG = True
SELCFG2 = True
NH_DBG = 16


def sub_l1attnb(mk, W, last, stage=99):
    wq = mk.add_weight("L1b_wq", W["wq_l"])
    wk = mk.add_weight("L1b_wk", W["wk_l"])
    wv = mk.add_weight("L1b_wv", W["wv_l"])
    wo = mk.add_weight("L1b_wo", W["wo_l"])
    cos_h = mk.add_weight("L1b_cos", W["cos128"])
    sin_h = mk.add_weight("L1b_sin", W["sin128"])
    onw_h = mk.add_weight("L1b_onw", W["onw"])
    sW, sX, sT, sV, sS, sG, sO = (mk.sW, mk.sX, mk.sT, mk.sV, mk.sS, mk.sG,
                                  mk.sO)
    ln = mk.emit_stats_center(rms2=False)
    VSP_off = 2048 * 2048

    def gcs(g):
        g.wait_ge(sG.h, ln["centered"][1])
        g.dma_start(out=mk.BCF[:, 0:2048], in_=cos_h.ap()).then_inc(sW.h, 16)
        g.dma_start(out=mk.BCF[:, 2048:4096], in_=sin_h.ap()).then_inc(sW.h, 16)
    mk.G(gcs)
    cs_ld = sW.inc(32)

    def yonw(y):
        y.wait_ge(sG.h, ln["centered"][1])
        y.dma_start(out=mk.LNC[:, 0:1], in_=onw_h.ap()).then_inc(sX.h, 16)
    mk.Y(yonw)
    onw_ld = sX.inc(16)

    def gwv(g):
        g.wait_ge(sT.h, mk.ln["stats_T"][1])
        for i in range(4):
            g.dma_start(
                out=bass.AP(mk.SA, i * 4 * 2048, [[32768, 128], [2048, 4], [1, 2048]]),
                in_=bass.AP(wv, i * 4 * 128 * 2048,
                            [[2048, 128], [128 * 2048, 4], [1, 2048]]),
            ).then_inc(sW.h, 16)
    mk.G(gwv)
    wv_ld = sW.inc(64)
    vp = mk.emit_vprod(0, 2048, 0, extra_waits=[(sW, wv_ld)],
                       spill=(mk.d_spill, VSP_off), w_tensor=mk.SA, stg_off=0)

    kp = mk.emit_proj(wk, list(range(16)),
                      lambda ci, tb: mk.SA[:, ci * 2048 + tb * 512:ci * 2048 + (tb + 1) * 512],
                      extra_waits=[vp], wb_w=True)
    kr = emit_rope_kappa(mk, [(mk.SA, j * 2048) for j in range(16)],
                         0, 2048, "rope_p128", False,
                         waits=[kp["ev"], (sW, cs_ld)])

    def vkm(v):
        v.wait_ge(kr[0].h, kr[1])
        for h in range(16):
            v.reduce_sum(mk.SC[:, 2048 + h * 8:2048 + h * 8 + 8],
                         bass.AP(mk.SA, h * 2048, [[32768, 128], [256, 8], [1, 256]]),
                         axis=AX.X)
        v.memset(mk.ROWS[:, 12:13], 0.0)
        ins = v.tensor_copy(mk.SB[:, 14336:14464], mk.SC[:, 2048:2176])
        ins.then_inc(sV.h, 1)
    mk.V(vkm)
    km_d = sV.inc()

    ksp_d = {}
    for h in range(16):
        def yk(y, h=h):
            if h == 0:
                y.wait_ge(kr[0].h, kr[1])
            y.dma_start(out=bass.AP(mk.d_spill, h * 128 * 2048,
                                    [[2048, 128], [1, 2048]]),
                        in_=mk.SA[:, h * 2048:(h + 1) * 2048]).then_inc(sO.h, 16)
        mk.Y(yk)
        ksp_d[h] = sO.inc(16)
    ksp = (sO, ksp_d[15])

    mk.wbuf_tail = kr
    qp = mk.emit_proj(wq, list(range(16)),
                      lambda ci, tb: mk.SA[:, ci * 2048 + tb * 512:ci * 2048 + (tb + 1) * 512],
                      scale=1.0 / math.sqrt(128.0),
                      extra_waits=[ksp, (sV, km_d)], wb_w=True)
    mk.dbg_qp = qp["ev"]
    qr = emit_rope_kappa(mk, [(mk.SA, j * 2048) for j in range(16)],
                         0, 2048, "rope_p128", False, waits=[qp["ev"]])
    mk.dbg_qr = qr

    def vfsm(v):
        v.wait_ge(sG.h, ln["centered"][1])
        v.memset(mk.SB[:, 14464:14592], 0.0)
        v.memset(mk.ROWS[:, 12:13], 0.0)
        ins = None
        for tc in range(16):
            bq = tc // 2
            if bq + 1 < 8:
                ins = v.memset(mk.SB[:, 14464 + tc * 8 + bq + 1:
                                     14464 + (tc + 1) * 8], NEG)
            ins = v.memset(mk.SB[:, 14464 + tc * 8 + bq:
                                 14464 + tc * 8 + bq + 1], -NEG)
        ins.then_inc(sV.h, 1)
    mk.V(vfsm)
    fsm_d = sV.inc()

    g_mm, g_ev = {}, {}
    for h in range(16):
        def tg(t, h=h):
            if h == 0:
                t.wait_ge(qr[0].h, qr[1])
            if h >= 2:
                t.wait_ge(sV.h, g_ev[h - 2])
            ins = None
            for tc in range(16):
                ins = t.matmul(
                    mk.PB0[:, (h % 2) * 512 + tc * 8:(h % 2) * 512 + tc * 8 + 8],
                    mk.SA[:, h * 2048 + tc * 128:h * 2048 + (tc + 1) * 128],
                    mk.SB[:, 14336 + h * 8:14336 + h * 8 + 8],
                    start=True, stop=True)
            ins.then_inc(sT.h, 1)
        mk.T(tg)
        g_mm[h] = sT.inc()

        def vg(v, h=h):
            v.wait_ge(sT.h, g_mm[h])
            if h == 0:
                v.wait_ge(sV.h, fsm_d)
            ins = v.scalar_tensor_tensor(
                bass.AP(mk.SC, h * 8, [[3072, 128], [128, 16], [1, 8]]),
                mk.PB0[:, (h % 2) * 512:(h % 2) * 512 + 128], 1.0,
                mk.SB[:, 14464:14592], ALU.mult, ALU.add)
            ins.then_inc(sV.h, 1)
        mk.V(vg)
        g_ev[h] = sV.inc()

    def vsel(v):
        v.wait_ge(sV.h, g_ev[15])
        v.tensor_copy(mk.BCF[:, 0:2048], mk.SC[:, 0:2048])
        v.memset(mk.ROWS[:, 12:13], 0.0)
        for it in range(4):
            v.reduce_max(mk.BCF[:, 2048 + it * 256:2048 + (it + 1) * 256],
                         bass.AP(mk.BCF, 0, [[4096, 128], [8, 256], [1, 8]]),
                         axis=AX.X)
            v.memset(mk.ROWS[:, 12:13], 0.0)
            if it < 3:
                v.tensor_tensor(mk.WBUF[:, 0:2048], mk.BCF[:, 0:2048],
                                bass.AP(mk.BCF, 2048 + it * 256,
                                        [[4096, 128], [1, 256], [0, 8]]),
                                ALU.is_equal)
                v.memset(mk.ROWS[:, 13:14], 0.0)
                v.scalar_tensor_tensor(mk.BCF[:, 0:2048], mk.WBUF[:, 0:2048],
                                       -1e32, mk.BCF[:, 0:2048],
                                       ALU.mult, ALU.add)
                v.memset(mk.ROWS[:, 12:13], 0.0)
        v.tensor_tensor(mk.BCF[:, 0:2048], mk.SC[:, 0:2048],
                        bass.AP(mk.BCF, 2048 + 3 * 256,
                                [[4096, 128], [1, 256], [0, 8]]),
                        ALU.is_ge)
        v.memset(mk.ROWS[:, 13:14], 0.0)
        v.tensor_scalar(mk.WBUF[:, 0:2048], mk.SC[:, 0:2048], -5e29,
                        None, ALU.is_gt)
        v.memset(mk.ROWS[:, 12:13], 0.0)
        v.tensor_tensor(mk.SC[:, 0:2048], mk.BCF[:, 0:2048],
                        mk.WBUF[:, 0:2048], ALU.mult)
        v.memset(mk.ROWS[:, 13:14], 0.0)
        ins = v.tensor_scalar(mk.SB[:, 10240:12288], mk.SC[:, 0:2048],
                              1e15, -1e15, ALU.mult, ALU.add)
        ins.then_inc(sV.h, 1)
    mk.V(vsel)
    sel_d = sV.inc()
    mk.dbg_sel = sel_d
    if stage <= 1:
        return mk.emit_out_proj_residual(wo, last, attn_wait=(sV, sel_d))

    qv_ld = {}

    def emit_kv_load(hh, waits):
        def y0(y, hh=hh, waits=waits):
            for sm, val in waits:
                y.wait_ge(sm.h, val)
            y.dma_start(
                out=mk.SB[:, (hh % 2) * 2048:(hh % 2) * 2048 + 2048],
                in_=bass.AP(mk.d_spill, hh * 128 * 2048,
                            [[2048, 128], [1, 2048]])).then_inc(sX.h, 16)
            y.dma_start(
                out=mk.SB[:, 4096 + (hh % 2) * 2048:4096 + (hh % 2) * 2048 + 2048],
                in_=bass.AP(mk.d_spill, VSP_off + hh * 128,
                            [[2048, 128], [128 * 2048, 16], [1, 128]])
            ).then_inc(sX.h, 16)
        mk.Y(y0)
        qv_ld[hh] = sX.inc(32)

    def emit_selt8(h, prev_tail):
        tr_d, ev_d = {}, {}
        for tc in range(16):
            def t5(t, h=h, tc=tc, prev_tail=prev_tail):
                if tc == 0:
                    t.wait_ge(sV.h, sel_d)
                    if prev_tail is not None:
                        t.wait_ge(prev_tail[0].h, prev_tail[1])
                if tc >= 2:
                    t.wait_ge(sV.h, ev_d[tc - 2])
                t.matmul(mk.PB0[0:8, 1024 + (tc % 2) * 512:1024 + (tc % 2) * 512 + 128],
                         mk.SC[:, tc * 128 + h * 8:tc * 128 + h * 8 + 8],
                         mk.CB["ident"][:],
                         start=True, stop=True).then_inc(sT.h, 1)
            mk.T(t5)
            tr_d[tc] = sT.inc()

            def v5(v, h=h, tc=tc):
                v.wait_ge(sT.h, tr_d[tc])
                if tc == 15:
                    v.wait_ge(sX.h, qv_ld[h])
                ins = v.tensor_copy(
                    mk.SB[0:8, 8192 + tc * 128:8192 + (tc + 1) * 128],
                    mk.PB0[0:8, 1024 + (tc % 2) * 512:1024 + (tc % 2) * 512 + 128])
                ins.then_inc(sV.h, 1)
            mk.V(v5)
            ev_d[tc] = sV.inc()
        return (sV, ev_d[15])

    def head_pre(h, prev_tail):
        if h == 0:
            emit_kv_load(0, [ksp, vp, (sV, sel_d)])
        if h + 1 < 16:
            emit_kv_load(h + 1, [prev_tail] if prev_tail is not None
                         else [ksp, vp, (sV, sel_d)])
        if SELCFG:
            return emit_selt8(h, prev_tail)
        return (sX, qv_ld[h])

    att = mk.emit_attn_core(
        nheads=NH_DBG,
        kt_tensor=mk.SB, kt_off_fn=lambda h: (h % 2) * 2048,
        v_lhsT_fn=lambda h, kt: mk.SB[:, 4096 + (h % 2) * 2048 + kt * 128:
                                      4096 + (h % 2) * 2048 + (kt + 1) * 128],
        qkv_wait=(sV, sel_d), head_pre=head_pre, sel_cfg=(SELCFG and SELCFG2))
    if stage <= 2:
        return mk.emit_out_proj_residual(wo, last, attn_wait=att)

    on_d = {}
    for h in range(16):
        def s5(s, h=h):
            if h == 0:
                s.wait_ge(att[0].h, att[1])
            else:
                s.wait_ge(sV.h, on_d[h - 1])
            ins = s.activation(mk.WBUF[:, (h % 2) * 2048:(h % 2) * 2048 + 2048],
                               mk.SA[:, h * 2048:(h + 1) * 2048], AF.Square)
            ins.then_inc(sS.h, 1)
        mk.Sc(s5)
        sq_d = sS.inc()

        def t6(t, h=h, sq_d=sq_d):
            t.wait_ge(sS.h, sq_d)
            ins = None
            for b in range(4):
                ins = t.matmul(mk.PB1[0:1, b * 512:(b + 1) * 512],
                               mk.CB["ones_col"][:],
                               mk.WBUF[:, (h % 2) * 2048 + b * 512:
                                       (h % 2) * 2048 + (b + 1) * 512],
                               start=True, stop=True)
            ins.then_inc(sT.h, 1)
        mk.T(t6)
        mmo = sT.inc()

        def v7(v, h=h, mmo=mmo):
            v.wait_ge(sT.h, mmo)
            ins = v.tensor_scalar(mk.SC[0:1, 0:2048], mk.PB1[0:1, 0:2048],
                                  1.0 / 128.0, None, ALU.mult)
            ins.then_inc(sV.h, 1)
        mk.V(v7)
        mean_d = sV.inc()

        def s6(s, h=h, mean_d=mean_d):
            s.wait_ge(sV.h, mean_d)
            s.activation(mk.BCF[0:1, (h % 2) * 2048:(h % 2) * 2048 + 2048],
                         mk.SC[0:1, 0:2048], AF.Ln,
                         bias=mk.CB["eps6"][0:1, 0:1])
            ins = s.activation(mk.SC[0:1, 0:2048],
                               mk.BCF[0:1, (h % 2) * 2048:(h % 2) * 2048 + 2048],
                               AF.Exp, scale=-0.5)
            ins.then_inc(sS.h, 1)
        mk.Sc(s6)
        rso_d = sS.inc()

        w_on = mk.row_to_dram(mk.SC, 0, 6 + h % 2, (sS, rso_d))
        on_tile = mk.bcast_from_dram(6 + h % 2, mk.BCF, (h % 2) * 2048, 2048,
                                     w_on)

        def v8(v, h=h, on_tile=on_tile):
            v.wait_ge(sX.h, on_tile[1])
            v.wait_ge(sX.h, onw_ld)
            v.tensor_scalar(mk.BCF[:, (h % 2) * 2048:(h % 2) * 2048 + 2048],
                            mk.BCF[:, (h % 2) * 2048:(h % 2) * 2048 + 2048],
                            mk.LNC[:, 0:1], None, ALU.mult)
            v.memset(mk.ROWS[:, 12:13], 0.0)
            ins = v.scalar_tensor_tensor(
                mk.SA[:, h * 2048:(h + 1) * 2048],
                mk.SA[:, h * 2048:(h + 1) * 2048], 1.0,
                mk.BCF[:, (h % 2) * 2048:(h % 2) * 2048 + 2048],
                ALU.mult, ALU.mult)
            ins.then_inc(sV.h, 1)
        mk.V(v8)
        on_d[h] = sV.inc()

    return mk.emit_out_proj_residual(wo, last, attn_wait=(sV, on_d[15]),
                                     wb_w=True)


def build_ffn_only(weights_np):
    mk = MK()
    sub_ffn(mk, weights_np, last=True)
    mk.finish()
    return mk


def build_attn_b_only(weights_np):
    mk = MK()
    sub_attn_b(mk, weights_np, last=True)
    mk.finish()
    return mk


def _wb(w, n_in=16):
    w = np.asarray(w, dtype=np.float32)
    r, c = w.shape
    assert r == n_in * 128
    return np.ascontiguousarray(
        w.reshape(n_in, 128, c // 128, 128).transpose(1, 2, 0, 3)
         .reshape(128, (c // 128) * n_in * 128))


def kernel(**inputs):
    T = lambda a: np.ascontiguousarray(np.asarray(a, dtype=np.float32).T)
    A_ = lambda a: np.ascontiguousarray(np.asarray(a, dtype=np.float32))
    x = np.asarray(inputs["L0_attn_a_x"], dtype=np.float32)

    Wa = dict(wqkv_l=T(inputs["L0_attn_a_w_qkv"]),
              wqkv_wb=_wb(T(inputs["L0_attn_a_w_qkv"])),
              wout_wb=_wb(T(inputs["L0_attn_a_w_out"])))
    Wb = dict(w_q_wb=_wb(A_(inputs["L0_attn_b_w_q"])),
              w_kv=A_(inputs["L0_attn_b_w_kv"]),
              w_kv_wb=_wb(A_(inputs["L0_attn_b_w_kv"])),
              w_dense_wb=_wb(A_(inputs["L0_attn_b_w_dense"])))
    ln1 = np.asarray(inputs["L0_ffn_ln1_w"], dtype=np.float32)
    ln2 = np.asarray(inputs["L0_ffn_ln2_w"], dtype=np.float32)
    wdn_t = T(inputs["L0_ffn_W_down"])
    wdn3 = np.ascontiguousarray(
        wdn_t.reshape(4, 16, 128, 16, 128).transpose(2, 3, 0, 1, 4)
             .reshape(128, 131072))
    Wf = dict(wq_l=_wb(T(inputs["L0_ffn_Wq"])), wk_l=_wb(T(inputs["L0_ffn_Wk"])),
              wv_l=T(inputs["L0_ffn_Wv"]), wo_raw=_wb(A_(inputs["L0_ffn_Wo"])),
              A=A_(inputs["L0_ffn_A"]), B=A_(inputs["L0_ffn_B"]),
              win_l=_wb(T(inputs["L0_ffn_W_in_base"])),
              wup_l=_wb(T(inputs["L0_ffn_W_up_base"])),
              wdn_l=wdn3,
              ln1c=np.ascontiguousarray(ln1.reshape(16, 128).T),
              ln2c=np.ascontiguousarray(ln2.reshape(16, 128).T))
    s = np.arange(2048)
    inv64 = 1.0 / 10000.0 ** (np.arange(0, 64, 2) / 64.0)
    fr64 = s[None, :] * inv64[(np.arange(128) % 32)][:, None]
    W1a = dict(wq_l=_wb(A_(inputs["L1_attn_a_Wq"])),
               wk_l=_wb(A_(inputs["L1_attn_a_Wk"])),
               wv_l=A_(inputs["L1_attn_a_Wv"]),
               wphi_l=_wb(A_(inputs["L1_attn_a_Wphi"])),
               wo_l=_wb(A_(inputs["L1_attn_a_Wo"])),
               cos64=np.cos(fr64).astype(np.float32),
               sin64=np.sin(fr64).astype(np.float32))
    W1f = dict(
        A_l=_wb(np.asarray(inputs["L1_ffn_A"], dtype=np.float32
                           ).transpose(2, 0, 1).reshape(2048, 512)),
        Bw_l=_wb(np.asarray(inputs["L1_ffn_Bw"], dtype=np.float32
                            ).transpose(0, 2, 1).reshape(512, 2048), n_in=4),
        Wg_l=T(inputs["L1_ffn_Wg"]))
    inv128 = 1.0 / 10000.0 ** (np.arange(0, 128, 2) / 128.0)
    fr128 = s[None, :] * inv128[(np.arange(128) % 64)][:, None]
    onw = np.asarray(inputs["L1_attn_b_o_norm_w"], dtype=np.float32).reshape(128, 1)
    W1b = dict(wq_l=_wb(T(inputs["L1_attn_b_Wq"])),
               wk_l=_wb(T(inputs["L1_attn_b_Wk"])),
               wv_l=T(inputs["L1_attn_b_Wv"]),
               wo_l=_wb(T(inputs["L1_attn_b_Wo"])),
               cos128=np.cos(fr128).astype(np.float32),
               sin128=np.sin(fr128).astype(np.float32),
               onw=np.ascontiguousarray(onw))

    mk = MK()
    sub_attn_a(mk, Wa, last=False)
    sub_ffn(mk, Wf, last=False)
    sub_attn_b(mk, Wb, last=False)
    sub_ffn(mk, Wf, last=False)
    sub_l1attna(mk, W1a, last=False)
    sub_l1ffn(mk, W1f, last=False)
    sub_l1attnb(mk, W1b, last=False)
    sub_l1ffn(mk, W1f, last=True)
    mk.finish()

    in_maps = []
    for b in range(8):
        ins = dict(mk.inputs)
        ins["x_in"] = np.ascontiguousarray(x[b].T)
        in_maps.append(ins)
    res = run_bass_kernel_spmd(mk.nc, in_maps, core_ids=list(range(8)))
    out = np.stack([np.ascontiguousarray(res.results[b]["x_out"].T)
                    for b in range(8)], axis=0)
    return out.astype(np.float32)
